# version 03
# baseline (speedup 1.0000x reference)
import numpy as np
import concourse.bass as bass
import concourse.mybir as mybir
from concourse.bass_utils import run_bass_kernel_spmd
from concourse.aot_env import allow_ici_all_to_all

F32 = mybir.dt.float32
BF16 = mybir.dt.bfloat16
I32 = mybir.dt.int32
AF = mybir.ActivationFunctionType
OP = mybir.AluOpType
AX = mybir.AxisListType.X

NCORE = 8
P = 128
TLOC = 4096
D = 1024
DT = D // P
NBW = 512
NB = TLOC // NBW
HEADS = 16
DH = 64
INNER = 2048
HID0 = 2816
HID1 = 4096
NSTATE = 16
DSL = INNER // NCORE
EPS = 1e-5

ENGS = ("tensor", "vector", "scalar", "gpsimd", "sync")


DMA_LANES = 4


class Prog:

    def __init__(self, nc):
        self.nc = nc
        self.items = {e: [] for e in ENGS}
        self.ticks = {e: 0 for e in ENGS}
        self.dmas = {q: 0 for q in ("sync", "scalar", "gpsimd")}
        self._forced = {e: [] for e in ENGS}

    def op(self, eng, fn, wait=(), tick=False, dma=False):
        waits = list(self._forced[eng]) + [w for w in wait if w is not None]
        self._forced[eng] = []
        incs = []
        token = None
        if dma:
            idx = self.dmas[eng]
            self.dmas[eng] += 1
            lane = "dma_%s%d" % (eng, idx % DMA_LANES)
            rounds = idx // DMA_LANES
            if rounds >= 1:
                waits.append((lane, 16 * rounds))
            incs.append((lane, 16))
            token = (lane, 16 * (rounds + 1))
        if tick:
            k = "tick_" + eng
            self.ticks[eng] += 1
            incs.append((k, 1))
            token = (k, self.ticks[eng])
        self.items[eng].append([fn, waits, incs])
        return token

    def tick(self, eng, wait=()):
        if wait or not self.items[eng] or any(
                sk.startswith("dma_") for sk, _ in self.items[eng][-1][2]):
            return self.op(eng, lambda e: e.nop(), wait=wait, tick=True)
        k = "tick_" + eng
        incs = self.items[eng][-1][2]
        for sk, amt in incs:
            if sk == k:
                return (k, self.ticks[eng])
        self.ticks[eng] += 1
        incs.append((k, 1))
        return (k, self.ticks[eng])

    def dma_tokens(self, q):
        toks = []
        for lane in range(DMA_LANES):
            n = (self.dmas[q] - lane + DMA_LANES - 1) // DMA_LANES
            if n:
                toks.append(("dma_%s%d" % (q, lane), 16 * n))
        return toks

    def barrier(self):
        toks = [self.tick(e) for e in ENGS]
        for q in self.dmas:
            toks += self.dma_tokens(q)
        for e in ENGS:
            self._forced[e].extend(toks)

    def finalize(self, block, sems):
        def body_for(ename):
            def body(eng):
                marks = {}
                for fn, waits, incs in self.items[ename]:
                    best = {}
                    for (sk, v) in waits:
                        if v > best.get(sk, 0):
                            best[sk] = v
                    for sk, v in best.items():
                        if v > marks.get(sk, 0):
                            eng.wait_ge(sems[sk], v)
                            marks[sk] = v
                    r = fn(eng)
                    for (sk, amt) in incs:
                        r.then_inc(sems[sk], amt)
            return body

        for ename in ENGS:
            getattr(block, ename)(body_for(ename))


def build(debug_taps=()):
    nc = bass.Bass("TRN2", target_bir_lowering=False)
    pr = Prog(nc)

    di = {}

    def inp(name, shape, dt=F32):
        di[name] = nc.dram_tensor(name, list(shape), dt, kind="ExternalInput")
        return di[name]

    x_in = inp("x_t", [D, TLOC])
    w_qkv = inp("w_qkv", [D, 3 * D])
    w_out = inp("w_out", [D, D])
    b_out = inp("b_out", [1, D])
    clip_x = inp("clip_x", [1, 1])
    clip_down = inp("clip_down", [1, 1])
    w_gate = inp("w_gate", [HID0, D])
    w_up = inp("w_up", [HID0, D])
    w_down = inp("w_down", [D, HID0])
    inp_wT = inp("inp_wT", [D, 2 * INNER])
    conv_w = inp("conv_w", [INNER, 4])
    conv_b = inp("conv_b", [INNER, 1])
    dt_wT = inp("dt_wT", [INNER, INNER])
    dt_b = inp("dt_b", [INNER, 1])
    a_log = inp("a_log", [INNER, NSTATE])
    b_wT = inp("b_wT", [INNER, NSTATE])
    c_wT = inp("c_wT", [INNER, NSTATE])
    out_wT = inp("out_wT", [INNER, D])
    w1T = inp("w1T", [D, HID1])
    cw1 = inp("cw1", [HID1, 3])
    cb1 = inp("cb1", [HID1, 1])
    w2T = inp("w2T", [HID1, D])
    c_eye = inp("c_eye", [P, P])
    c_ones_row = inp("c_ones_row", [1, P])
    c_ones_col = inp("c_ones_col", [P, 1])
    c_patt16 = [inp("c_patt16_%d" % f, [16, P]) for f in range(DT)]
    c_core = inp("c_core", [1, 4])
    a_slice = inp("a_slice", [DSL, NSTATE])
    c_sel16 = inp("c_sel16", [16, 16 * P])

    out_d = nc.dram_tensor("out", [TLOC, D], F32, kind="ExternalOutput")

    ds = {}

    def scr(name, shape, dt=F32):
        ds[name] = nc.dram_tensor(name, list(shape), dt)
        return ds[name]

    xres = scr("xres", [D, TLOC])
    qb_sp = scr("qb_sp", [D, TLOC], BF16)
    dbg_att = scr("dbg_att", [D, TLOC], BF16)
    dbg_x2 = scr("dbg_x2", [D, TLOC])
    dbg_gu = scr("dbg_gu", [P, 2 * NBW])
    dbg_xq = scr("dbg_xq", [D, TLOC], BF16)
    dbg_wg = scr("dbg_wg", [P, D], BF16)
    dbg_den = scr("dbg_den", [16, TLOC])
    dbg_sc = [scr("dbg_sc%d" % i, [P, 1]) for i in range(4)]
    dbg_tqw = scr("dbg_tqw", [P, 1])
    dbg_tqr = scr("dbg_tqr", [1, P])
    dbg_tqm = scr("dbg_tqm", [1, 1])
    dbg_tqd = scr("dbg_tqd", [P, 1])
    dbg_tqs = scr("dbg_tqs", [P, 1])
    dbg_x1 = scr("dbg_x1", [D, TLOC])
    dbg_f1xq = scr("dbg_f1xq", [D, TLOC], BF16)
    dbg_f1w1 = scr("dbg_f1w1", [D, HID1], BF16)
    dbg_f1hx = scr("dbg_f1hx", [D, TLOC])
    cc_kv_i = scr("cc_kv_i", [DH, HEADS * 65])
    cc_kv_o = scr("cc_kv_o", [DH, HEADS * 65])
    cc_mx_i = scr("cc_mx_i", [1, 8])
    cc_mx_o = scr("cc_mx_o", [1, 8])
    h_sp = scr("h_sp", [HID0, TLOC], BF16)
    hs_sp = scr("hs_sp", [HID1, TLOC], BF16)
    wd_sp = scr("wd_sp", [D, HID0], BF16)
    a2a_dt_i = scr("a2a_dt_i", [NCORE, DSL, TLOC])
    a2a_dt_o = scr("a2a_dt_o", [NCORE, DSL, TLOC])
    a2a_dx_i = scr("a2a_dx_i", [NCORE, DSL, TLOC], BF16)
    a2a_dx_o = scr("a2a_dx_o", [NCORE, DSL, TLOC], BF16)
    a2a_g_i = scr("a2a_g_i", [NCORE, DSL, TLOC], BF16)
    a2a_g_o = scr("a2a_g_o", [NCORE, DSL, TLOC], BF16)
    a2a_y_i = scr("a2a_y_i", [NCORE, DSL, TLOC], BF16)
    a2a_y_o = scr("a2a_y_o", [NCORE, DSL, TLOC], BF16)
    bc_ag_i = scr("bc_ag_i", [32, TLOC])
    bc_ag_o = scr("bc_ag_o", [NCORE * 32, TLOC])
    halo3_i = scr("halo3_i", [1, 3 * D])
    halo3_o = scr("halo3_o", [2, 3 * D])
    halo1_i = scr("halo1_i", [1, 16 * D])
    halo1_o = scr("halo1_o", [2, 16 * D])

    ctx_stack = [[]]

    def sb(name, shape, dt=F32):
        cm = nc.sbuf_tensor(name, list(shape), dt)
        h = cm.__enter__()
        ctx_stack[-1].append(cm)
        return h

    def begin():
        ctx_stack.append([])

    def end():
        pr.barrier()
        for cm in reversed(ctx_stack.pop()):
            cm.__exit__(None, None, None)

    psum_cms = [nc.psum_tensor("ps%d" % i, [P, NBW], F32) for i in range(6)]
    PS = [cm.__enter__() for cm in psum_cms]
    pt_cms = [nc.psum_tensor("pt_b%d" % i, [P, 2 * NBW], BF16) for i in range(2)]
    PTB = [cm.__enter__() for cm in pt_cms]
    psum_cms += pt_cms

    sem_keys = ["tick_" + e for e in ENGS] + [
        "dma_%s%d" % (q, l) for q in ("sync", "scalar", "gpsimd") for l in range(4)]
    sem_cms = {k: nc.semaphore(k) for k in sem_keys}
    sems = {k: cm.__enter__() for k, cm in sem_cms.items()}

    eye_f = sb("eye_f", [P, P])
    eye_b = sb("eye_b", [P, P], BF16)
    ones_row = sb("ones_row", [1, P])
    ones_col = sb("ones_col", [P, 1])
    ones512_b = sb("ones512_b", [1, NBW], BF16)
    core_c = sb("core_c", [1, 4])
    aneg = [sb("aneg%d" % i, [P, NSTATE]) for i in range(2)]
    hcol = sb("hcol", [P, 1])
    rcol = sb("rcol", [P, 1])
    sig_cx = sb("sig_cx", [1, 1])
    sig_cd = sb("sig_cd", [1, 1])
    sc_a = sb("sc_a", [1, 1])
    mxrow = sb("mxrow", [1, 8])
    sc_b = sb("sc_b", [1, 1])
    scol_a = sb("scol_a", [P, 1])
    scol_b = sb("scol_b", [P, 1])

    t0 = pr.op("sync", lambda e: e.dma_start(out=eye_f[:, :], in_=c_eye[:, :]), dma=True)
    pr.op("sync", lambda e: e.dma_start(out=ones_row[:1, :], in_=c_ones_row[:1, :]), dma=True)
    pr.op("sync", lambda e: e.dma_start(out=ones_col[:, :], in_=c_ones_col[:, :]), dma=True)
    pr.op("sync", lambda e: e.dma_start(out=core_c[:1, :], in_=c_core[:1, :]), dma=True)
    pr.op("vector", lambda e: e.tensor_copy(eye_b[:, :], eye_f[:, :]), wait=[t0], tick=True)
    pr.op("gpsimd", lambda e: e.memset(ones512_b[:1, :], 1.0), tick=True)
    pr.op("vector", lambda e: e.memset(mxrow[:1, :], 0.0), tick=True)
    tasl = pr.op("sync", lambda e: e.dma_start(out=aneg[0][:, :], in_=a_slice[0:P, :]), dma=True)
    tasl2 = pr.op("sync", lambda e: e.dma_start(out=aneg[1][:, :], in_=a_slice[P:2 * P, :]), dma=True)
    tcx = pr.op("sync", lambda e: e.dma_start(out=sig_cx[:1, :1], in_=clip_x[:1, :1]), dma=True)
    tcd = pr.op("sync", lambda e: e.dma_start(out=sig_cd[:1, :1], in_=clip_down[:1, :1]), dma=True)
    pr.op("scalar", lambda e: e.activation(sig_cx[:1, :1], sig_cx[:1, :1], AF.Sigmoid),
          wait=[tcx], tick=True)
    pr.op("scalar", lambda e: e.activation(sig_cd[:1, :1], sig_cd[:1, :1], AF.Sigmoid),
          wait=[tcd], tick=True)
    for i in range(2):
        te_ = pr.op("scalar", lambda e, i=i: e.activation(
            aneg[i][:, :], aneg[i][:, :], AF.Exp), wait=[tasl, tasl2], tick=True)
        pr.op("vector", lambda e, i=i: e.tensor_scalar(
            aneg[i][:, :], aneg[i][:, :], -1.0, None, op0=OP.mult),
            wait=[te_], tick=True)
    tcc0 = pr.tick("sync")
    mmh = pr.op("tensor", lambda e: e.matmul(PS[0][:, 0:1], ones_row[0:1, :],
                                             core_c[0:1, 0:1], start=True, stop=True),
                wait=pr.dma_tokens("sync"), tick=True)
    pr.op("scalar", lambda e: e.copy(hcol[:, :], PS[0][:, 0:1]), wait=[mmh], tick=True)
    ths = pr.tick("scalar")
    mmr = pr.op("tensor", lambda e: e.matmul(PS[0][:, 0:1], ones_row[0:1, :],
                                             core_c[0:1, 2:3], start=True, stop=True),
                wait=[ths], tick=True)
    pr.op("scalar", lambda e: e.copy(rcol[:, :], PS[0][:, 0:1]), wait=[mmr], tick=True)
    pr.barrier()

    _bc_last = {}

    def bcast_col(dst_col, src11, wait=(), bank=0):
        t1 = pr.op("tensor", lambda e: e.matmul(
            PS[bank][:, 0:1], ones_row[0:1, :], src11[0:1, 0:1], start=True, stop=True),
            wait=list(wait) + [_bc_last.get(bank)], tick=True)
        tok = pr.op("scalar", lambda e: e.copy(dst_col[:, 0:1], PS[bank][:, 0:1]),
                    wait=[t1], tick=True)
        _bc_last[bank] = tok
        return tok

    def part_max(dst11, src_col, bank, wait=()):
        tt_ = tr(PS[bank][0:1, 0:P], src_col[:, 0:1], eye_f[:, :],
                 wait=list(wait) + [_bc_last.get(bank)], tick=True)
        tc_ = pr.op("scalar", lambda e: e.copy(rows_g["tmprow"][0:1, 0:P], PS[bank][0:1, 0:P]),
                    wait=[tt_, rows_g.get("tmprow_free")], tick=True)
        _bc_last[bank] = tc_
        tok = pr.op("vector", lambda e: e.tensor_reduce(
            dst11[0:1, 0:1], rows_g["tmprow"][0:1, 0:P], axis=AX, op=OP.max),
            wait=[tc_], tick=True)
        rows_g["tmprow_free"] = tok
        return tok

    rows_g = {"tmprow": sb("tmprow_g", [1, P]), "tmprow_free": None}

    def mm(ps_ap, lhsT_ap, rhs_ap, start, stop, wait=(), tick=False):
        return pr.op("tensor",
                     lambda e: e.matmul(ps_ap, lhsT_ap, rhs_ap, start=start, stop=stop),
                     wait=wait, tick=tick)

    def tr(ps_ap, in_ap, ident, wait=(), tick=False):
        return pr.op("tensor", lambda e: e.transpose(ps_ap, in_ap, ident),
                     wait=wait, tick=tick)

    def feat_ln_block(n, xr, sqt, rows, bcast_banks, stat_banks, out_fn, extra_stat_wait=()):
        bx, bq = stat_banks
        tprev = None
        for m in range(DT):
            tsq = pr.op("scalar", lambda e, m=m: e.activation(
                sqt[:, :], xr[m][:, :], AF.Square),
                wait=[rows["xr_tok"][m], tprev], tick=True)
            mm(PS[bx][0:1, :], ones_col[:, :], xr[m][:, :], start=(m == 0), stop=(m == DT - 1),
               wait=[rows["xr_tok"][m]] + (list(extra_stat_wait) if m == 0 else []))
            mm(PS[bq][0:1, :], ones_col[:, :], sqt[:, :], start=(m == 0), stop=(m == DT - 1),
               wait=[tsq])
            tprev = pr.tick("tensor")
        tst = pr.tick("tensor")
        pr.op("scalar", lambda e: e.activation(rows["mrow"][:1, :], PS[bx][0:1, :],
                                               AF.Copy, scale=1.0 / D), wait=[tst])
        tmr = pr.tick("scalar")
        pr.op("scalar", lambda e: e.activation(rows["m2row"][:1, :], rows["mrow"][:1, :],
                                               AF.Square))
        tm2 = pr.tick("scalar")
        pr.op("vector", lambda e: e.scalar_tensor_tensor(
            rows["vrow"][:1, :], PS[bq][0:1, :], 1.0 / D, rows["m2row"][:1, :],
            op0=OP.mult, op1=OP.subtract), wait=[tm2, tst])
        tve = pr.op("vector", lambda e: e.tensor_scalar(
            rows["vrow"][:1, :], rows["vrow"][:1, :], EPS, None, op0=OP.add), tick=True)
        pr.op("vector", lambda e: e.reciprocal(rows["vrow"][:1, :], rows["vrow"][:1, :]),
              wait=[tve], tick=True)
        tvr = pr.tick("vector")
        pr.op("scalar", lambda e: e.activation(rows["rrow"][:1, :], rows["vrow"][:1, :],
                                               AF.Sqrt), wait=[tvr])
        tsr = pr.tick("scalar")
        mm(PS[bx][:, :], ones_row[0:1, :], rows["mrow"][0:1, :], start=True, stop=True,
           wait=[tmr, tvr])
        mm(PS[bq][:, :], ones_row[0:1, :], rows["rrow"][0:1, :], start=True, stop=True,
           wait=[tsr])
        tbc = pr.tick("tensor")
        last = None
        for m in range(DT):
            pr.op("vector", lambda e, m=m: e.tensor_tensor(
                rows["xo"][:, :], xr[m][:, :], PS[bx][:, :], op=OP.subtract),
                wait=[tbc] + ([last] if last else []))
            tmul = pr.op("vector", lambda e: e.tensor_tensor(
                rows["xo"][:, :], rows["xo"][:, :], PS[bq][:, :], op=OP.mult), tick=True)
            last = out_fn(m, rows["xo"], tmul) or tmul
        return last

    begin()
    patt = [sb("patt_%d" % f, [16, P]) for f in range(DT)]
    for f in range(DT):
        pr.op("sync", lambda e, f=f: e.dma_start(out=patt[f][:, :], in_=c_patt16[f][:, :]),
              dma=True)
    wq = [sb("wq%d" % k, [P, 3 * D], BF16) for k in range(DT)]
    wo = [sb("wo%d" % k, [P, D], BF16) for k in range(DT)]
    bo_b = sb("bo_b", [1, D], BF16)
    xb = [sb("xbA%d" % k, [P, NBW], BF16) for k in range(DT)]
    kb = [sb("kbA%d" % k, [P, NBW], BF16) for k in range(DT)]
    vb = [sb("vbA%d" % k, [P, NBW], BF16) for k in range(DT)]
    qtmp = sb("qtmpA", [P, NBW])
    qexp = sb("qexpA", [P, NBW])
    qout = sb("qoutA", [P, NBW], BF16)
    ktm = sb("ktmA", [P, HEADS * DH], BF16)
    vtm = sb("vtmA", [P, HEADS * 65], BF16)
    kv_sb = sb("kv_sbA", [DH, HEADS * 65])
    kvb = sb("kvbA", [P, HEADS * 65], BF16)
    zt = [sb("ztA%d" % f, [P, 16], BF16) for f in range(DT)]
    qfb = [sb("qfbA%d" % k, [P, NBW], BF16) for k in range(DT)]
    d16 = sb("d16A", [16, NBW])
    r16 = sb("r16A", [16, NBW])
    af32 = sb("af32A", [P, NBW])
    atb = [sb("atbA%d" % k, [P, NBW], BF16) for k in range(DT)]
    xs = [sb("xsA%d" % m, [P, NBW]) for m in range(DT)]
    xr = [sb("xrA%d" % m, [P, NBW]) for m in range(DT)]
    sqt = sb("sqtA", [P, NBW])
    rows = {"mrow": sb("mrowA", [1, NBW]), "m2row": sb("m2rowA", [1, NBW]),
            "vrow": sb("vrowA", [1, NBW]), "rrow": sb("rrowA", [1, NBW]),
            "xo": sb("xoA", [P, NBW])}

    for k in range(DT):
        pr.op("gpsimd", lambda e, k=k: e.dma_start(
            out=wq[k][:, :], in_=w_qkv[P * k:P * (k + 1), :]), dma=True)
    for k in range(DT):
        pr.op("gpsimd", lambda e, k=k: e.dma_start(
            out=wo[k][:, :], in_=w_out[P * k:P * (k + 1), :]), dma=True)
    pr.op("gpsimd", lambda e: e.dma_start(out=bo_b[:1, :], in_=b_out[:1, :]), dma=True)
    for h in range(HEADS):
        pr.op("vector", lambda e, h=h: e.memset(vtm[:, 65 * h + 64:65 * h + 65], 1.0),
              tick=True)
    for f in range(DT):
        pr.op("vector", lambda e, f=f: e.memset(zt[f][:, :], 0.0), tick=True)
    pr.op("vector", lambda e: e.memset(kv_sb[:, :], 0.0), tick=True)
    pr.barrier()

    for n in range(NB):
        for k in range(DT):
            pr.op("gpsimd", lambda e, k=k, n=n: e.dma_start(
                out=xb[k][:, :], in_=x_in[P * k:P * (k + 1), NBW * n:NBW * (n + 1)]),
                dma=True)
        t_xb_l = pr.dma_tokens("gpsimd")
        tok_ps_free = {}
        tok_evac = {}
        tok_qs = None
        tok_qout = None
        for m in range(24):
            bank = m % 3
            for k in range(DT):
                mm(PS[bank][:, :], wq[k][:, P * m:P * (m + 1)], xb[k][:, :],
                   start=(k == 0), stop=(k == DT - 1),
                   wait=(t_xb_l if m == 0 and k == 0 else []) +
                        ([tok_ps_free.get(bank)] if k == 0 else []))
            tmm = pr.tick("tensor")
            if m < 16:
                dst = qout if m < 8 else kb[m - 8]
                pr.op("vector", lambda e, bank=bank: e.tensor_scalar(
                    qtmp[:, :], PS[bank][:, :], 0.0, None, op0=OP.min),
                    wait=[tmm] + ([tok_qs] if tok_qs else []))
                tq1 = pr.tick("vector")
                tq2 = pr.op("scalar", lambda e: e.activation(
                    qexp[:, :], qtmp[:, :], AF.Exp), wait=[tq1], tick=True)
                tev = pr.op("vector", lambda e, dst=dst, bank=bank: e.scalar_tensor_tensor(
                    dst[:, :], PS[bank][:, :], 0.0, qexp[:, :], op0=OP.max, op1=OP.add),
                    wait=[tq2] + ([tok_qout] if (m < 8 and tok_qout) else []), tick=True)
                tok_qs = tev
                if m < 8:
                    tok_qout = pr.op("sync", lambda e, m=m, n=n: e.dma_start(
                        out=qb_sp[P * m:P * (m + 1), NBW * n:NBW * (n + 1)], in_=qout[:, :]),
                        wait=[tev], dma=True)
                tok_evac[m] = tev
            else:
                tok_evac[m] = pr.op("scalar", lambda e, m=m, bank=bank: e.copy(
                    vb[m - 16][:, :], PS[bank][:, :]), wait=[tmm], tick=True)
            tok_ps_free[bank] = tok_evac[m]
        tok_tr_evac = {0: None, 1: None}
        tok_kvadd = {}
        for c in range(4):
            g = 4 * n + c
            for f in range(DT):
                tb = f % 2
                PT = PTB[tb]
                ttr = tr(PT[:, 0:P], kb[f][:, P * c:P * (c + 1)], eye_b[:, :],
                         wait=[tok_evac[8 + f], tok_tr_evac[tb]], tick=True)
                tke = pr.op("scalar", lambda e, f=f, PT=PT: e.copy(
                    ktm[:, P * f:P * (f + 1)], PT[:, 0:P]), wait=[ttr], tick=True)
                ttr = tr(PT[:, 0:P], vb[f][:, P * c:P * (c + 1)], eye_b[:, :],
                         wait=[tok_evac[16 + f], tke], tick=True)
                pr.op("scalar", lambda e, f=f, PT=PT: e.copy(
                    vtm[:, 65 * 2 * f:65 * 2 * f + 64], PT[:, 0:64]),
                    wait=[ttr], tick=True)
                tok_tr_evac[tb] = pr.op("scalar", lambda e, f=f, PT=PT: e.copy(
                    vtm[:, 65 * (2 * f + 1):65 * (2 * f + 1) + 64], PT[:, 64:P]),
                    tick=True)
            tse = pr.tick("scalar")
            for h in range(HEADS):
                kvbank = 3 + h // 7
                off = 65 * (h % 7) if h < 14 else 65 * (h - 14)
                first = h % 7 == 0 or h == 14
                last = h % 7 == 6 or h == 13 or h == 15
                mm(PS[kvbank][0:DH, off:off + 65],
                   ktm[:, DH * h:DH * (h + 1)], vtm[:, 65 * h:65 * h + 65],
                   start=first, stop=last,
                   wait=([tse] if h == 0 else []) +
                        ([tok_kvadd.get(kvbank)] if first else []))
            tkvm = pr.tick("tensor")
            for bk, lo, width in ((3, 0, 455), (4, 455, 455), (5, 910, 130)):
                tok_kvadd[bk] = pr.op("vector", lambda e, bk=bk, lo=lo, width=width: e.tensor_tensor(
                    kv_sb[:, lo:lo + width], kv_sb[:, lo:lo + width],
                    PS[bk][0:DH, 0:width], op=OP.add), wait=[tkvm], tick=True)
        pr.barrier()

    tkv = pr.tick("vector")
    tw = pr.op("sync", lambda e: e.dma_start(out=cc_kv_i[:, :], in_=kv_sb[:, :]),
               wait=[tkv], dma=True)
    tcc = pr.op("gpsimd", lambda e: e.collective_compute(
        "AllReduce", OP.add, replica_groups=[[0, 1], [2, 3], [4, 5], [6, 7]],
        ins=[cc_kv_i[:, :]], outs=[cc_kv_o[:, :]]), wait=[tw], tick=True)
    trkv = pr.op("sync", lambda e: e.dma_start(out=kv_sb[:, :], in_=cc_kv_o[:, :]),
                 wait=[tcc], dma=True)
    pr.op("vector", lambda e: e.tensor_copy(kvb[0:DH, :], kv_sb[:, :]), wait=[trkv],
          tick=True)
    pr.op("vector", lambda e: e.tensor_copy(kvb[DH:P, :], kv_sb[:, :]), tick=True)
    for f in range(DT):
        pr.op("scalar", lambda e, f=f: e.copy(
            zt[f][0:DH, 2 * f:2 * f + 1], kv_sb[:, 65 * 2 * f + 64:65 * 2 * f + 65]),
            wait=[trkv], tick=True)
        pr.op("scalar", lambda e, f=f: e.copy(
            zt[f][DH:P, 2 * f + 1:2 * f + 2],
            kv_sb[:, 65 * (2 * f + 1) + 64:65 * (2 * f + 1) + 65]), tick=True)
    pr.barrier()

    for n in range(NB):
        tq = [None] * DT
        txs = [None] * DT
        for k in range(DT):
            tq[k] = pr.op("sync", lambda e, k=k, n=n: e.dma_start(
                out=qfb[k][:, :], in_=qb_sp[P * k:P * (k + 1), NBW * n:NBW * (n + 1)]),
                dma=True)
            txs[k] = pr.op("scalar", lambda e, k=k, n=n: e.dma_start(
                out=xs[k][:, :], in_=x_in[P * k:P * (k + 1), NBW * n:NBW * (n + 1)]),
                dma=True)
        for k in range(DT):
            mm(PS[5][0:16, :], zt[k][:, :], qfb[k][:, :], start=(k == 0), stop=(k == DT - 1),
               wait=[tq[k]])
        tden = pr.tick("tensor")
        td6 = pr.op("vector", lambda e: e.tensor_scalar(
            d16[:, :], PS[5][0:16, :], 1e-6, None, op0=OP.add), wait=[tden], tick=True)
        trc = pr.op("vector", lambda e: e.reciprocal(r16[:, :], d16[:, :]),
                    wait=[td6], tick=True)
        pr._forced["vector"].append(trc)
        pr.op("sync", lambda e, n=n: e.dma_start(
            out=dbg_den[:, NBW * n:NBW * (n + 1)], in_=d16[:, :]), wait=[trc], dma=True)
        tok_ps_free = {}
        tok_af = None
        tok_atb = [None] * DT
        for f in range(DT):
            nb_ = f % 2
            mm(PS[nb_][0:DH, :], kvb[0:DH, 65 * 2 * f:65 * 2 * f + DH],
               qfb[f][0:DH, :], start=True, stop=True,
               wait=[tok_ps_free.get(nb_), tq[f]])
            mm(PS[nb_][DH:P, :], kvb[DH:P, 65 * (2 * f + 1):65 * (2 * f + 1) + DH],
               qfb[f][DH:P, :], start=True, stop=True)
            tnum = pr.tick("tensor")
            taf = pr.op("scalar", lambda e, nb_=nb_: e.copy(af32[:, :], PS[nb_][:, :]),
                        wait=[tnum] + ([tok_af] if tok_af else []), tick=True)
            tok_ps_free[nb_] = taf
            mm(PS[2][:, :], patt[f][:, :], r16[:, :], start=True, stop=True,
               wait=[trc, tok_ps_free.get(2)])
            trb = pr.tick("tensor")
            tok_atb[f] = pr.op("vector", lambda e, f=f: e.tensor_tensor(
                atb[f][:, :], af32[:, :], PS[2][:, :], op=OP.mult), wait=[trb, taf],
                tick=True)
            tok_af = tok_atb[f]
            tok_ps_free[2] = tok_atb[f]
            pr.op("sync", lambda e, f=f, n=n: e.dma_start(
                out=dbg_att[P * f:P * (f + 1), NBW * n:NBW * (n + 1)], in_=atb[f][:, :]),
                wait=[tok_atb[f]], dma=True)
        xr_tok = [None] * DT
        for m in range(DT):
            bank = 3 + (m % 2)
            for k in range(DT):
                mm(PS[bank][:, :], wo[k][:, P * m:P * (m + 1)], atb[k][:, :],
                   start=(k == 0), stop=False,
                   wait=([tok_atb[k]] if m < 2 else []) +
                        ([tok_ps_free.get(bank)] if k == 0 else []))
            mm(PS[bank][:, :], bo_b[0:1, P * m:P * (m + 1)], ones512_b[0:1, :],
               start=False, stop=True)
            tmo = pr.tick("tensor")
            trs = pr.op("vector", lambda e, m=m, bank=bank: e.tensor_tensor(
                xr[m][:, :], PS[bank][:, :], xs[m][:, :], op=OP.add),
                wait=[tmo, txs[m]], tick=True)
            tok_ps_free[bank] = trs
            xr_tok[m] = trs
        rows["xr_tok"] = xr_tok

        def out_fn(m, xo_t, tok, n=n):
            return pr.op("scalar", lambda e, m=m, n=n: e.dma_start(
                out=xres[P * m:P * (m + 1), NBW * n:NBW * (n + 1)], in_=xo_t[:, :]),
                wait=[tok], dma=True)

        feat_ln_block(n, xr, sqt, rows, None, (5, 2), out_fn)
        pr.barrier()
    end()


    begin()
    if debug_taps:
        for m_ in range(DT):
            pr.op("sync", lambda e, m_=m_: e.dma_start(
                out=dbg_x1[P * m_:P * (m_ + 1), :], in_=xres[P * m_:P * (m_ + 1), :]),
                dma=True)
    HTL = TLOC // 2
    xt0 = sb("xtF0", [P, HTL])
    gmx = sb("gmxF0", [P, 1])
    tmpc0 = sb("tmpc0F0", [P, 1])
    mx11 = sb("mx11F0", [1, 1])
    wint = sb("wintF0", [P, HID0], I32)
    xi0 = wint
    xqb = [sb("xqbF%d" % k, [P, TLOC], BF16) for k in range(DT)]
    tld = None
    tgm = None
    trdd = None
    for mh in range(2 * DT):
        m, hh = mh // 2, mh % 2
        tld = pr.op("sync", lambda e, m=m, hh=hh: e.dma_start(
            out=xt0[:, :], in_=xres[P * m:P * (m + 1), HTL * hh:HTL * (hh + 1)]),
            wait=[tgm, trdd], dma=True)
        if mh == 0:
            tgm = pr.op("vector", lambda e: e.tensor_reduce(
                gmx[:, 0:1], xt0[:, :], axis=AX, op=OP.max,
                apply_absolute_value=True), wait=[tld], tick=True)
        else:
            trdd = pr.op("vector", lambda e: e.tensor_reduce(
                tmpc0[:, 0:1], xt0[:, :], axis=AX, op=OP.max,
                apply_absolute_value=True), wait=[tld], tick=True)
            tgm = pr.op("vector", lambda e: e.tensor_tensor(
                gmx[:, 0:1], gmx[:, 0:1], tmpc0[:, 0:1], op=OP.max),
                wait=[trdd], tick=True)
    tpm = part_max(mx11, gmx, 2, wait=[tgm])
    tcpm = pr.op("vector", lambda e: e.tensor_copy(mxrow[0:1, 0:1], mx11[0:1, 0:1]),
                 wait=[tpm], tick=True)
    tw_ = pr.op("sync", lambda e: e.dma_start(out=cc_mx_i[0:1, :], in_=mxrow[0:1, :]),
                wait=[tcpm], dma=True)
    tcc = pr.op("gpsimd", lambda e: e.collective_compute(
        "AllReduce", OP.max, replica_groups=[list(range(NCORE))],
        ins=[cc_mx_i[:, :]], outs=[cc_mx_o[:, :]]), wait=[tw_], tick=True)
    trd = pr.op("sync", lambda e: e.dma_start(out=mx11[0:1, 0:1], in_=cc_mx_o[0:1, 0:1]),
                wait=[tcc], dma=True)
    t_a = pr.op("vector", lambda e: e.tensor_tensor(
        sc_a[0:1, 0:1], mx11[0:1, 0:1], sig_cx[0:1, 0:1], op=OP.mult), wait=[trd], tick=True)
    t_b = pr.op("vector", lambda e: e.tensor_scalar(
        sc_b[0:1, 0:1], sc_a[0:1, 0:1], 1.0 / 127.0, None, op0=OP.mult),
        wait=[t_a], tick=True)
    t_r = pr.op("vector", lambda e: e.reciprocal(sc_a[0:1, 0:1], sc_a[0:1, 0:1]),
                wait=[t_b], tick=True)
    tqi = pr.op("vector", lambda e: e.tensor_scalar(
        sc_a[0:1, 0:1], sc_a[0:1, 0:1], 127.0, None, op0=OP.mult), wait=[t_r], tick=True)
    tbc1 = bcast_col(scol_a, sc_a, wait=[tqi], bank=2)
    tbc2 = bcast_col(scol_b, sc_b, wait=[tqi], bank=2)
    if debug_taps:
        pr.op("sync", lambda e: e.dma_start(
            out=dbg_sc[0][:, 0:1], in_=scol_a[:, 0:1]), wait=[tbc1], dma=True)
        pr.op("sync", lambda e: e.dma_start(
            out=dbg_sc[1][:, 0:1], in_=scol_b[:, 0:1]), wait=[tbc2], dma=True)
    tqdone = []
    t1p = None
    for mh in range(2 * DT):
        m, hh = mh // 2, mh % 2
        tld = pr.op("sync", lambda e, m=m, hh=hh: e.dma_start(
            out=xt0[:, :], in_=xres[P * m:P * (m + 1), HTL * hh:HTL * (hh + 1)]),
            wait=[tbc2, t1p], dma=True)
        t1p = pr.op("scalar", lambda e: e.activation(
            xi0[:, 0:HTL], xt0[:, :], AF.Copy, scale=scol_a[:, 0:1]),
            wait=[tld, tbc2], tick=True)
        t2_ = pr.op("vector", lambda e: e.tensor_scalar(
            xi0[:, 0:HTL], xi0[:, 0:HTL], 127.0, -128.0, op0=OP.min, op1=OP.max),
            wait=[t1p], tick=True)
        t3_ = pr.op("scalar", lambda e, m=m, hh=hh: e.activation(
            xqb[m][:, HTL * hh:HTL * (hh + 1)], xi0[:, 0:HTL], AF.Copy,
            scale=scol_b[:, 0:1]), wait=[t2_], tick=True)
        tqdone.append(t3_)
        pr.op("sync", lambda e, m=m, hh=hh: e.dma_start(
            out=dbg_xq[P * m:P * (m + 1), HTL * hh:HTL * (hh + 1)],
            in_=xqb[m][:, HTL * hh:HTL * (hh + 1)]), wait=[t3_], dma=True)
    pr.barrier()
    wnat = sb("wnatF0", [P, HID0])
    wrow = sb("wrowF0", [P, 1])
    qcol = sb("qcolF0", [P, 1])
    wqn = sb("wqnF0", [P, HID0], BF16)
    wgT = sb("wgTF0", [P, D], BF16)
    wuT = sb("wuTF0", [P, D], BF16)
    sgt = sb("sgtF0", [P, NBW])
    hmaxc = sb("hmaxcF0", [P, 22])
    hbt = sb("hbtF0", [P, NBW], BF16)
    NC0 = HID0 // P
    qst = {"ev": {0: None, 1: None}, "wq_free": None, "nat_free": None}

    def quant_rows_to(dstT, src_dram, r0, ncols):
        tl = pr.op("sync", lambda e: e.dma_start(
            out=wnat[:, 0:ncols], in_=src_dram[r0:r0 + P, 0:ncols]),
            wait=[qst["nat_free"]], dma=True)
        trw = pr.op("vector", lambda e: e.tensor_reduce(
            wrow[:, 0:1], wnat[:, 0:ncols], axis=AX, op=OP.max,
            apply_absolute_value=True), wait=[tl], tick=True)
        trq = pr.op("vector", lambda e: e.reciprocal(qcol[:, 0:1], wrow[:, 0:1]),
                    wait=[trw], tick=True)
        tiv = pr.op("vector", lambda e: e.tensor_scalar(
            qcol[:, 0:1], qcol[:, 0:1], 127.0, None, op0=OP.mult), wait=[trq], tick=True)
        tdq = pr.op("vector", lambda e: e.tensor_scalar(
            wrow[:, 0:1], wrow[:, 0:1], 1.0 / 127.0, None, op0=OP.mult),
            wait=[trw], tick=True)
        t1_ = pr.op("scalar", lambda e: e.activation(
            wint[:, 0:ncols], wnat[:, 0:ncols], AF.Copy,
            scale=qcol[:, 0:1]), wait=[tiv], tick=True)
        qst["nat_free"] = t1_
        t2_ = pr.op("vector", lambda e: e.tensor_scalar(
            wint[:, 0:ncols], wint[:, 0:ncols], 127.0, -128.0, op0=OP.min, op1=OP.max),
            wait=[t1_], tick=True)
        t3_ = pr.op("scalar", lambda e: e.activation(
            wqn[:, 0:ncols], wint[:, 0:ncols], AF.Copy, scale=wrow[:, 0:1]),
            wait=[t2_, tdq, qst["wq_free"]], tick=True)
        toks = []
        for kk in range(ncols // P):
            tb_ = kk % 2
            ttr_ = tr(PTB[tb_][:, 0:P], wqn[:, P * kk:P * (kk + 1)], eye_b[:, :],
                      wait=[t3_, qst["ev"].get(tb_)], tick=True)
            qst["wq_free"] = ttr_
            qst["ev"][tb_] = pr.op("scalar", lambda e, kk=kk, tb_=tb_: e.copy(
                dstT[:, P * kk:P * (kk + 1)], PTB[tb_][:, 0:P]), wait=[ttr_], tick=True)
            toks.append(qst["ev"][tb_])
        return toks

    tok_psf = {}
    tok_sg_free = None
    tok_hb_free = None
    for mc in range(NC0):
        tg = quant_rows_to(wgT, w_gate, P * mc, D)
        tu = quant_rows_to(wuT, w_up, P * mc, D)
        for n in range(NB):
            for k in range(DT):
                mm(PS[3][:, :], wgT[:, P * k:P * (k + 1)], xqb[k][:, NBW * n:NBW * (n + 1)],
                   start=(k == 0), stop=(k == DT - 1),
                   wait=(tg + tu + tqdone if n == 0 and k == 0 else []) +
                        ([tok_psf.get(0)] if k == 0 else []))
            tgm_ = pr.tick("tensor")
            for k in range(DT):
                mm(PS[4][:, :], wuT[:, P * k:P * (k + 1)], xqb[k][:, NBW * n:NBW * (n + 1)],
                   start=(k == 0), stop=(k == DT - 1),
                   wait=[tok_psf.get(1)] if k == 0 else [])
            tum_ = pr.tick("tensor")

            ts_ = pr.op("scalar", lambda e: e.activation(
                sgt[:, :], PS[3][:, :], AF.Silu), wait=[tgm_, tok_sg_free], tick=True)
            tok_psf[0] = ts_
            th_ = pr.op("vector", lambda e: e.tensor_tensor(
                hbt[:, :], sgt[:, :], PS[4][:, :], op=OP.mult),
                wait=[ts_, tum_, tok_hb_free], tick=True)
            tok_psf[1] = th_
            tok_sg_free = th_
            if n == 0:
                pr.op("vector", lambda e, mc=mc: e.tensor_reduce(
                    hmaxc[:, mc:mc + 1], hbt[:, :], axis=AX, op=OP.max,
                    apply_absolute_value=True), wait=[th_], tick=True)
            else:
                trh_ = pr.op("vector", lambda e: e.tensor_reduce(
                    tmpc0[:, 0:1], hbt[:, :], axis=AX, op=OP.max,
                    apply_absolute_value=True), wait=[th_], tick=True)
                pr.op("vector", lambda e, mc=mc: e.tensor_tensor(
                    hmaxc[:, mc:mc + 1], hmaxc[:, mc:mc + 1], tmpc0[:, 0:1],
                    op=OP.max), wait=[trh_], tick=True)
            tok_hb_free = pr.op("scalar", lambda e, mc=mc, n=n: e.dma_start(
                out=h_sp[P * mc:P * (mc + 1), NBW * n:NBW * (n + 1)], in_=hbt[:, :]),
                wait=[th_], dma=True)
        pr.barrier()
        tok_psf = {}
        tok_sg_free = None
        tok_hb_free = None
    pr.barrier()
    pr.op("vector", lambda e: e.tensor_reduce(
        gmx[:, 0:1], hmaxc[:, 0:NC0], axis=AX, op=OP.max), tick=True)
    tgm = pr.tick("vector")
    tpm = part_max(mx11, gmx, 2, wait=[tgm])
    tcpm = pr.op("vector", lambda e: e.tensor_copy(mxrow[0:1, 0:1], mx11[0:1, 0:1]),
                 wait=[tpm], tick=True)
    tw_ = pr.op("sync", lambda e: e.dma_start(out=cc_mx_i[0:1, :], in_=mxrow[0:1, :]),
                wait=[tcpm], dma=True)
    tcc = pr.op("gpsimd", lambda e: e.collective_compute(
        "AllReduce", OP.max, replica_groups=[list(range(NCORE))],
        ins=[cc_mx_i[:, :]], outs=[cc_mx_o[:, :]]), wait=[tw_], tick=True)
    trd = pr.op("sync", lambda e: e.dma_start(out=mx11[0:1, 0:1], in_=cc_mx_o[0:1, 0:1]),
                wait=[tcc], dma=True)
    t_a = pr.op("vector", lambda e: e.tensor_tensor(
        sc_a[0:1, 0:1], mx11[0:1, 0:1], sig_cd[0:1, 0:1], op=OP.mult), wait=[trd], tick=True)
    t_b = pr.op("vector", lambda e: e.tensor_scalar(
        sc_b[0:1, 0:1], sc_a[0:1, 0:1], 1.0 / 127.0, None, op0=OP.mult),
        wait=[t_a], tick=True)
    t_r = pr.op("vector", lambda e: e.reciprocal(sc_a[0:1, 0:1], sc_a[0:1, 0:1]),
                wait=[t_b], tick=True)
    tqi = pr.op("vector", lambda e: e.tensor_scalar(
        sc_a[0:1, 0:1], sc_a[0:1, 0:1], 127.0, None, op0=OP.mult), wait=[t_r], tick=True)
    tbh1 = bcast_col(scol_a, sc_a, wait=[tqi], bank=2)
    tbc2 = bcast_col(scol_b, sc_b, wait=[tqi], bank=2)
    if debug_taps:
        pr.op("sync", lambda e: e.dma_start(
            out=dbg_sc[2][:, 0:1], in_=scol_a[:, 0:1]), wait=[tbh1], dma=True)
        pr.op("sync", lambda e: e.dma_start(
            out=dbg_sc[3][:, 0:1], in_=scol_b[:, 0:1]), wait=[tbc2], dma=True)
    pr.barrier()
    wdq_t = sb("wdqF0", [P, HID0], BF16)
    for m in range(DT):
        tks = quant_rows_to(wdq_t, w_down, P * m, HID0)
        tst_ = pr.op("scalar", lambda e, m=m: e.dma_start(
            out=wd_sp[P * m:P * (m + 1), :], in_=wdq_t[:, :]), wait=tks, dma=True)
        qst["wq_free"] = tst_
    pr.barrier()
    hqb = [sb("hqbF0_%d" % k, [P, NBW], BF16) for k in range(NC0)]
    wdl = [sb("wdlF0_%d" % i, [P, HID0], BF16) for i in range(2)]
    hin = sb("hinF0", [P, NBW], BF16)
    hii = sb("hiiF0", [P, NBW], I32)
    xs2 = [sb("xs2F0_%d" % m, [P, NBW]) for m in range(DT)]
    xr2 = [sb("xr2F0_%d" % m, [P, NBW]) for m in range(DT)]
    sqt2 = sb("sqt2F0", [P, NBW])
    rows2 = {"mrow": sb("mrow2", [1, NBW]), "m2row": sb("m2row2", [1, NBW]),
             "vrow": sb("vrow2", [1, NBW]), "rrow": sb("rrow2", [1, NBW]),
             "xo": sb("xo2", [P, NBW])}
    for n in range(NB):
        tq = [None] * NC0
        for k in range(NC0):
            tl = pr.op("sync", lambda e, k=k, n=n: e.dma_start(
                out=hin[:, :], in_=h_sp[P * k:P * (k + 1), NBW * n:NBW * (n + 1)]),
                wait=[tq[k - 1] if k else None], dma=True)
            t1_ = pr.op("scalar", lambda e: e.activation(
                hii[:, :], hin[:, :], AF.Copy, scale=scol_a[:, 0:1]), wait=[tl], tick=True)
            t2_ = pr.op("vector", lambda e: e.tensor_scalar(
                hii[:, :], hii[:, :], 127.0, -128.0, op0=OP.min, op1=OP.max),
                wait=[t1_], tick=True)
            tq[k] = pr.op("scalar", lambda e, k=k: e.activation(
                hqb[k][:, :], hii[:, :], AF.Copy, scale=scol_b[:, 0:1]),
                wait=[t2_], tick=True)
        txs = [None] * DT
        for m_ in range(DT):
            txs[m_] = pr.op("scalar", lambda e, m_=m_, n=n: e.dma_start(
                out=xs2[m_][:, :], in_=xres[P * m_:P * (m_ + 1), NBW * n:NBW * (n + 1)]),
                dma=True)
        xr_tok = [None] * DT
        tok_psf = {}
        tok_wdl = {0: None, 1: None}
        for m_ in range(DT):
            bank = m_ % 2
            twl = pr.op("gpsimd", lambda e, m_=m_, bank=bank: e.dma_start(
                out=wdl[bank][:, :], in_=wd_sp[P * m_:P * (m_ + 1), :]),
                wait=[tok_wdl[bank]], dma=True)
            for k in range(NC0):
                mm(PS[bank][:, :], wdl[bank][:, P * k:P * (k + 1)], hqb[k][:, :],
                   start=(k == 0), stop=(k == NC0 - 1),
                   wait=([tq[k]] if m_ < 2 else []) + ([twl] if k == 0 else []) +
                        ([tok_psf.get(bank)] if k == 0 else []))
            tdm = pr.tick("tensor")
            tok_wdl[bank] = tdm
            xr_tok[m_] = pr.op("vector", lambda e, m_=m_, bank=bank: e.tensor_tensor(
                xr2[m_][:, :], PS[bank][:, :], xs2[m_][:, :], op=OP.add),
                wait=[tdm, txs[m_]], tick=True)
            tok_psf[bank] = xr_tok[m_]
        rows2["xr_tok"] = xr_tok

        def out_fn2(m_, xo_t, tok, n=n):
            return pr.op("scalar", lambda e, m_=m_, n=n: e.dma_start(
                out=xres[P * m_:P * (m_ + 1), NBW * n:NBW * (n + 1)], in_=xo_t[:, :]),
                wait=[tok], dma=True)

        feat_ln_block(n, xr2, sqt2, rows2, None, (4, 5), out_fn2)
        pr.barrier()
    end()
    pr.op("sync", lambda e: e.dma_start(out=dbg_x2[:, :], in_=xres[:, :]), dma=True)
    pr.barrier()

    begin()
    halo3v = halo3_i[0:1, :].rearrange("one (d t) -> d (one t)", t=3)
    halo3ov = halo3_o[0:1, :].rearrange("one (d t) -> d (one t)", t=3)
    th_w = pr.op("sync", lambda e: e.dma_start(
        out=halo3v, in_=xres[:, TLOC - 3:TLOC]), dma=True)
    tcc = pr.op("gpsimd", lambda e: e.collective_compute(
        "AllGather", OP.bypass, replica_groups=[[0, 1], [2, 3], [4, 5], [6, 7]],
        ins=[halo3_i[:, :]], outs=[halo3_o[:, :]]), wait=[th_w], tick=True)
    ipw = [sb("ipwMA%d" % k, [P, 2 * INNER], BF16) for k in range(DT)]
    dtw = [sb("dtwMA%d" % k, [P, INNER], BF16) for k in range(16)]
    bw_sb = sb("bwMA", [P, 16 * 16], BF16)
    cw_sb2 = sb("cwMA", [P, 16 * 16], BF16)
    cwc = sb("cwcMA", [P, 16 * 4])
    cbc = sb("cbcMA", [P, 16])
    dtbc = sb("dtbMA", [P, 16])
    for k in range(DT):
        pr.op("gpsimd", lambda e, k=k: e.dma_start(
            out=ipw[k][:, :], in_=inp_wT[P * k:P * (k + 1), :]), dma=True)
    for k in range(16):
        pr.op("gpsimd", lambda e, k=k: e.dma_start(
            out=dtw[k][:, :], in_=dt_wT[P * k:P * (k + 1), :]), dma=True)
    for k in range(16):
        pr.op("gpsimd", lambda e, k=k: e.dma_start(
            out=bw_sb[:, 16 * k:16 * (k + 1)], in_=b_wT[P * k:P * (k + 1), :]), dma=True)
        pr.op("gpsimd", lambda e, k=k: e.dma_start(
            out=cw_sb2[:, 16 * k:16 * (k + 1)], in_=c_wT[P * k:P * (k + 1), :]), dma=True)
        pr.op("sync", lambda e, k=k: e.dma_start(
            out=cwc[:, 4 * k:4 * (k + 1)], in_=conv_w[P * k:P * (k + 1), :]), dma=True)
        pr.op("sync", lambda e, k=k: e.dma_start(
            out=cbc[:, k:k + 1], in_=conv_b[P * k:P * (k + 1), :]), dma=True)
        pr.op("sync", lambda e, k=k: e.dma_start(
            out=dtbc[:, k:k + 1], in_=dt_b[P * k:P * (k + 1), :]), dma=True)
    xbm = [sb("xbMA%d" % k, [P, NBW], BF16) for k in range(DT)]
    xh = sb("xhMA", [P, 3])
    xhb = [sb("xhbMA%d" % k, [P, 3], BF16) for k in range(DT)]
    xin_ext = [sb("xinMA%d" % m, [P, 515]) for m in range(16)]
    cv1 = sb("cv1MA", [P, NBW])
    cv2 = sb("cv2MA", [P, NBW])
    xin_b = [sb("xinbMA%d" % m, [P, NBW], BF16) for m in range(16)]
    gt = sb("gtMA", [P, NBW], BF16)
    ztile = sb("ztMA", [P, NBW])
    abst = cv1
    expt = cv2
    lnt = cv1
    dtv = sb("dtvMA", [P, NBW])
    dxv = sb("dxvMA", [P, NBW], BF16)
    brow = sb("browMA", [16, NBW])
    pr.barrier()
    thbs = []
    for k in range(DT):
        tl = pr.op("sync", lambda e, k=k: e.dma_start(
            out=xh[:, :], in_=halo3ov[P * k:P * (k + 1), :]),
            wait=[tcc, thbs[-1] if thbs else None], dma=True)
        tm_ = pr.op("vector", lambda e: e.tensor_scalar(
            xh[:, :], xh[:, :], hcol[:, 0:1], None, op0=OP.mult), wait=[tl], tick=True)
        thbs.append(pr.op("vector", lambda e, k=k: e.tensor_copy(xhb[k][:, :], xh[:, :]),
                          wait=[tm_], tick=True))
    thb = pr.tick("vector")
    tgw = pr.dma_tokens("gpsimd")
    tprev_ev = None
    for m_ in range(16):
        for k in range(DT):
            mm(PS[2][:, 0:3], ipw[k][:, P * m_:P * (m_ + 1)], xhb[k][:, :],
               start=(k == 0), stop=(k == DT - 1),
               wait=([thb, tprev_ev] + tgw) if k == 0 else [])
        th_ = pr.tick("tensor")
        tprev_ev = pr.op("vector", lambda e, m_=m_: e.tensor_copy(
            xin_ext[m_][:, 0:3], PS[2][:, 0:3]), wait=[th_], tick=True)
    pr.barrier()
    for n in range(NB):
        for k in range(DT):
            pr.op("gpsimd", lambda e, k=k, n=n: e.dma_start(
                out=xbm[k][:, :], in_=xres[P * k:P * (k + 1), NBW * n:NBW * (n + 1)]),
                dma=True)
        tgl = pr.dma_tokens("gpsimd")
        if n > 0:
            for m_ in range(16):
                pr.op("vector", lambda e, m_=m_: e.tensor_copy(
                    xin_ext[m_][:, 0:3], xin_ext[m_][:, NBW:NBW + 3]), tick=True)
        tok_tail = pr.tick("vector")
        tok_psf = {}
        tev_x = [None] * 16
        tok_g = None
        for m_ in range(32):
            bank = m_ % 2
            for k in range(DT):
                mm(PS[bank][:, :], ipw[k][:, P * m_:P * (m_ + 1)], xbm[k][:, :],
                   start=(k == 0), stop=(k == DT - 1),
                   wait=((tgl + [tok_tail]) if m_ < 2 and k == 0 else []) +
                        ([tok_psf.get(bank)] if k == 0 else []))
            tmm_ = pr.tick("tensor")
            if m_ < 16:
                tev_x[m_] = pr.op("vector", lambda e, m_=m_, bank=bank: e.tensor_copy(
                    xin_ext[m_][:, 3:3 + NBW], PS[bank][:, :]), wait=[tmm_], tick=True)
                tok_psf[bank] = tev_x[m_]
            else:
                tsl = pr.op("scalar", lambda e, bank=bank: e.activation(
                    gt[:, :], PS[bank][:, :], AF.Silu), wait=[tmm_, tok_g], tick=True)
                tok_psf[bank] = tsl
                r = m_ - 16
                tok_g = pr.op("scalar", lambda e, r=r, n=n: e.dma_start(
                    out=a2a_g_i[r // 2, P * (r % 2):P * (r % 2) + P, NBW * n:NBW * (n + 1)],
                    in_=gt[:, :]), wait=[tsl], dma=True)
        tsil = [None] * 16
        tok_cv = None
        for m_ in range(16):
            t1_ = pr.op("vector", lambda e, m_=m_: e.tensor_scalar(
                cv1[:, :], xin_ext[m_][:, 0:NBW], cwc[:, 4 * m_:4 * m_ + 1],
                cbc[:, m_:m_ + 1], op0=OP.mult, op1=OP.add),
                wait=[tev_x[m_], tok_cv], tick=True)
            pr.op("vector", lambda e, m_=m_: e.scalar_tensor_tensor(
                cv2[:, :], xin_ext[m_][:, 1:1 + NBW], cwc[:, 4 * m_ + 1:4 * m_ + 2],
                cv1[:, :], op0=OP.mult, op1=OP.add), tick=True)
            pr.op("vector", lambda e, m_=m_: e.scalar_tensor_tensor(
                cv1[:, :], xin_ext[m_][:, 2:2 + NBW], cwc[:, 4 * m_ + 2:4 * m_ + 3],
                cv2[:, :], op0=OP.mult, op1=OP.add), tick=True)
            t4_ = pr.op("vector", lambda e, m_=m_: e.scalar_tensor_tensor(
                cv2[:, :], xin_ext[m_][:, 3:3 + NBW], cwc[:, 4 * m_ + 3:4 * m_ + 4],
                cv1[:, :], op0=OP.mult, op1=OP.add), tick=True)
            tsil[m_] = pr.op("scalar", lambda e, m_=m_: e.activation(
                xin_b[m_][:, :], cv2[:, :], AF.Silu), wait=[t4_], tick=True)
            tok_cv = tsil[m_]
        tok_psf = {}
        tok_z = None
        tok_dtv = None
        tok_dxv = None
        tok_ln = None
        tok_abst = None
        for md in range(16):
            bank = md % 2
            for k in range(16):
                mm(PS[bank][:, :], dtw[k][:, P * md:P * (md + 1)], xin_b[k][:, :],
                   start=(k == 0), stop=(k == 15),
                   wait=([tsil[k]] if md < 2 else []) +
                        ([tok_psf.get(bank)] if k == 0 else []))
            tmm_ = pr.tick("tensor")
            tz = pr.op("vector", lambda e, md=md, bank=bank: e.tensor_scalar(
                ztile[:, :], PS[bank][:, :], dtbc[:, md:md + 1], None, op0=OP.add),
                wait=[tmm_, tok_z], tick=True)
            tok_psf[bank] = tz
            ta = pr.op("vector", lambda e: e.scalar_tensor_tensor(
                abst[:, :], ztile[:, :], -1.0, ztile[:, :], op0=OP.mult, op1=OP.max),
                wait=[tz, tok_abst], tick=True)
            te = pr.op("scalar", lambda e: e.activation(
                expt[:, :], abst[:, :], AF.Exp, scale=-1.0), wait=[ta], tick=True)
            tok_abst = te
            tl_ = pr.op("scalar", lambda e: e.activation(
                lnt[:, :], expt[:, :], AF.Ln, bias=1.0), wait=[tok_ln], tick=True)
            td = pr.op("vector", lambda e: e.scalar_tensor_tensor(
                dtv[:, :], ztile[:, :], 0.0, lnt[:, :], op0=OP.max, op1=OP.add),
                wait=[tl_, tok_dtv], tick=True)
            tok_z = td
            tok_ln = td
            tx_ = pr.op("vector", lambda e, md=md: e.tensor_tensor(
                dxv[:, :], dtv[:, :], xin_b[md][:, :], op=OP.mult),
                wait=[td, tok_dxv], tick=True)
            tok_dtv = pr.op("scalar", lambda e, md=md, n=n: e.dma_start(
                out=a2a_dt_i[md // 2, P * (md % 2):P * (md % 2) + P, NBW * n:NBW * (n + 1)],
                in_=dtv[:, :]), wait=[td], dma=True)
            tok_dxv = pr.op("sync", lambda e, md=md, n=n: e.dma_start(
                out=a2a_dx_i[md // 2, P * (md % 2):P * (md % 2) + P, NBW * n:NBW * (n + 1)],
                in_=dxv[:, :]), wait=[tx_], dma=True)
        for wsb, rrow0 in ((bw_sb, 0), (cw_sb2, 16)):
            for k in range(16):
                mm(PS[2][0:16, :], wsb[:, 16 * k:16 * (k + 1)], xin_b[k][:, :],
                   start=(k == 0), stop=(k == 15),
                   wait=[rows_g.get("bc_free")] if k == 0 else [])
            tmm_ = pr.tick("tensor")
            tbr = pr.op("scalar", lambda e: e.copy(brow[0:16, :], PS[2][0:16, :]),
                        wait=[tmm_, rows_g.get("bc_st")], tick=True)
            rows_g["bc_free"] = tbr
            rows_g["bc_st"] = pr.op("sync", lambda e, rrow0=rrow0, n=n: e.dma_start(
                out=bc_ag_i[rrow0:rrow0 + 16, NBW * n:NBW * (n + 1)], in_=brow[0:16, :]),
                wait=[tbr], dma=True)
        pr.barrier()
    tg1 = pr.op("gpsimd", lambda e: e.collective_compute(
        "AllGather", OP.bypass, replica_groups=[list(range(NCORE))],
        ins=[bc_ag_i[:, :]], outs=[bc_ag_o[:, :]]), tick=True)
    tg2 = pr.op("gpsimd", lambda e: e.collective_compute(
        "AllToAll", OP.bypass, replica_groups=[list(range(NCORE))],
        ins=[a2a_dt_i[:, :, :]], outs=[a2a_dt_o[:, :, :]]), tick=True)
    tg3 = pr.op("gpsimd", lambda e: e.collective_compute(
        "AllToAll", OP.bypass, replica_groups=[list(range(NCORE))],
        ins=[a2a_dx_i[:, :, :]], outs=[a2a_dx_o[:, :, :]]), tick=True)
    tg4 = pr.op("gpsimd", lambda e: e.collective_compute(
        "AllToAll", OP.bypass, replica_groups=[list(range(NCORE))],
        ins=[a2a_g_i[:, :, :]], outs=[a2a_g_o[:, :, :]]), tick=True)
    end()

    begin()
    sel_sb = sb("sel_sb", [16, 16 * P])
    pr.op("sync", lambda e: e.dma_start(out=sel_sb[:, :], in_=c_sel16[:, :]), dma=True)
    dt_sb = sb("dtMB", [P, 2 * TLOC])
    dx_sb = sb("dxMB", [P, 2 * TLOC], BF16)
    g_sb = sb("gMB", [P, 2 * TLOC], BF16)
    y_sb = sb("yMB", [P, 2 * TLOC])
    yb_sb = sb("ybMB", [P, 2 * TLOC], BF16)
    bg_sb = sb("bgMB", [16, 2 * TLOC])
    cg_sb = sb("cgMB", [16, 2 * TLOC])
    a_t = [sb("atMB%d" % i, [P, NBW]) for i in range(2)]
    u_t = [sb("utMB%d" % i, [P, NBW]) for i in range(2)]
    st_t = [sb("stMB%d" % i, [P, NBW]) for i in range(2)]
    w_t = sb("wtMB", [P, NBW])
    TCN = 2 * TLOC // NBW
    tok_yst = None
    for b_ in range(4):
        tbgl = []
        for hh in range(2):
            tbgl.append(pr.op("sync", lambda e, b_=b_, hh=hh: e.dma_start(
                out=bg_sb[0:16, TLOC * hh:TLOC * (hh + 1)],
                in_=bc_ag_o[32 * (2 * b_ + hh):32 * (2 * b_ + hh) + 16, :]),
                wait=[tg1], dma=True))
            tbgl.append(pr.op("sync", lambda e, b_=b_, hh=hh: e.dma_start(
                out=cg_sb[0:16, TLOC * hh:TLOC * (hh + 1)],
                in_=bc_ag_o[32 * (2 * b_ + hh) + 16:32 * (2 * b_ + hh) + 32, :]),
                dma=True))
        for dtile in range(2):
            r0 = P * dtile
            tld = []
            for hh in range(2):
                tld.append(pr.op("sync", lambda e, b_=b_, hh=hh, r0=r0: e.dma_start(
                    out=dt_sb[:, TLOC * hh:TLOC * (hh + 1)],
                    in_=a2a_dt_o[2 * b_ + hh, r0:r0 + P, :]), wait=[tg2], dma=True))
                tld.append(pr.op("scalar", lambda e, b_=b_, hh=hh, r0=r0: e.dma_start(
                    out=dx_sb[:, TLOC * hh:TLOC * (hh + 1)],
                    in_=a2a_dx_o[2 * b_ + hh, r0:r0 + P, :]), wait=[tg3], dma=True))
                tld.append(pr.op("gpsimd", lambda e, b_=b_, hh=hh, r0=r0: e.dma_start(
                    out=g_sb[:, TLOC * hh:TLOC * (hh + 1)],
                    in_=a2a_g_o[2 * b_ + hh, r0:r0 + P, :]), wait=[tg4], dma=True))
            tym = pr.op("vector", lambda e: e.memset(y_sb[:, :], 0.0),
                        wait=[tok_yst], tick=True)
            tok_u = {}
            tok_w = {}
            tok_afree = {0: None, 1: None}
            tok_scan = {0: None, 1: None}
            for s_ in range(NSTATE):
                for tc in range(TCN):
                    bb = 2 + 2 * (tc % 2)
                    bcq = 3 + 2 * (tc % 2)
                    ia = tc % 2
                    tB = mm(PS[bb][:, :], sel_sb[:, P * s_:P * (s_ + 1)],
                            bg_sb[0:16, NBW * tc:NBW * (tc + 1)], start=True, stop=True,
                            wait=[tok_u.get(bb)] + tbgl, tick=True)
                    tC = mm(PS[bcq][:, :], sel_sb[:, P * s_:P * (s_ + 1)],
                            cg_sb[0:16, NBW * tc:NBW * (tc + 1)], start=True, stop=True,
                            wait=[tok_w.get(bcq)], tick=True)
                    ta_ = pr.op("scalar",
                                lambda e, s_=s_, tc=tc, ia=ia, dtile=dtile: e.activation(
                                    a_t[ia][:, :], dt_sb[:, NBW * tc:NBW * (tc + 1)],
                                    AF.Exp, scale=aneg[dtile][:, s_:s_ + 1]),
                                wait=[tok_afree[ia], tld[0], tld[3]], tick=True)
                    tu_ = pr.op("vector", lambda e, tc=tc, ia=ia, bb=bb: e.tensor_tensor(
                        u_t[ia][:, :], dx_sb[:, NBW * tc:NBW * (tc + 1)], PS[bb][:, :],
                        op=OP.mult), wait=[tB, tld[1], tld[4]], tick=True)
                    tok_u[bb] = tu_
                    init = 0.0 if tc == 0 else st_t[(tc + 1) % 2][:, NBW - 1:NBW]
                    tst_ = pr.op("vector", lambda e, ia=ia, init=init: e.tensor_tensor_scan(
                        st_t[ia][:, :], a_t[ia][:, :], u_t[ia][:, :], init,
                        op0=OP.mult, op1=OP.add), wait=[ta_], tick=True)
                    tok_afree[ia] = tst_
                    tw_ = pr.op("vector", lambda e, ia=ia, bcq=bcq: e.tensor_tensor(
                        w_t[:, :], st_t[ia][:, :], PS[bcq][:, :], op=OP.mult),
                        wait=[tC], tick=True)
                    tok_w[bcq] = tw_
                    pr.op("vector", lambda e, tc=tc: e.tensor_tensor(
                        y_sb[:, NBW * tc:NBW * (tc + 1)], y_sb[:, NBW * tc:NBW * (tc + 1)],
                        w_t[:, :], op=OP.add), wait=[tym], tick=True)
            tyv = pr.tick("vector")
            tyb = pr.op("vector", lambda e: e.tensor_tensor(
                yb_sb[:, :], y_sb[:, :], g_sb[:, :], op=OP.mult),
                wait=[tyv, tld[2], tld[5]], tick=True)
            ty1 = pr.op("sync", lambda e, b_=b_, r0=r0: e.dma_start(
                out=a2a_y_i[2 * b_, r0:r0 + P, :], in_=yb_sb[:, 0:TLOC]),
                wait=[tyb], dma=True)
            ty2 = pr.op("sync", lambda e, b_=b_, r0=r0: e.dma_start(
                out=a2a_y_i[2 * b_ + 1, r0:r0 + P, :], in_=yb_sb[:, TLOC:2 * TLOC]),
                wait=[tyb], dma=True)
            tok_yst = ty2
            pr.barrier()
    tg5 = pr.op("gpsimd", lambda e: e.collective_compute(
        "AllToAll", OP.bypass, replica_groups=[list(range(NCORE))],
        ins=[a2a_y_i[:, :, :]], outs=[a2a_y_o[:, :, :]]), tick=True)
    end()

    begin()
    owT = [sb("owMC%d" % k, [P, D], BF16) for k in range(16)]
    for k in range(16):
        pr.op("gpsimd", lambda e, k=k: e.dma_start(
            out=owT[k][:, :], in_=out_wT[P * k:P * (k + 1), :]), dma=True)
    ytl = [sb("ytlMC%d" % k, [P, NBW], BF16) for k in range(16)]
    xs3 = [sb("xs3MC%d" % m, [P, NBW]) for m in range(DT)]
    xr3 = [sb("xr3MC%d" % m, [P, NBW]) for m in range(DT)]
    sqt3 = sb("sqt3MC", [P, NBW])
    rows3 = {"mrow": sb("mrow3", [1, NBW]), "m2row": sb("m2row3", [1, NBW]),
             "vrow": sb("vrow3", [1, NBW]), "rrow": sb("rrow3", [1, NBW]),
             "xo": sb("xo3", [P, NBW])}
    pr.barrier()
    for n in range(NB):
        ty = [None] * 16
        for k in range(16):
            ty[k] = pr.op("sync", lambda e, k=k, n=n: e.dma_start(
                out=ytl[k][:, :],
                in_=a2a_y_o[k // 2, P * (k % 2):P * (k % 2) + P, NBW * n:NBW * (n + 1)]),
                wait=[tg5], dma=True)
        txs = [None] * DT
        for m_ in range(DT):
            txs[m_] = pr.op("scalar", lambda e, m_=m_, n=n: e.dma_start(
                out=xs3[m_][:, :], in_=xres[P * m_:P * (m_ + 1), NBW * n:NBW * (n + 1)]),
                dma=True)
        xr_tok = [None] * DT
        tok_psf = {}
        for m_ in range(DT):
            bank = m_ % 2
            for k in range(16):
                mm(PS[bank][:, :], owT[k][:, P * m_:P * (m_ + 1)], ytl[k][:, :],
                   start=(k == 0), stop=(k == 15),
                   wait=([ty[k]] + pr.dma_tokens("gpsimd") if m_ < 2 else []) +
                        ([tok_psf.get(bank)] if k == 0 else []))
            tdm = pr.tick("tensor")
            xr_tok[m_] = pr.op("vector", lambda e, m_=m_, bank=bank: e.tensor_tensor(
                xr3[m_][:, :], PS[bank][:, :], xs3[m_][:, :], op=OP.add),
                wait=[tdm, txs[m_]], tick=True)
            tok_psf[bank] = xr_tok[m_]
        rows3["xr_tok"] = xr_tok

        def out_fn3(m_, xo_t, tok, n=n):
            return pr.op("scalar", lambda e, m_=m_, n=n: e.dma_start(
                out=xres[P * m_:P * (m_ + 1), NBW * n:NBW * (n + 1)], in_=xo_t[:, :]),
                wait=[tok], dma=True)

        feat_ln_block(n, xr3, sqt3, rows3, None, (4, 5), out_fn3)
        pr.barrier()
    end()

    begin()
    halo1vA = halo1_i[0:1, 0:8 * D].rearrange("one (d t) -> d (one t)", t=8)
    halo1vB = halo1_i[0:1, 8 * D:16 * D].rearrange("one (d t) -> d (one t)", t=8)
    halo1oA = halo1_o[1:2, 0:8 * D].rearrange("one (d t) -> d (one t)", t=8)
    halo1oB = halo1_o[0:1, 8 * D:16 * D].rearrange("one (d t) -> d (one t)", t=8)
    tw1 = pr.op("sync", lambda e: e.dma_start(out=halo1vA, in_=xres[:, 0:8]), dma=True)
    tw2 = pr.op("sync", lambda e: e.dma_start(out=halo1vB,
                                              in_=xres[:, TLOC - 8:TLOC]), dma=True)
    tcch = pr.op("gpsimd", lambda e: e.collective_compute(
        "AllGather", OP.bypass, replica_groups=[[0, 1], [2, 3], [4, 5], [6, 7]],
        ins=[halo1_i[:, :]], outs=[halo1_o[:, :]]), wait=[tw1, tw2], tick=True)
    hmacc = sb("hmaccF1", [P, TLOC], BF16)
    s2cols = sb("s2colsF1", [P, 32])
    is2cols = sb("is2colsF1", [P, 32])
    xq_hal = [sb("xqhF1_%d" % k, [P, 2], BF16) for k in range(DT)]
    sr32 = sb("sr32F1", [32, P])
    begin()
    xqb_f = [sb("xqbF1_%d" % k, [P, TLOC], BF16) for k in range(DT)]
    begin()
    xfm = [sb("xfmF1_%d" % i, [P, P]) for i in range(2)]
    xtm = sb("xtmF1", [P, D])
    xqtm = sb("xqtmF1", [P, D], BF16)
    xitm = sb("xitmF1", [P, D], I32)
    tkmax = sb("tkmaxF1", [P, 1])
    tscl = sb("tsclF1", [P, 1])
    tiscl = sb("tisclF1", [P, 1])
    tok_xfm = {0: None, 1: None}
    tok_trev = {0: None, 1: None, "b0": None, "b1": None}
    tok_round = None
    tok_xtm = None
    for tb in range(TLOC // P):
        tcps = []
        for f in range(DT):
            i2 = f % 2
            bk = 2 + i2
            tlx = pr.op("sync", lambda e, f=f, tb=tb, i2=i2: e.dma_start(
                out=xfm[i2][:, :], in_=xres[P * f:P * (f + 1), P * tb:P * (tb + 1)]),
                wait=[tok_xfm[i2]], dma=True)
            ttr_ = tr(PS[bk][:, 0:P], xfm[i2][:, :], eye_f[:, :],
                      wait=[tlx, tok_trev[i2]], tick=True)
            tok_xfm[i2] = ttr_
            tcp = pr.op("scalar", lambda e, f=f, bk=bk: e.copy(
                xtm[:, P * f:P * (f + 1)], PS[bk][:, 0:P]),
                wait=[ttr_, tok_xtm], tick=True)
            tok_trev[i2] = tcp
            tcps.append(tcp)
        tmx = pr.op("vector", lambda e: e.tensor_reduce(
            tkmax[:, 0:1], xtm[:, :], axis=AX, op=OP.max,
            apply_absolute_value=True), wait=tcps, tick=True)
        tkc = pr.op("vector", lambda e: e.tensor_scalar(
            tkmax[:, 0:1], tkmax[:, 0:1], 1e-5, None, op0=OP.max),
            wait=[tmx], tick=True)
        trk = pr.op("vector", lambda e: e.reciprocal(tscl[:, 0:1], tkmax[:, 0:1]),
                    wait=[tkc], tick=True)
        tsc = pr.op("vector", lambda e: e.tensor_scalar(
            tscl[:, 0:1], tscl[:, 0:1], 127.0, None, op0=OP.mult), wait=[trk], tick=True)
        tis = pr.op("vector", lambda e: e.tensor_scalar(
            tiscl[:, 0:1], tkmax[:, 0:1], 1.0 / 127.0, None, op0=OP.mult),
            wait=[tok_round], tick=True)
        tri = pr.op("scalar", lambda e: e.activation(
            xitm[:, :], xtm[:, :], AF.Copy, scale=tscl[:, 0:1]),
            wait=[tsc, tok_round], tick=True)
        tok_xtm = tri
        trq = pr.op("scalar", lambda e: e.activation(
            xqtm[:, :], xitm[:, :], AF.Copy, scale=tiscl[:, 0:1]),
            wait=[tis, tok_trev["b0"], tok_trev["b1"]], tick=True)
        tok_round = trq
        for f in range(DT):
            i2 = f % 2
            ttr_ = tr(PTB[i2][:, 0:P], xqtm[:, P * f:P * (f + 1)], eye_b[:, :],
                      wait=[trq, tok_trev["b%d" % i2]], tick=True)
            tok_trev["b%d" % i2] = pr.op("scalar", lambda e, f=f, tb=tb, i2=i2: e.copy(
                xqb_f[f][:, P * tb:P * (tb + 1)], PTB[i2][:, 0:P]), wait=[ttr_], tick=True)
    pr.barrier()
    if debug_taps:
        for k in range(DT):
            pr.op("sync", lambda e, k=k: e.dma_start(
                out=dbg_f1xq[P * k:P * (k + 1), :], in_=xqb_f[k][:, :]), dma=True)
    xcol = sb("xcolF1", [P, 2])
    xci = sb("xciF1", [P, 2], I32)
    hmx2 = sb("hmx2F1", [P, 2])
    m11a = sb("m11aF1", [1, 1])
    colsc = sb("colscF1", [P, 2])
    xcl = [sb("xclF1_%d" % k, [P, 2]) for k in range(DT)]
    xcl8 = sb("xcl8F1", [P, 8])
    tmsk = []
    tl2 = None
    for k in range(DT):
        tl1 = pr.op("sync", lambda e, k=k: e.dma_start(
            out=xcl8[:, :], in_=halo1oB[P * k:P * (k + 1), :]),
            wait=[tcch, tl2], dma=True)
        tc1 = pr.op("vector", lambda e, k=k: e.tensor_copy(
            xcl[k][:, 0:1], xcl8[:, 7:8]), wait=[tl1], tick=True)
        tl2 = pr.op("sync", lambda e, k=k: e.dma_start(
            out=xcl8[:, :], in_=halo1oA[P * k:P * (k + 1), :]), wait=[tc1], dma=True)
        tc2 = pr.op("vector", lambda e, k=k: e.tensor_copy(
            xcl[k][:, 1:2], xcl8[:, 0:1]), wait=[tl2], tick=True)
        tm1 = pr.op("vector", lambda e, k=k: e.tensor_scalar(
            xcl[k][:, 0:1], xcl[k][:, 0:1], hcol[:, 0:1], None, op0=OP.mult),
            wait=[tc1], tick=True)
        tm2 = pr.op("vector", lambda e, k=k: e.tensor_scalar(
            xcl[k][:, 1:2], xcl[k][:, 1:2], rcol[:, 0:1], None, op0=OP.mult),
            wait=[tc2], tick=True)
        tmsk.append(tm2)
        if k == 0:
            pr.op("vector", lambda e, k=k: e.scalar_tensor_tensor(
                hmx2[:, :], xcl[k][:, :], -1.0, xcl[k][:, :], op0=OP.mult, op1=OP.max),
                wait=[tm1, tm2], tick=True)
        else:
            pr.op("vector", lambda e, k=k: e.scalar_tensor_tensor(
                xcol[:, :], xcl[k][:, :], -1.0, xcl[k][:, :], op0=OP.mult, op1=OP.max),
                wait=[tm1, tm2], tick=True)
            pr.op("vector", lambda e: e.tensor_tensor(
                hmx2[:, :], hmx2[:, :], xcol[:, :], op=OP.max), tick=True)
    thx = pr.tick("vector")
    for cidx in range(2):
        tpm_ = part_max(m11a, hmx2[:, cidx:cidx + 1], 2, wait=[thx])
        pr.op("vector", lambda e: e.tensor_scalar(
            m11a[0:1, 0:1], m11a[0:1, 0:1], 1e-5, None, op0=OP.max), wait=[tpm_], tick=True)
        tbs = bcast_col(colsc, m11a, wait=[pr.tick("vector")], bank=2)
        tcl = pr.op("vector", lambda e, cidx=cidx: e.tensor_scalar(
            colsc[:, 0:1], colsc[:, 0:1], 1e30, None, op0=OP.min), wait=[tbs], tick=True)
        trc_ = pr.op("vector", lambda e: e.reciprocal(colsc[:, 1:2], colsc[:, 0:1]),
                     wait=[tcl], tick=True)
        ts1 = pr.op("vector", lambda e: e.tensor_scalar(
            colsc[:, 1:2], colsc[:, 1:2], 127.0, None, op0=OP.mult),
            wait=[trc_], tick=True)
        ts2 = pr.op("vector", lambda e: e.tensor_scalar(
            colsc[:, 0:1], colsc[:, 0:1], 1.0 / 127.0, None, op0=OP.mult), tick=True)
        for k in range(DT):
            t1_ = pr.op("scalar", lambda e, k=k, cidx=cidx: e.activation(
                xci[:, cidx:cidx + 1], xcl[k][:, cidx:cidx + 1], AF.Copy,
                scale=colsc[:, 1:2]), wait=[ts1, ts2], tick=True)
            pr.op("scalar", lambda e, k=k, cidx=cidx: e.activation(
                xq_hal[k][:, cidx:cidx + 1], xci[:, cidx:cidx + 1], AF.Copy,
                scale=colsc[:, 0:1]), tick=True)
    end()
    w1q = [sb("w1qF1_%d" % k, [P, HID1], BF16) for k in range(DT)]
    begin()
    wnat1 = sb("wnat1F1", [P, HID1])
    wacc = sb("waccF1", [P, 1])
    wi1 = sb("wi1F1", [P, HID1], I32)
    s1col = sb("s1colF1", [P, 1])
    d1col = sb("d1colF1", [P, 1])
    tqtmp = sb("tqtmpF1", [P, 1])
    m11w = sb("m11wF1", [1, 1])

    def ternary_quant(dst_list, src_dram, nkc, ncols, nel):
        tred = None
        tldw = None
        for k in range(nkc):
            tldw = pr.op("sync", lambda e, k=k: e.dma_start(
                out=wnat1[:, 0:ncols], in_=src_dram[P * k:P * (k + 1), 0:ncols]),
                wait=[tred], dma=True)
            if k == 0:
                tred = pr.op("vector", lambda e: e.tensor_reduce(
                    wacc[:, 0:1], wnat1[:, 0:ncols], axis=AX, op=OP.add,
                    apply_absolute_value=True), wait=[tldw], tick=True)
            else:
                trt_ = pr.op("vector", lambda e: e.tensor_reduce(
                    tqtmp[:, 0:1], wnat1[:, 0:ncols], axis=AX, op=OP.add,
                    apply_absolute_value=True), wait=[tldw], tick=True)
                tred = pr.op("vector", lambda e: e.tensor_tensor(
                    wacc[:, 0:1], wacc[:, 0:1], tqtmp[:, 0:1], op=OP.add),
                    wait=[trt_], tick=True)
        if debug_taps:
            pr.op("sync", lambda e: e.dma_start(
                out=dbg_tqw[:, 0:1], in_=wacc[:, 0:1]), wait=[tred], dma=True)
        ttp = tr(PS[2][0:1, 0:P], wacc[:, 0:1], eye_f[:, :],
                 wait=[tred, _bc_last.get(2)], tick=True)
        tcp = pr.op("scalar", lambda e: e.copy(rows_g["tmprow"][0:1, 0:P],
                                               PS[2][0:1, 0:P]),
                    wait=[ttp, rows_g.get("tmprow_free")], tick=True)
        _bc_last[2] = tcp
        if debug_taps:
            tcp = pr.op("sync", lambda e: e.dma_start(
                out=dbg_tqr[0:1, :], in_=rows_g["tmprow"][0:1, 0:P]),
                wait=[tcp], dma=True)
        rows_g["tmprow_free"] = pr.op("vector", lambda e: e.tensor_reduce(
            m11w[0:1, 0:1], rows_g["tmprow"][0:1, 0:P], axis=AX, op=OP.add),
            wait=[tcp], tick=True)
        tnl = pr.op("vector", lambda e, nel=nel: e.tensor_scalar(
            m11w[0:1, 0:1], m11w[0:1, 0:1], 1.0 / nel, 1e-5, op0=OP.mult, op1=OP.max),
            wait=[rows_g["tmprow_free"]], tick=True)
        if debug_taps:
            pr.op("sync", lambda e: e.dma_start(
                out=dbg_tqm[0:1, 0:1], in_=m11w[0:1, 0:1]), wait=[tnl], dma=True)
        tmn = pr.tick("vector")
        td1 = bcast_col(d1col, m11w, wait=[tmn], bank=2)
        pr.op("vector", lambda e: e.reciprocal(m11w[0:1, 0:1], m11w[0:1, 0:1]),
              wait=[td1], tick=True)
        tiv = pr.tick("vector")
        ts1_ = bcast_col(s1col, m11w, wait=[tiv], bank=2)
        tw_free = None
        for k in range(nkc):
            tldw = pr.op("sync", lambda e, k=k: e.dma_start(
                out=wnat1[:, 0:ncols], in_=src_dram[P * k:P * (k + 1), 0:ncols]),
                wait=[tw_free, ts1_ if k == 0 else None], dma=True)
            t1_ = pr.op("scalar", lambda e: e.activation(
                wi1[:, 0:ncols], wnat1[:, 0:ncols], AF.Copy, scale=s1col[:, 0:1]),
                wait=[tldw, ts1_], tick=True)
            tw_free = t1_
            t2_ = pr.op("vector", lambda e: e.tensor_scalar(
                wi1[:, 0:ncols], wi1[:, 0:ncols], 1.0, -1.0, op0=OP.min, op1=OP.max),
                wait=[t1_], tick=True)
            pr.op("scalar", lambda e, k=k: e.activation(
                dst_list[k][:, 0:ncols], wi1[:, 0:ncols], AF.Copy, scale=d1col[:, 0:1]),
                wait=[t2_], tick=True)
        return pr.tick("scalar")

    tq1w = ternary_quant(w1q, w1T, DT, HID1, HID1 * D)
    if debug_taps:
        pr.op("sync", lambda e: e.dma_start(
            out=dbg_tqd[:, 0:1], in_=d1col[:, 0:1]), wait=[tq1w], dma=True)
        pr.op("sync", lambda e: e.dma_start(
            out=dbg_tqs[:, 0:1], in_=s1col[:, 0:1]), wait=[tq1w], dma=True)
    if debug_taps:
        for k in range(DT):
            pr.op("sync", lambda e, k=k: e.dma_start(
                out=dbg_f1w1[P * k:P * (k + 1), :], in_=w1q[k][:, :]),
                wait=[tq1w], dma=True)
    end()
    hx = sb("hxF1", [P, TLOC + 2])
    cbuf = sb("cbufF1", [P, TLOC])
    hsb = sb("hsbF1", [P, TLOC], BF16)
    habs = cbuf
    cwf = sb("cwfF1", [P, 3])
    cbf = sb("cbfF1", [P, 1])
    tok_hxf = None
    tok_hs = None
    tok_hab = None
    tok_cvfree = None
    for mc in range(HID1 // P):
        tcw = pr.op("sync", lambda e, mc=mc: e.dma_start(
            out=cwf[:, :], in_=cw1[P * mc:P * (mc + 1), :]),
            wait=[tok_cvfree], dma=True)
        tcb = pr.op("sync", lambda e, mc=mc: e.dma_start(
            out=cbf[:, 0:1], in_=cb1[P * mc:P * (mc + 1), :]),
            wait=[tok_cvfree], dma=True)
        for cidx, col in ((0, 0), (1, TLOC + 1)):
            for k in range(DT):
                mm(PS[2][:, 0:1], w1q[k][:, P * mc:P * (mc + 1)],
                   xq_hal[k][:, cidx:cidx + 1], start=(k == 0), stop=(k == DT - 1),
                   wait=[tq1w, tok_hxf] if k == 0 else [])
            th_ = pr.tick("tensor")
            tok_hxf = pr.op("vector", lambda e, col=col: e.tensor_copy(
                hx[:, col:col + 1], PS[2][:, 0:1]), wait=[th_], tick=True)
        for n in range(NB):
            bank = n % 2
            for k in range(DT):
                mm(PS[bank][:, :], w1q[k][:, P * mc:P * (mc + 1)],
                   xqb_f[k][:, NBW * n:NBW * (n + 1)], start=(k == 0), stop=(k == DT - 1),
                   wait=[tok_hxf] if k == 0 else [])
            tmm_ = pr.tick("tensor")
            tok_hxf = pr.op("vector", lambda e, n=n, bank=bank: e.tensor_copy(
                hx[:, 1 + NBW * n:1 + NBW * (n + 1)], PS[bank][:, :]),
                wait=[tmm_], tick=True)
        if debug_taps and mc < DT:
            tok_hxf = pr.op("sync", lambda e, mc=mc: e.dma_start(
                out=dbg_f1hx[P * mc:P * (mc + 1), :], in_=hx[:, 1:1 + TLOC]),
                wait=[tok_hxf], dma=True)
        t1_ = pr.op("vector", lambda e: e.tensor_scalar(
            cbuf[:, :], hx[:, 0:TLOC], cwf[:, 0:1], cbf[:, 0:1],
            op0=OP.mult, op1=OP.add), wait=[tok_hxf, tcw, tcb, tok_hs], tick=True)
        t2_ = pr.op("vector", lambda e: e.scalar_tensor_tensor(
            cbuf[:, :], hx[:, 1:1 + TLOC], cwf[:, 1:2], cbuf[:, :],
            op0=OP.mult, op1=OP.add), tick=True)
        t3_ = pr.op("vector", lambda e: e.scalar_tensor_tensor(
            cbuf[:, :], hx[:, 2:2 + TLOC], cwf[:, 2:3], cbuf[:, :],
            op0=OP.mult, op1=OP.add), tick=True)
        tok_cvfree = t3_
        tok_hs = pr.op("scalar", lambda e: e.activation(
            hsb[:, :], cbuf[:, :], AF.Silu), wait=[t3_, tok_hab], tick=True)
        tab = pr.op("vector", lambda e: e.scalar_tensor_tensor(
            habs[:, :], hsb[:, :], -1.0, hsb[:, :], op0=OP.mult, op1=OP.max),
            wait=[tok_hs], tick=True)
        if mc == 0:
            tok_hab = pr.op("vector", lambda e: e.tensor_copy(
                hmacc[:, :], habs[:, :]), tick=True)
        else:
            tok_hab = pr.op("vector", lambda e: e.tensor_tensor(
                hmacc[:, :], hmacc[:, :], habs[:, :], op=OP.max), tick=True)
        tok_hs = pr.op("scalar", lambda e, mc=mc: e.dma_start(
            out=hs_sp[P * mc:P * (mc + 1), :], in_=hsb[:, :]), wait=[tok_hs], dma=True)
    end()
    s2row = sb("s2rowF1", [1, TLOC])
    is2row = sb("is2rowF1", [1, TLOC])
    tok_red = None
    for tb in range(TLOC // P):
        bk = tb % 2
        ttr_ = tr(PTB[bk][:, 0:P], hmacc[:, P * tb:P * (tb + 1)], eye_b[:, :],
                  wait=[tok_red], tick=True)
        tok_red = pr.op("vector", lambda e, tb=tb, bk=bk: e.tensor_reduce(
            s2cols[:, tb:tb + 1], PTB[bk][:, 0:P], axis=AX, op=OP.max),
            wait=[ttr_], tick=True)
    pr.op("vector", lambda e: e.tensor_scalar(
        s2cols[:, :], s2cols[:, :], 1e-5, None, op0=OP.max), wait=[tok_red], tick=True)
    tis2 = pr.op("vector", lambda e: e.tensor_scalar(
        is2cols[:, :], s2cols[:, :], 1.0 / 127.0, None, op0=OP.mult), tick=True)
    trs2 = pr.op("vector", lambda e: e.reciprocal(s2cols[:, :], s2cols[:, :]),
                 wait=[tis2], tick=True)
    tsc2 = pr.op("vector", lambda e: e.tensor_scalar(
        s2cols[:, :], s2cols[:, :], 127.0, None, op0=OP.mult), wait=[trs2], tick=True)
    tcp_prev = None
    tdma_prev = None
    for which, cols in ((s2row, s2cols), (is2row, is2cols)):
        ttr_ = tr(PS[2][0:32, 0:P], cols[:, 0:32], eye_f[:, :],
                  wait=[tsc2, tcp_prev], tick=True)
        tcp_ = pr.op("scalar", lambda e: e.copy(sr32[0:32, :], PS[2][0:32, 0:P]),
                     wait=[ttr_, tdma_prev], tick=True)
        tcp_prev = tcp_
        tdma_prev = pr.op("sync", lambda e, which=which: e.dma_start(
            out=which[0:1, :], in_=sr32[0:32, :]), wait=[tcp_], dma=True)
    pr.barrier()
    w2q = [sb("w2qF1_%d" % k, [P, D], BF16) for k in range(HID1 // P)]
    begin()
    wnat2 = sb("wnat2F1", [P, D])
    wi2 = sb("wi2F1", [P, D], I32)
    wacc2 = sb("wacc2F1", [P, 1])
    tqtmp2 = sb("tqtmp2F1", [P, 1])
    m11b = sb("m11b_F1", [1, 1])
    s1col2 = sb("s1col2F1", [P, 1])
    d1col2 = sb("d1col2F1", [P, 1])

    def ternary_quant2(dst_list, src_dram, nkc, ncols, nel):
        tred = None
        tldw = None
        for k in range(nkc):
            tldw = pr.op("sync", lambda e, k=k: e.dma_start(
                out=wnat2[:, 0:ncols], in_=src_dram[P * k:P * (k + 1), 0:ncols]),
                wait=[tred], dma=True)
            if k == 0:
                tred = pr.op("vector", lambda e: e.tensor_reduce(
                    wacc2[:, 0:1], wnat2[:, 0:ncols], axis=AX, op=OP.add,
                    apply_absolute_value=True), wait=[tldw], tick=True)
            else:
                trt_ = pr.op("vector", lambda e: e.tensor_reduce(
                    tqtmp2[:, 0:1], wnat2[:, 0:ncols], axis=AX, op=OP.add,
                    apply_absolute_value=True), wait=[tldw], tick=True)
                tred = pr.op("vector", lambda e: e.tensor_tensor(
                    wacc2[:, 0:1], wacc2[:, 0:1], tqtmp2[:, 0:1], op=OP.add),
                    wait=[trt_], tick=True)
        ttp = tr(PS[2][0:1, 0:P], wacc2[:, 0:1], eye_f[:, :],
                 wait=[tred, _bc_last.get(2)], tick=True)
        tcp = pr.op("scalar", lambda e: e.copy(rows_g["tmprow"][0:1, 0:P],
                                               PS[2][0:1, 0:P]),
                    wait=[ttp, rows_g.get("tmprow_free")], tick=True)
        _bc_last[2] = tcp
        rows_g["tmprow_free"] = pr.op("vector", lambda e: e.tensor_reduce(
            m11b[0:1, 0:1], rows_g["tmprow"][0:1, 0:P], axis=AX, op=OP.add),
            wait=[tcp], tick=True)
        pr.op("vector", lambda e, nel=nel: e.tensor_scalar(
            m11b[0:1, 0:1], m11b[0:1, 0:1], 1.0 / nel, 1e-5, op0=OP.mult, op1=OP.max),
            wait=[rows_g["tmprow_free"]], tick=True)
        tmn = pr.tick("vector")
        td1 = bcast_col(d1col2, m11b, wait=[tmn], bank=2)
        pr.op("vector", lambda e: e.reciprocal(m11b[0:1, 0:1], m11b[0:1, 0:1]),
              wait=[td1], tick=True)
        tiv = pr.tick("vector")
        ts1_ = bcast_col(s1col2, m11b, wait=[tiv], bank=2)
        tw_free = None
        for k in range(nkc):
            tldw = pr.op("sync", lambda e, k=k: e.dma_start(
                out=wnat2[:, 0:ncols], in_=src_dram[P * k:P * (k + 1), 0:ncols]),
                wait=[tw_free, ts1_ if k == 0 else None], dma=True)
            t1_ = pr.op("scalar", lambda e: e.activation(
                wi2[:, 0:ncols], wnat2[:, 0:ncols], AF.Copy, scale=s1col2[:, 0:1]),
                wait=[tldw, ts1_], tick=True)
            tw_free = t1_
            t2_ = pr.op("vector", lambda e: e.tensor_scalar(
                wi2[:, 0:ncols], wi2[:, 0:ncols], 1.0, -1.0, op0=OP.min, op1=OP.max),
                wait=[t1_], tick=True)
            pr.op("scalar", lambda e, k=k: e.activation(
                dst_list[k][:, 0:ncols], wi2[:, 0:ncols], AF.Copy, scale=d1col2[:, 0:1]),
                wait=[t2_], tick=True)
        return pr.tick("scalar")

    tq2w = ternary_quant2(w2q, w2T, HID1 // P, D, HID1 * D)
    end()
    hqk = [sb("hqkF1_%d" % k, [P, NBW], BF16) for k in range(HID1 // P)]
    hld = sb("hldF1", [P, NBW], BF16)
    hqi = sb("hqiF1", [P, NBW], I32)
    xs4 = [sb("xs4F1_%d" % m, [P, NBW]) for m in range(DT)]
    xr4 = [sb("xr4F1_%d" % m, [P, NBW]) for m in range(DT)]
    sqt4 = sb("sqt4F1", [P, NBW])
    rows4 = {"mrow": sb("mrow4", [1, NBW]), "m2row": sb("m2row4", [1, NBW]),
             "vrow": sb("vrow4", [1, NBW]), "rrow": sb("rrow4", [1, NBW]),
             "xo": sb("xo4", [P, NBW])}
    otm = [sb("otmF1_%d" % c4, [P, D]) for c4 in range(4)]
    pr.barrier()
    for n in range(NB):
        ts2b = mm(PS[4][:, :], ones_row[0:1, :], s2row[0:1, NBW * n:NBW * (n + 1)],
                  start=True, stop=True, tick=True)
        ti2b = mm(PS[5][:, :], ones_row[0:1, :], is2row[0:1, NBW * n:NBW * (n + 1)],
                  start=True, stop=True, tick=True)
        tq = [None] * (HID1 // P)
        tld_p = None
        thq_p = None
        for k in range(HID1 // P):
            tld_ = pr.op("sync", lambda e, k=k, n=n: e.dma_start(
                out=hld[:, :], in_=hs_sp[P * k:P * (k + 1), NBW * n:NBW * (n + 1)]),
                wait=[tld_p], dma=True)
            tld_p = pr.op("vector", lambda e: e.tensor_tensor(
                hqi[:, :], hld[:, :], PS[4][:, :], op=OP.mult),
                wait=[tld_, ts2b, thq_p], tick=True)
            thq_p = tq[k] = pr.op("vector", lambda e, k=k: e.tensor_tensor(
                hqk[k][:, :], hqi[:, :], PS[5][:, :], op=OP.mult),
                wait=[ti2b], tick=True)
        txs = [None] * DT
        for m_ in range(DT):
            txs[m_] = pr.op("scalar", lambda e, m_=m_, n=n: e.dma_start(
                out=xs4[m_][:, :], in_=xres[P * m_:P * (m_ + 1), NBW * n:NBW * (n + 1)]),
                dma=True)
        xr_tok = [None] * DT
        tok_psf = {}
        for m_ in range(DT):
            bank = m_ % 2
            for k in range(HID1 // P):
                mm(PS[bank][:, :], w2q[k][:, P * m_:P * (m_ + 1)], hqk[k][:, :],
                   start=(k == 0), stop=(k == HID1 // P - 1),
                   wait=([tq[k], tq2w] if m_ < 2 else []) +
                        ([tok_psf.get(bank)] if k == 0 else []))
            tdm = pr.tick("tensor")
            xr_tok[m_] = pr.op("vector", lambda e, m_=m_, bank=bank: e.tensor_tensor(
                xr4[m_][:, :], PS[bank][:, :], xs4[m_][:, :], op=OP.add),
                wait=[tdm, txs[m_]], tick=True)
            tok_psf[bank] = xr_tok[m_]
        rows4["xr_tok"] = xr_tok
        otm_bank = {0: None, 1: None}

        def out_fn4(m_, xo_t, tok, n=n, otm_bank=otm_bank):
            last = None
            for c4 in range(4):
                bk = c4 % 2
                ttr_ = tr(PS[bk][:, 0:P], xo_t[:, P * c4:P * (c4 + 1)], eye_f[:, :],
                          wait=[tok, otm_bank[bk]], tick=True)
                last = otm_bank[bk] = pr.op("scalar", lambda e, m_=m_, c4=c4, bk=bk: e.copy(
                    otm[c4][:, P * m_:P * (m_ + 1)], PS[bk][:, 0:P]),
                    wait=[ttr_], tick=True)
            return last

        feat_ln_block(n, xr4, sqt4, rows4, None, (2, 3), out_fn4)
        tfin = pr.tick("scalar")
        for c4 in range(4):
            pr.op("sync", lambda e, n=n, c4=c4: e.dma_start(
                out=out_d[NBW * n + P * c4:NBW * n + P * (c4 + 1), :],
                in_=otm[c4][:, :]), wait=[tfin], dma=True)
        pr.barrier()
    end()

    tap_outs = {}
    pr.barrier()
    for name in debug_taps:
        t = ds[name]
        tap_outs[name] = nc.dram_tensor("tap_" + name, list(t.shape), t.dtype,
                                        kind="ExternalOutput")
        sl = tuple(slice(None) for _ in t.shape)
        pr.op("sync", lambda e, name=name, sl=sl: e.dma_start(
            out=tap_outs[name][sl], in_=ds[name][sl]), dma=True)
    pr.op("sync", lambda e: e.nop(),
          wait=pr.dma_tokens("sync") + pr.dma_tokens("scalar") + pr.dma_tokens("gpsimd"))

    with nc.Block() as block:
        pr.finalize(block, sems)

    for k in reversed(list(sems)):
        sem_cms[k].__exit__(None, None, None)
    for cm in reversed(psum_cms):
        cm.__exit__(None, None, None)
    for cm in reversed(ctx_stack[0]):
        cm.__exit__(None, None, None)
    return nc


def _host_inputs(inputs):
    x = np.asarray(inputs["L0_attn_x"])
    base = {
        "w_qkv": np.asarray(inputs["L0_attn_W_qkv"]),
        "w_out": np.asarray(inputs["L0_attn_W_out"]),
        "b_out": np.asarray(inputs["L0_attn_b_out"]).reshape(1, D),
        "clip_x": np.asarray(inputs["L0_ffn_clip_x"]).reshape(1, 1),
        "clip_down": np.asarray(inputs["L0_ffn_clip_down"]).reshape(1, 1),
        "w_gate": np.asarray(inputs["L0_ffn_w_gate"]),
        "w_up": np.asarray(inputs["L0_ffn_w_up"]),
        "w_down": np.asarray(inputs["L0_ffn_w_down"]),
        "inp_wT": np.ascontiguousarray(np.asarray(inputs["L1_attn_in_proj_w"]).T),
        "conv_w": np.asarray(inputs["L1_attn_conv_w"]),
        "conv_b": np.asarray(inputs["L1_attn_conv_b"]).reshape(INNER, 1),
        "dt_wT": np.ascontiguousarray(np.asarray(inputs["L1_attn_dt_w"]).T),
        "dt_b": np.asarray(inputs["L1_attn_dt_b"]).reshape(INNER, 1),
        "a_log": np.asarray(inputs["L1_attn_A_log"]),
        "b_wT": np.ascontiguousarray(np.asarray(inputs["L1_attn_B_w"]).T),
        "c_wT": np.ascontiguousarray(np.asarray(inputs["L1_attn_C_w"]).T),
        "out_wT": np.ascontiguousarray(np.asarray(inputs["L1_attn_out_w"]).T),
        "w1T": np.ascontiguousarray(np.asarray(inputs["L1_ffn_w1"]).T),
        "cw1": np.asarray(inputs["L1_ffn_conv_w"]).reshape(HID1, 3),
        "cb1": np.asarray(inputs["L1_ffn_conv_b"]).reshape(HID1, 1),
        "w2T": np.ascontiguousarray(np.asarray(inputs["L1_ffn_w2"]).T),
        "c_eye": np.eye(P, dtype=np.float32),
        "c_ones_row": np.ones((1, P), np.float32),
        "c_ones_col": np.ones((P, 1), np.float32),
    }
    sel = np.zeros((16, 16 * P), np.float32)
    for s in range(16):
        sel[s, P * s:P * (s + 1)] = 1.0
    base["c_sel16"] = sel
    for f in range(DT):
        pt = np.zeros((16, P), np.float32)
        for h2 in range(2):
            pt[2 * f + h2, 64 * h2:64 * (h2 + 1)] = 1.0
        base["c_patt16_%d" % f] = pt
    in_maps = []
    for c in range(NCORE):
        b, h = c >> 1, c & 1
        m = dict(base)
        m["x_t"] = np.ascontiguousarray(x[b, h * TLOC:(h + 1) * TLOC, :].T)
        m["a_slice"] = np.ascontiguousarray(
            np.asarray(inputs["L1_attn_A_log"])[DSL * c:DSL * (c + 1), :])
        m["c_core"] = np.array([[h, float(h == 1), float(h == 0), c]], np.float32)
        in_maps.append(m)
    return in_maps


def kernel(**inputs):
    in_maps = _host_inputs(inputs)
    with allow_ici_all_to_all():
        nc = build()
        res = run_bass_kernel_spmd(nc, in_maps, core_ids=list(range(NCORE)))
    out = np.empty((4, 2 * TLOC, D), np.float32)
    for c in range(NCORE):
        b, h = c >> 1, c & 1
        out[b, h * TLOC:(h + 1) * TLOC, :] = res.results[c]["out"]
    return out


# version 05 (final)
# speedup vs baseline: 1.0225x; 1.0225x over previous
import numpy as np
import concourse.bass as bass
import concourse.mybir as mybir
from concourse.bass_utils import run_bass_kernel_spmd
from concourse.aot_env import allow_ici_all_to_all

F32 = mybir.dt.float32
BF16 = mybir.dt.bfloat16
I32 = mybir.dt.int32
AF = mybir.ActivationFunctionType
OP = mybir.AluOpType
AX = mybir.AxisListType.X

NCORE = 8
P = 128
TLOC = 4096
D = 1024
DT = D // P
NBW = 512
NB = TLOC // NBW
HEADS = 16
DH = 64
INNER = 2048
HID0 = 2816
HID1 = 4096
NSTATE = 16
DSL = INNER // NCORE
EPS = 1e-5

ENGS = ("tensor", "vector", "scalar", "gpsimd", "sync")


DMA_LANES = 4


class Prog:

    def __init__(self, nc):
        self.nc = nc
        self.items = {e: [] for e in ENGS}
        self.ticks = {e: 0 for e in ENGS}
        self.dmas = {q: 0 for q in ("sync", "scalar", "gpsimd")}
        self._forced = {e: [] for e in ENGS}

    def op(self, eng, fn, wait=(), tick=False, dma=False):
        waits = list(self._forced[eng]) + [w for w in wait if w is not None]
        self._forced[eng] = []
        incs = []
        token = None
        if dma:
            idx = self.dmas[eng]
            self.dmas[eng] += 1
            lane = "dma_%s%d" % (eng, idx % DMA_LANES)
            rounds = idx // DMA_LANES
            if rounds >= 1:
                waits.append((lane, 16 * rounds))
            incs.append((lane, 16))
            token = (lane, 16 * (rounds + 1))
        if tick:
            k = "tick_" + eng
            self.ticks[eng] += 1
            incs.append((k, 1))
            token = (k, self.ticks[eng])
        self.items[eng].append([fn, waits, incs])
        return token

    def tick(self, eng, wait=()):
        if wait or not self.items[eng] or any(
                sk.startswith("dma_") for sk, _ in self.items[eng][-1][2]):
            return self.op(eng, lambda e: e.nop(), wait=wait, tick=True)
        k = "tick_" + eng
        incs = self.items[eng][-1][2]
        for sk, amt in incs:
            if sk == k:
                return (k, self.ticks[eng])
        self.ticks[eng] += 1
        incs.append((k, 1))
        return (k, self.ticks[eng])

    def dma_tokens(self, q):
        toks = []
        for lane in range(DMA_LANES):
            n = (self.dmas[q] - lane + DMA_LANES - 1) // DMA_LANES
            if n:
                toks.append(("dma_%s%d" % (q, lane), 16 * n))
        return toks

    def barrier(self):
        toks = [self.tick(e) for e in ENGS]
        for q in self.dmas:
            toks += self.dma_tokens(q)
        for e in ENGS:
            self._forced[e].extend(toks)

    def finalize(self, block, sems):
        def body_for(ename):
            def body(eng):
                marks = {}
                for fn, waits, incs in self.items[ename]:
                    best = {}
                    for (sk, v) in waits:
                        if v > best.get(sk, 0):
                            best[sk] = v
                    for sk, v in best.items():
                        if v > marks.get(sk, 0):
                            eng.wait_ge(sems[sk], v)
                            marks[sk] = v
                    r = fn(eng)
                    for (sk, amt) in incs:
                        r.then_inc(sems[sk], amt)
            return body

        for ename in ENGS:
            getattr(block, ename)(body_for(ename))


def build(debug_taps=()):
    nc = bass.Bass("TRN2", target_bir_lowering=False)
    pr = Prog(nc)

    di = {}

    def inp(name, shape, dt=F32):
        di[name] = nc.dram_tensor(name, list(shape), dt, kind="ExternalInput")
        return di[name]

    x_in = inp("x_t", [D, TLOC])
    w_qkv = inp("w_qkv", [D, 3 * D])
    w_out = inp("w_out", [D, D])
    b_out = inp("b_out", [1, D])
    clip_x = inp("clip_x", [1, 1])
    clip_down = inp("clip_down", [1, 1])
    w_gate = inp("w_gate", [HID0, D])
    w_up = inp("w_up", [HID0, D])
    w_down = inp("w_down", [D, HID0])
    inp_wT = inp("inp_wT", [D, 2 * INNER])
    conv_w = inp("conv_w", [INNER, 4])
    conv_b = inp("conv_b", [INNER, 1])
    dt_wT = inp("dt_wT", [INNER, INNER])
    dt_b = inp("dt_b", [INNER, 1])
    a_log = inp("a_log", [INNER, NSTATE])
    b_wT = inp("b_wT", [INNER, NSTATE])
    c_wT = inp("c_wT", [INNER, NSTATE])
    out_wT = inp("out_wT", [INNER, D])
    w1T = inp("w1T", [D, HID1])
    cw1 = inp("cw1", [HID1, 3])
    cb1 = inp("cb1", [HID1, 1])
    w2T = inp("w2T", [HID1, D])
    c_eye = inp("c_eye", [P, P])
    c_ones_row = inp("c_ones_row", [1, P])
    c_ones_col = inp("c_ones_col", [P, 1])
    c_patt16 = [inp("c_patt16_%d" % f, [16, P]) for f in range(DT)]
    c_core = inp("c_core", [1, 4])
    a_slice = inp("a_slice", [DSL, NSTATE])
    c_sel16 = inp("c_sel16", [16, 16 * P])

    out_d = nc.dram_tensor("out", [TLOC, D], F32, kind="ExternalOutput")

    ds = {}

    def scr(name, shape, dt=F32):
        ds[name] = nc.dram_tensor(name, list(shape), dt)
        return ds[name]

    xres = scr("xres", [D, TLOC])
    qb_sp = scr("qb_sp", [D, TLOC], BF16)
    dbg_att = scr("dbg_att", [D, TLOC], BF16)
    dbg_x2 = scr("dbg_x2", [D, TLOC])
    dbg_gu = scr("dbg_gu", [P, 2 * NBW])
    dbg_xq = scr("dbg_xq", [D, TLOC], BF16)
    dbg_wg = scr("dbg_wg", [P, D], BF16)
    dbg_den = scr("dbg_den", [16, TLOC])
    dbg_sc = [scr("dbg_sc%d" % i, [P, 1]) for i in range(4)]
    dbg_tqw = scr("dbg_tqw", [P, 1])
    dbg_tqr = scr("dbg_tqr", [1, P])
    dbg_tqm = scr("dbg_tqm", [1, 1])
    dbg_tqd = scr("dbg_tqd", [P, 1])
    dbg_tqs = scr("dbg_tqs", [P, 1])
    dbg_x1 = scr("dbg_x1", [D, TLOC])
    dbg_f1xq = scr("dbg_f1xq", [D, TLOC], BF16)
    dbg_f1w1 = scr("dbg_f1w1", [D, HID1], BF16)
    dbg_f1hx = scr("dbg_f1hx", [D, TLOC])
    cc_kv_i = scr("cc_kv_i", [DH, HEADS * 65])
    cc_kv_o = scr("cc_kv_o", [DH, HEADS * 65])
    cc_mx_i = scr("cc_mx_i", [1, 8])
    cc_mx_o = scr("cc_mx_o", [1, 8])
    h_sp = scr("h_sp", [HID0, TLOC], BF16)
    hs_sp = scr("hs_sp", [HID1, TLOC], BF16)
    wd_sp = scr("wd_sp", [D, HID0], BF16)
    a2a_dt_i = scr("a2a_dt_i", [NCORE, DSL, TLOC], BF16)
    a2a_dt_o = scr("a2a_dt_o", [NCORE, DSL, TLOC], BF16)
    a2a_dx_i = scr("a2a_dx_i", [NCORE, DSL, TLOC], BF16)
    a2a_dx_o = scr("a2a_dx_o", [NCORE, DSL, TLOC], BF16)
    a2a_g_i = scr("a2a_g_i", [NCORE, DSL, TLOC], BF16)
    a2a_g_o = scr("a2a_g_o", [NCORE, DSL, TLOC], BF16)
    a2a_y_i = scr("a2a_y_i", [NCORE, DSL, TLOC], BF16)
    a2a_y_o = scr("a2a_y_o", [NCORE, DSL, TLOC], BF16)
    bc_ag_i = scr("bc_ag_i", [32, TLOC])
    bc_ag_o = scr("bc_ag_o", [NCORE * 32, TLOC])
    halo3_i = scr("halo3_i", [1, 3 * D])
    halo3_o = scr("halo3_o", [2, 3 * D])
    halo1_i = scr("halo1_i", [1, 16 * D])
    halo1_o = scr("halo1_o", [2, 16 * D])

    ctx_stack = [[]]

    def sb(name, shape, dt=F32):
        cm = nc.sbuf_tensor(name, list(shape), dt)
        h = cm.__enter__()
        ctx_stack[-1].append(cm)
        return h

    def begin():
        ctx_stack.append([])

    def end():
        pr.barrier()
        for cm in reversed(ctx_stack.pop()):
            cm.__exit__(None, None, None)

    psum_cms = [nc.psum_tensor("ps%d" % i, [P, NBW], F32) for i in range(6)]
    PS = [cm.__enter__() for cm in psum_cms]
    pt_cms = [nc.psum_tensor("pt_b%d" % i, [P, 2 * NBW], BF16) for i in range(2)]
    PTB = [cm.__enter__() for cm in pt_cms]
    psum_cms += pt_cms

    sem_keys = ["tick_" + e for e in ENGS] + [
        "dma_%s%d" % (q, l) for q in ("sync", "scalar", "gpsimd") for l in range(4)]
    sem_cms = {k: nc.semaphore(k) for k in sem_keys}
    sems = {k: cm.__enter__() for k, cm in sem_cms.items()}

    eye_f = sb("eye_f", [P, P])
    eye_b = sb("eye_b", [P, P], BF16)
    ones_row = sb("ones_row", [1, P])
    ones_col = sb("ones_col", [P, 1])
    ones512_b = sb("ones512_b", [1, NBW], BF16)
    core_c = sb("core_c", [1, 4])
    aneg = [sb("aneg%d" % i, [P, NSTATE]) for i in range(2)]
    hcol = sb("hcol", [P, 1])
    rcol = sb("rcol", [P, 1])
    sig_cx = sb("sig_cx", [1, 1])
    sig_cd = sb("sig_cd", [1, 1])
    sc_a = sb("sc_a", [1, 1])
    mxrow = sb("mxrow", [1, 8])
    sc_b = sb("sc_b", [1, 1])
    scol_a = sb("scol_a", [P, 1])
    scol_b = sb("scol_b", [P, 1])

    t0 = pr.op("sync", lambda e: e.dma_start(out=eye_f[:, :], in_=c_eye[:, :]), dma=True)
    pr.op("sync", lambda e: e.dma_start(out=ones_row[:1, :], in_=c_ones_row[:1, :]), dma=True)
    pr.op("sync", lambda e: e.dma_start(out=ones_col[:, :], in_=c_ones_col[:, :]), dma=True)
    pr.op("sync", lambda e: e.dma_start(out=core_c[:1, :], in_=c_core[:1, :]), dma=True)
    pr.op("vector", lambda e: e.tensor_copy(eye_b[:, :], eye_f[:, :]), wait=[t0], tick=True)
    pr.op("gpsimd", lambda e: e.memset(ones512_b[:1, :], 1.0), tick=True)
    pr.op("vector", lambda e: e.memset(mxrow[:1, :], 0.0), tick=True)
    tasl = pr.op("sync", lambda e: e.dma_start(out=aneg[0][:, :], in_=a_slice[0:P, :]), dma=True)
    tasl2 = pr.op("sync", lambda e: e.dma_start(out=aneg[1][:, :], in_=a_slice[P:2 * P, :]), dma=True)
    tcx = pr.op("sync", lambda e: e.dma_start(out=sig_cx[:1, :1], in_=clip_x[:1, :1]), dma=True)
    tcd = pr.op("sync", lambda e: e.dma_start(out=sig_cd[:1, :1], in_=clip_down[:1, :1]), dma=True)
    pr.op("scalar", lambda e: e.activation(sig_cx[:1, :1], sig_cx[:1, :1], AF.Sigmoid),
          wait=[tcx], tick=True)
    pr.op("scalar", lambda e: e.activation(sig_cd[:1, :1], sig_cd[:1, :1], AF.Sigmoid),
          wait=[tcd], tick=True)
    for i in range(2):
        te_ = pr.op("scalar", lambda e, i=i: e.activation(
            aneg[i][:, :], aneg[i][:, :], AF.Exp), wait=[tasl, tasl2], tick=True)
        pr.op("vector", lambda e, i=i: e.tensor_scalar(
            aneg[i][:, :], aneg[i][:, :], -1.0, None, op0=OP.mult),
            wait=[te_], tick=True)
    tcc0 = pr.tick("sync")
    mmh = pr.op("tensor", lambda e: e.matmul(PS[0][:, 0:1], ones_row[0:1, :],
                                             core_c[0:1, 0:1], start=True, stop=True),
                wait=pr.dma_tokens("sync"), tick=True)
    pr.op("scalar", lambda e: e.copy(hcol[:, :], PS[0][:, 0:1]), wait=[mmh], tick=True)
    ths = pr.tick("scalar")
    mmr = pr.op("tensor", lambda e: e.matmul(PS[0][:, 0:1], ones_row[0:1, :],
                                             core_c[0:1, 2:3], start=True, stop=True),
                wait=[ths], tick=True)
    pr.op("scalar", lambda e: e.copy(rcol[:, :], PS[0][:, 0:1]), wait=[mmr], tick=True)
    pr.barrier()

    _bc_last = {}

    def bcast_col(dst_col, src11, wait=(), bank=0):
        t1 = pr.op("tensor", lambda e: e.matmul(
            PS[bank][:, 0:1], ones_row[0:1, :], src11[0:1, 0:1], start=True, stop=True),
            wait=list(wait) + [_bc_last.get(bank)], tick=True)
        tok = pr.op("scalar", lambda e: e.copy(dst_col[:, 0:1], PS[bank][:, 0:1]),
                    wait=[t1], tick=True)
        _bc_last[bank] = tok
        return tok

    def part_max(dst11, src_col, bank, wait=()):
        tt_ = tr(PS[bank][0:1, 0:P], src_col[:, 0:1], eye_f[:, :],
                 wait=list(wait) + [_bc_last.get(bank)], tick=True)
        tc_ = pr.op("scalar", lambda e: e.copy(rows_g["tmprow"][0:1, 0:P], PS[bank][0:1, 0:P]),
                    wait=[tt_, rows_g.get("tmprow_free")], tick=True)
        _bc_last[bank] = tc_
        tok = pr.op("vector", lambda e: e.tensor_reduce(
            dst11[0:1, 0:1], rows_g["tmprow"][0:1, 0:P], axis=AX, op=OP.max),
            wait=[tc_], tick=True)
        rows_g["tmprow_free"] = tok
        return tok

    rows_g = {"tmprow": sb("tmprow_g", [1, P]), "tmprow_free": None}

    def mm(ps_ap, lhsT_ap, rhs_ap, start, stop, wait=(), tick=False):
        return pr.op("tensor",
                     lambda e: e.matmul(ps_ap, lhsT_ap, rhs_ap, start=start, stop=stop),
                     wait=wait, tick=tick)

    def tr(ps_ap, in_ap, ident, wait=(), tick=False):
        return pr.op("tensor", lambda e: e.transpose(ps_ap, in_ap, ident),
                     wait=wait, tick=tick)

    def feat_ln_block(n, xr, sqt, rows, bcast_banks, stat_banks, out_fn, extra_stat_wait=()):
        bx, bq = stat_banks
        tprev = None
        for m in range(DT):
            tsq = pr.op("scalar", lambda e, m=m: e.activation(
                sqt[:, :], xr[m][:, :], AF.Square),
                wait=[rows["xr_tok"][m], tprev], tick=True)
            mm(PS[bx][0:1, :], ones_col[:, :], xr[m][:, :], start=(m == 0), stop=(m == DT - 1),
               wait=[rows["xr_tok"][m]] + (list(extra_stat_wait) if m == 0 else []))
            mm(PS[bq][0:1, :], ones_col[:, :], sqt[:, :], start=(m == 0), stop=(m == DT - 1),
               wait=[tsq])
            tprev = pr.tick("tensor")
        tst = pr.tick("tensor")
        pr.op("scalar", lambda e: e.activation(rows["mrow"][:1, :], PS[bx][0:1, :],
                                               AF.Copy, scale=1.0 / D), wait=[tst])
        tmr = pr.tick("scalar")
        pr.op("scalar", lambda e: e.activation(rows["m2row"][:1, :], rows["mrow"][:1, :],
                                               AF.Square))
        tm2 = pr.tick("scalar")
        pr.op("vector", lambda e: e.scalar_tensor_tensor(
            rows["vrow"][:1, :], PS[bq][0:1, :], 1.0 / D, rows["m2row"][:1, :],
            op0=OP.mult, op1=OP.subtract), wait=[tm2, tst])
        tve = pr.op("vector", lambda e: e.tensor_scalar(
            rows["vrow"][:1, :], rows["vrow"][:1, :], EPS, None, op0=OP.add), tick=True)
        pr.op("vector", lambda e: e.reciprocal(rows["vrow"][:1, :], rows["vrow"][:1, :]),
              wait=[tve], tick=True)
        tvr = pr.tick("vector")
        pr.op("scalar", lambda e: e.activation(rows["rrow"][:1, :], rows["vrow"][:1, :],
                                               AF.Sqrt), wait=[tvr])
        tsr = pr.tick("scalar")
        mm(PS[bx][:, :], ones_row[0:1, :], rows["mrow"][0:1, :], start=True, stop=True,
           wait=[tmr, tvr])
        mm(PS[bq][:, :], ones_row[0:1, :], rows["rrow"][0:1, :], start=True, stop=True,
           wait=[tsr])
        tbc = pr.tick("tensor")
        last = None
        for m in range(DT):
            pr.op("vector", lambda e, m=m: e.tensor_tensor(
                rows["xo"][:, :], xr[m][:, :], PS[bx][:, :], op=OP.subtract),
                wait=[tbc] + ([last] if last else []))
            tmul = pr.op("vector", lambda e: e.tensor_tensor(
                rows["xo"][:, :], rows["xo"][:, :], PS[bq][:, :], op=OP.mult), tick=True)
            last = out_fn(m, rows["xo"], tmul) or tmul
        return last

    begin()
    patt = [sb("patt_%d" % f, [16, P]) for f in range(DT)]
    for f in range(DT):
        pr.op("sync", lambda e, f=f: e.dma_start(out=patt[f][:, :], in_=c_patt16[f][:, :]),
              dma=True)
    wq = [sb("wq%d" % k, [P, 3 * D], BF16) for k in range(DT)]
    wo = [sb("wo%d" % k, [P, D], BF16) for k in range(DT)]
    bo_b = sb("bo_b", [1, D], BF16)
    xb = [sb("xbA%d" % k, [P, NBW], BF16) for k in range(DT)]
    kb = [sb("kbA%d" % k, [P, NBW], BF16) for k in range(DT)]
    vb = [sb("vbA%d" % k, [P, NBW], BF16) for k in range(DT)]
    qtmp = sb("qtmpA", [P, NBW])
    qexp = sb("qexpA", [P, NBW])
    qout = sb("qoutA", [P, NBW], BF16)
    ktm = sb("ktmA", [P, HEADS * DH], BF16)
    vtm = sb("vtmA", [P, HEADS * 65], BF16)
    kv_sb = sb("kv_sbA", [DH, HEADS * 65])
    kvb = sb("kvbA", [P, HEADS * 65], BF16)
    zt = [sb("ztA%d" % f, [P, 16], BF16) for f in range(DT)]
    qfb = [sb("qfbA%d" % k, [P, NBW], BF16) for k in range(DT)]
    d16 = sb("d16A", [16, NBW])
    r16 = sb("r16A", [16, NBW])
    af32 = sb("af32A", [P, NBW])
    atb = [sb("atbA%d" % k, [P, NBW], BF16) for k in range(DT)]
    xs = [sb("xsA%d" % m, [P, NBW]) for m in range(DT)]
    xr = [sb("xrA%d" % m, [P, NBW]) for m in range(DT)]
    sqt = sb("sqtA", [P, NBW])
    rows = {"mrow": sb("mrowA", [1, NBW]), "m2row": sb("m2rowA", [1, NBW]),
            "vrow": sb("vrowA", [1, NBW]), "rrow": sb("rrowA", [1, NBW]),
            "xo": sb("xoA", [P, NBW])}

    for k in range(DT):
        pr.op("gpsimd", lambda e, k=k: e.dma_start(
            out=wq[k][:, :], in_=w_qkv[P * k:P * (k + 1), :]), dma=True)
    for k in range(DT):
        pr.op("gpsimd", lambda e, k=k: e.dma_start(
            out=wo[k][:, :], in_=w_out[P * k:P * (k + 1), :]), dma=True)
    pr.op("gpsimd", lambda e: e.dma_start(out=bo_b[:1, :], in_=b_out[:1, :]), dma=True)
    for h in range(HEADS):
        pr.op("vector", lambda e, h=h: e.memset(vtm[:, 65 * h + 64:65 * h + 65], 1.0),
              tick=True)
    for f in range(DT):
        pr.op("vector", lambda e, f=f: e.memset(zt[f][:, :], 0.0), tick=True)
    pr.op("vector", lambda e: e.memset(kv_sb[:, :], 0.0), tick=True)
    pr.barrier()

    for n in range(NB):
        for k in range(DT):
            pr.op("gpsimd", lambda e, k=k, n=n: e.dma_start(
                out=xb[k][:, :], in_=x_in[P * k:P * (k + 1), NBW * n:NBW * (n + 1)]),
                dma=True)
        t_xb_l = pr.dma_tokens("gpsimd")
        tok_ps_free = {}
        tok_evac = {}
        tok_qs = None
        tok_qout = None
        for m in range(24):
            bank = m % 3
            for k in range(DT):
                mm(PS[bank][:, :], wq[k][:, P * m:P * (m + 1)], xb[k][:, :],
                   start=(k == 0), stop=(k == DT - 1),
                   wait=(t_xb_l if m == 0 and k == 0 else []) +
                        ([tok_ps_free.get(bank)] if k == 0 else []))
            tmm = pr.tick("tensor")
            if m < 16:
                dst = qout if m < 8 else kb[m - 8]
                pr.op("vector", lambda e, bank=bank: e.tensor_scalar(
                    qtmp[:, :], PS[bank][:, :], 0.0, None, op0=OP.min),
                    wait=[tmm] + ([tok_qs] if tok_qs else []))
                tq1 = pr.tick("vector")
                tq2 = pr.op("scalar", lambda e: e.activation(
                    qexp[:, :], qtmp[:, :], AF.Exp), wait=[tq1], tick=True)
                tev = pr.op("vector", lambda e, dst=dst, bank=bank: e.scalar_tensor_tensor(
                    dst[:, :], PS[bank][:, :], 0.0, qexp[:, :], op0=OP.max, op1=OP.add),
                    wait=[tq2] + ([tok_qout] if (m < 8 and tok_qout) else []), tick=True)
                tok_qs = tev
                if m < 8:
                    tok_qout = pr.op("sync", lambda e, m=m, n=n: e.dma_start(
                        out=qb_sp[P * m:P * (m + 1), NBW * n:NBW * (n + 1)], in_=qout[:, :]),
                        wait=[tev], dma=True)
                tok_evac[m] = tev
            else:
                tok_evac[m] = pr.op("scalar", lambda e, m=m, bank=bank: e.copy(
                    vb[m - 16][:, :], PS[bank][:, :]), wait=[tmm], tick=True)
            tok_ps_free[bank] = tok_evac[m]
        tok_tr_evac = {0: None, 1: None}
        tok_kvadd = {}
        for c in range(4):
            g = 4 * n + c
            for f in range(DT):
                tb = f % 2
                PT = PTB[tb]
                ttr = tr(PT[:, 0:P], kb[f][:, P * c:P * (c + 1)], eye_b[:, :],
                         wait=[tok_evac[8 + f], tok_tr_evac[tb]], tick=True)
                tke = pr.op("scalar", lambda e, f=f, PT=PT: e.copy(
                    ktm[:, P * f:P * (f + 1)], PT[:, 0:P]), wait=[ttr], tick=True)
                ttr = tr(PT[:, 0:P], vb[f][:, P * c:P * (c + 1)], eye_b[:, :],
                         wait=[tok_evac[16 + f], tke], tick=True)
                pr.op("scalar", lambda e, f=f, PT=PT: e.copy(
                    vtm[:, 65 * 2 * f:65 * 2 * f + 64], PT[:, 0:64]),
                    wait=[ttr], tick=True)
                tok_tr_evac[tb] = pr.op("scalar", lambda e, f=f, PT=PT: e.copy(
                    vtm[:, 65 * (2 * f + 1):65 * (2 * f + 1) + 64], PT[:, 64:P]),
                    tick=True)
            tse = pr.tick("scalar")
            for h in range(HEADS):
                kvbank = 3 + h // 7
                off = 65 * (h % 7) if h < 14 else 65 * (h - 14)
                first = h % 7 == 0 or h == 14
                last = h % 7 == 6 or h == 13 or h == 15
                mm(PS[kvbank][0:DH, off:off + 65],
                   ktm[:, DH * h:DH * (h + 1)], vtm[:, 65 * h:65 * h + 65],
                   start=first, stop=last,
                   wait=([tse] if h == 0 else []) +
                        ([tok_kvadd.get(kvbank)] if first else []))
            tkvm = pr.tick("tensor")
            for bk, lo, width in ((3, 0, 455), (4, 455, 455), (5, 910, 130)):
                tok_kvadd[bk] = pr.op("vector", lambda e, bk=bk, lo=lo, width=width: e.tensor_tensor(
                    kv_sb[:, lo:lo + width], kv_sb[:, lo:lo + width],
                    PS[bk][0:DH, 0:width], op=OP.add), wait=[tkvm], tick=True)
        pr.barrier()

    tkv = pr.tick("vector")
    tw = pr.op("sync", lambda e: e.dma_start(out=cc_kv_i[:, :], in_=kv_sb[:, :]),
               wait=[tkv], dma=True)
    tcc = pr.op("gpsimd", lambda e: e.collective_compute(
        "AllReduce", OP.add, replica_groups=[[0, 1], [2, 3], [4, 5], [6, 7]],
        ins=[cc_kv_i[:, :]], outs=[cc_kv_o[:, :]]), wait=[tw], tick=True)
    trkv = pr.op("sync", lambda e: e.dma_start(out=kv_sb[:, :], in_=cc_kv_o[:, :]),
                 wait=[tcc], dma=True)
    pr.op("vector", lambda e: e.tensor_copy(kvb[0:DH, :], kv_sb[:, :]), wait=[trkv],
          tick=True)
    pr.op("vector", lambda e: e.tensor_copy(kvb[DH:P, :], kv_sb[:, :]), tick=True)
    for f in range(DT):
        pr.op("scalar", lambda e, f=f: e.copy(
            zt[f][0:DH, 2 * f:2 * f + 1], kv_sb[:, 65 * 2 * f + 64:65 * 2 * f + 65]),
            wait=[trkv], tick=True)
        pr.op("scalar", lambda e, f=f: e.copy(
            zt[f][DH:P, 2 * f + 1:2 * f + 2],
            kv_sb[:, 65 * (2 * f + 1) + 64:65 * (2 * f + 1) + 65]), tick=True)
    pr.barrier()

    for n in range(NB):
        tq = [None] * DT
        txs = [None] * DT
        for k in range(DT):
            tq[k] = pr.op("sync", lambda e, k=k, n=n: e.dma_start(
                out=qfb[k][:, :], in_=qb_sp[P * k:P * (k + 1), NBW * n:NBW * (n + 1)]),
                dma=True)
            txs[k] = pr.op("scalar", lambda e, k=k, n=n: e.dma_start(
                out=xs[k][:, :], in_=x_in[P * k:P * (k + 1), NBW * n:NBW * (n + 1)]),
                dma=True)
        for k in range(DT):
            mm(PS[5][0:16, :], zt[k][:, :], qfb[k][:, :], start=(k == 0), stop=(k == DT - 1),
               wait=[tq[k]])
        tden = pr.tick("tensor")
        td6 = pr.op("vector", lambda e: e.tensor_scalar(
            d16[:, :], PS[5][0:16, :], 1e-6, None, op0=OP.add), wait=[tden], tick=True)
        trc = pr.op("vector", lambda e: e.reciprocal(r16[:, :], d16[:, :]),
                    wait=[td6], tick=True)
        pr._forced["vector"].append(trc)
        if debug_taps:
            pr.op("sync", lambda e, n=n: e.dma_start(
                out=dbg_den[:, NBW * n:NBW * (n + 1)], in_=d16[:, :]),
                wait=[trc], dma=True)
        tok_ps_free = {}
        tok_af = None
        tok_atb = [None] * DT
        for f in range(DT):
            nb_ = f % 2
            mm(PS[nb_][0:DH, :], kvb[0:DH, 65 * 2 * f:65 * 2 * f + DH],
               qfb[f][0:DH, :], start=True, stop=True,
               wait=[tok_ps_free.get(nb_), tq[f]])
            mm(PS[nb_][DH:P, :], kvb[DH:P, 65 * (2 * f + 1):65 * (2 * f + 1) + DH],
               qfb[f][DH:P, :], start=True, stop=True)
            tnum = pr.tick("tensor")
            taf = pr.op("scalar", lambda e, nb_=nb_: e.copy(af32[:, :], PS[nb_][:, :]),
                        wait=[tnum] + ([tok_af] if tok_af else []), tick=True)
            tok_ps_free[nb_] = taf
            mm(PS[2][:, :], patt[f][:, :], r16[:, :], start=True, stop=True,
               wait=[trc, tok_ps_free.get(2)])
            trb = pr.tick("tensor")
            tok_atb[f] = pr.op("vector", lambda e, f=f: e.tensor_tensor(
                atb[f][:, :], af32[:, :], PS[2][:, :], op=OP.mult), wait=[trb, taf],
                tick=True)
            tok_af = tok_atb[f]
            tok_ps_free[2] = tok_atb[f]
            if debug_taps:
                pr.op("sync", lambda e, f=f, n=n: e.dma_start(
                    out=dbg_att[P * f:P * (f + 1), NBW * n:NBW * (n + 1)],
                    in_=atb[f][:, :]), wait=[tok_atb[f]], dma=True)
        xr_tok = [None] * DT
        for m in range(DT):
            bank = 3 + (m % 2)
            for k in range(DT):
                mm(PS[bank][:, :], wo[k][:, P * m:P * (m + 1)], atb[k][:, :],
                   start=(k == 0), stop=False,
                   wait=([tok_atb[k]] if m < 2 else []) +
                        ([tok_ps_free.get(bank)] if k == 0 else []))
            mm(PS[bank][:, :], bo_b[0:1, P * m:P * (m + 1)], ones512_b[0:1, :],
               start=False, stop=True)
            tmo = pr.tick("tensor")
            trs = pr.op("vector", lambda e, m=m, bank=bank: e.tensor_tensor(
                xr[m][:, :], PS[bank][:, :], xs[m][:, :], op=OP.add),
                wait=[tmo, txs[m]], tick=True)
            tok_ps_free[bank] = trs
            xr_tok[m] = trs
        rows["xr_tok"] = xr_tok

        def out_fn(m, xo_t, tok, n=n):
            return pr.op("scalar", lambda e, m=m, n=n: e.dma_start(
                out=xres[P * m:P * (m + 1), NBW * n:NBW * (n + 1)], in_=xo_t[:, :]),
                wait=[tok], dma=True)

        feat_ln_block(n, xr, sqt, rows, None, (5, 2), out_fn)
        pr.barrier()
    end()


    begin()
    if debug_taps:
        for m_ in range(DT):
            pr.op("sync", lambda e, m_=m_: e.dma_start(
                out=dbg_x1[P * m_:P * (m_ + 1), :], in_=xres[P * m_:P * (m_ + 1), :]),
                dma=True)
    HTL = TLOC // 2
    xt0 = sb("xtF0", [P, HTL])
    gmx = sb("gmxF0", [P, 1])
    tmpc0 = sb("tmpc0F0", [P, 1])
    mx11 = sb("mx11F0", [1, 1])
    wint = sb("wintF0", [P, HID0], I32)
    xi0 = wint
    xqb = [sb("xqbF%d" % k, [P, TLOC], BF16) for k in range(DT)]
    tld = None
    tgm = None
    trdd = None
    for mh in range(2 * DT):
        m, hh = mh // 2, mh % 2
        tld = pr.op("sync", lambda e, m=m, hh=hh: e.dma_start(
            out=xt0[:, :], in_=xres[P * m:P * (m + 1), HTL * hh:HTL * (hh + 1)]),
            wait=[tgm, trdd], dma=True)
        if mh == 0:
            tgm = pr.op("vector", lambda e: e.tensor_reduce(
                gmx[:, 0:1], xt0[:, :], axis=AX, op=OP.max,
                apply_absolute_value=True), wait=[tld], tick=True)
        else:
            trdd = pr.op("vector", lambda e: e.tensor_reduce(
                tmpc0[:, 0:1], xt0[:, :], axis=AX, op=OP.max,
                apply_absolute_value=True), wait=[tld], tick=True)
            tgm = pr.op("vector", lambda e: e.tensor_tensor(
                gmx[:, 0:1], gmx[:, 0:1], tmpc0[:, 0:1], op=OP.max),
                wait=[trdd], tick=True)
    tpm = part_max(mx11, gmx, 2, wait=[tgm])
    tcpm = pr.op("vector", lambda e: e.tensor_copy(mxrow[0:1, 0:1], mx11[0:1, 0:1]),
                 wait=[tpm], tick=True)
    tw_ = pr.op("sync", lambda e: e.dma_start(out=cc_mx_i[0:1, :], in_=mxrow[0:1, :]),
                wait=[tcpm], dma=True)
    tcc = pr.op("gpsimd", lambda e: e.collective_compute(
        "AllReduce", OP.max, replica_groups=[list(range(NCORE))],
        ins=[cc_mx_i[:, :]], outs=[cc_mx_o[:, :]]), wait=[tw_], tick=True)
    trd = pr.op("sync", lambda e: e.dma_start(out=mx11[0:1, 0:1], in_=cc_mx_o[0:1, 0:1]),
                wait=[tcc], dma=True)
    t_a = pr.op("vector", lambda e: e.tensor_tensor(
        sc_a[0:1, 0:1], mx11[0:1, 0:1], sig_cx[0:1, 0:1], op=OP.mult), wait=[trd], tick=True)
    t_b = pr.op("vector", lambda e: e.tensor_scalar(
        sc_b[0:1, 0:1], sc_a[0:1, 0:1], 1.0 / 127.0, None, op0=OP.mult),
        wait=[t_a], tick=True)
    t_r = pr.op("vector", lambda e: e.reciprocal(sc_a[0:1, 0:1], sc_a[0:1, 0:1]),
                wait=[t_b], tick=True)
    tqi = pr.op("vector", lambda e: e.tensor_scalar(
        sc_a[0:1, 0:1], sc_a[0:1, 0:1], 127.0, None, op0=OP.mult), wait=[t_r], tick=True)
    tbc1 = bcast_col(scol_a, sc_a, wait=[tqi], bank=2)
    tbc2 = bcast_col(scol_b, sc_b, wait=[tqi], bank=2)
    if debug_taps:
        pr.op("sync", lambda e: e.dma_start(
            out=dbg_sc[0][:, 0:1], in_=scol_a[:, 0:1]), wait=[tbc1], dma=True)
        pr.op("sync", lambda e: e.dma_start(
            out=dbg_sc[1][:, 0:1], in_=scol_b[:, 0:1]), wait=[tbc2], dma=True)
    tqdone = []
    t1p = None
    for mh in range(2 * DT):
        m, hh = mh // 2, mh % 2
        tld = pr.op("sync", lambda e, m=m, hh=hh: e.dma_start(
            out=xt0[:, :], in_=xres[P * m:P * (m + 1), HTL * hh:HTL * (hh + 1)]),
            wait=[tbc2, t1p], dma=True)
        t1p = pr.op("scalar", lambda e: e.activation(
            xi0[:, 0:HTL], xt0[:, :], AF.Copy, scale=scol_a[:, 0:1]),
            wait=[tld, tbc2], tick=True)
        t2_ = pr.op("vector", lambda e: e.tensor_scalar(
            xi0[:, 0:HTL], xi0[:, 0:HTL], 127.0, -128.0, op0=OP.min, op1=OP.max),
            wait=[t1p], tick=True)
        t3_ = pr.op("scalar", lambda e, m=m, hh=hh: e.activation(
            xqb[m][:, HTL * hh:HTL * (hh + 1)], xi0[:, 0:HTL], AF.Copy,
            scale=scol_b[:, 0:1]), wait=[t2_], tick=True)
        tqdone.append(t3_)
        if debug_taps:
            pr.op("sync", lambda e, m=m, hh=hh: e.dma_start(
                out=dbg_xq[P * m:P * (m + 1), HTL * hh:HTL * (hh + 1)],
                in_=xqb[m][:, HTL * hh:HTL * (hh + 1)]), wait=[t3_], dma=True)
    pr.barrier()
    wnat = sb("wnatF0", [P, HID0])
    wrow = sb("wrowF0", [P, 1])
    qcol = sb("qcolF0", [P, 1])
    wqn = sb("wqnF0", [P, HID0], BF16)
    wgT = sb("wgTF0", [P, D], BF16)
    wuT = sb("wuTF0", [P, D], BF16)
    sgt = sb("sgtF0", [P, NBW])
    hmaxc = sb("hmaxcF0", [P, 22])
    hbt = sb("hbtF0", [P, NBW], BF16)
    NC0 = HID0 // P
    qst = {"ev": {0: None, 1: None}, "wq_free": None, "nat_free": None}

    def quant_rows_to(dstT, src_dram, r0, ncols):
        tl = pr.op("sync", lambda e: e.dma_start(
            out=wnat[:, 0:ncols], in_=src_dram[r0:r0 + P, 0:ncols]),
            wait=[qst["nat_free"]], dma=True)
        trw = pr.op("vector", lambda e: e.tensor_reduce(
            wrow[:, 0:1], wnat[:, 0:ncols], axis=AX, op=OP.max,
            apply_absolute_value=True), wait=[tl], tick=True)
        trq = pr.op("vector", lambda e: e.reciprocal(qcol[:, 0:1], wrow[:, 0:1]),
                    wait=[trw], tick=True)
        tiv = pr.op("vector", lambda e: e.tensor_scalar(
            qcol[:, 0:1], qcol[:, 0:1], 127.0, None, op0=OP.mult), wait=[trq], tick=True)
        tdq = pr.op("vector", lambda e: e.tensor_scalar(
            wrow[:, 0:1], wrow[:, 0:1], 1.0 / 127.0, None, op0=OP.mult),
            wait=[trw], tick=True)
        t1_ = pr.op("scalar", lambda e: e.activation(
            wint[:, 0:ncols], wnat[:, 0:ncols], AF.Copy,
            scale=qcol[:, 0:1]), wait=[tiv], tick=True)
        qst["nat_free"] = t1_
        t2_ = pr.op("vector", lambda e: e.tensor_scalar(
            wint[:, 0:ncols], wint[:, 0:ncols], 127.0, -128.0, op0=OP.min, op1=OP.max),
            wait=[t1_], tick=True)
        t3_ = pr.op("scalar", lambda e: e.activation(
            wqn[:, 0:ncols], wint[:, 0:ncols], AF.Copy, scale=wrow[:, 0:1]),
            wait=[t2_, tdq, qst["wq_free"]], tick=True)
        toks = []
        for kk in range(ncols // P):
            tb_ = kk % 2
            ttr_ = tr(PTB[tb_][:, 0:P], wqn[:, P * kk:P * (kk + 1)], eye_b[:, :],
                      wait=[t3_, qst["ev"].get(tb_)], tick=True)
            qst["wq_free"] = ttr_
            qst["ev"][tb_] = pr.op("scalar", lambda e, kk=kk, tb_=tb_: e.copy(
                dstT[:, P * kk:P * (kk + 1)], PTB[tb_][:, 0:P]), wait=[ttr_], tick=True)
            toks.append(qst["ev"][tb_])
        return toks

    tok_psf = {}
    tok_sg_free = None
    tok_hb_free = None
    for mc in range(NC0):
        tg = quant_rows_to(wgT, w_gate, P * mc, D)
        tu = quant_rows_to(wuT, w_up, P * mc, D)
        for n in range(NB):
            for k in range(DT):
                mm(PS[3][:, :], wgT[:, P * k:P * (k + 1)], xqb[k][:, NBW * n:NBW * (n + 1)],
                   start=(k == 0), stop=(k == DT - 1),
                   wait=(tg + tu + tqdone if n == 0 and k == 0 else []) +
                        ([tok_psf.get(0)] if k == 0 else []))
            tgm_ = pr.tick("tensor")
            for k in range(DT):
                mm(PS[4][:, :], wuT[:, P * k:P * (k + 1)], xqb[k][:, NBW * n:NBW * (n + 1)],
                   start=(k == 0), stop=(k == DT - 1),
                   wait=[tok_psf.get(1)] if k == 0 else [])
            tum_ = pr.tick("tensor")

            ts_ = pr.op("scalar", lambda e: e.activation(
                sgt[:, :], PS[3][:, :], AF.Silu), wait=[tgm_, tok_sg_free], tick=True)
            tok_psf[0] = ts_
            th_ = pr.op("vector", lambda e: e.tensor_tensor(
                hbt[:, :], sgt[:, :], PS[4][:, :], op=OP.mult),
                wait=[ts_, tum_, tok_hb_free], tick=True)
            tok_psf[1] = th_
            tok_sg_free = th_
            if n == 0:
                pr.op("vector", lambda e, mc=mc: e.tensor_reduce(
                    hmaxc[:, mc:mc + 1], hbt[:, :], axis=AX, op=OP.max,
                    apply_absolute_value=True), wait=[th_], tick=True)
            else:
                trh_ = pr.op("vector", lambda e: e.tensor_reduce(
                    tmpc0[:, 0:1], hbt[:, :], axis=AX, op=OP.max,
                    apply_absolute_value=True), wait=[th_], tick=True)
                pr.op("vector", lambda e, mc=mc: e.tensor_tensor(
                    hmaxc[:, mc:mc + 1], hmaxc[:, mc:mc + 1], tmpc0[:, 0:1],
                    op=OP.max), wait=[trh_], tick=True)
            tok_hb_free = pr.op("scalar", lambda e, mc=mc, n=n: e.dma_start(
                out=h_sp[P * mc:P * (mc + 1), NBW * n:NBW * (n + 1)], in_=hbt[:, :]),
                wait=[th_], dma=True)
        pr.barrier()
        tok_psf = {}
        tok_sg_free = None
        tok_hb_free = None
    pr.barrier()
    pr.op("vector", lambda e: e.tensor_reduce(
        gmx[:, 0:1], hmaxc[:, 0:NC0], axis=AX, op=OP.max), tick=True)
    tgm = pr.tick("vector")
    tpm = part_max(mx11, gmx, 2, wait=[tgm])
    tcpm = pr.op("vector", lambda e: e.tensor_copy(mxrow[0:1, 0:1], mx11[0:1, 0:1]),
                 wait=[tpm], tick=True)
    tw_ = pr.op("sync", lambda e: e.dma_start(out=cc_mx_i[0:1, :], in_=mxrow[0:1, :]),
                wait=[tcpm], dma=True)
    tcc = pr.op("gpsimd", lambda e: e.collective_compute(
        "AllReduce", OP.max, replica_groups=[list(range(NCORE))],
        ins=[cc_mx_i[:, :]], outs=[cc_mx_o[:, :]]), wait=[tw_], tick=True)
    trd = pr.op("sync", lambda e: e.dma_start(out=mx11[0:1, 0:1], in_=cc_mx_o[0:1, 0:1]),
                wait=[tcc], dma=True)
    t_a = pr.op("vector", lambda e: e.tensor_tensor(
        sc_a[0:1, 0:1], mx11[0:1, 0:1], sig_cd[0:1, 0:1], op=OP.mult), wait=[trd], tick=True)
    t_b = pr.op("vector", lambda e: e.tensor_scalar(
        sc_b[0:1, 0:1], sc_a[0:1, 0:1], 1.0 / 127.0, None, op0=OP.mult),
        wait=[t_a], tick=True)
    t_r = pr.op("vector", lambda e: e.reciprocal(sc_a[0:1, 0:1], sc_a[0:1, 0:1]),
                wait=[t_b], tick=True)
    tqi = pr.op("vector", lambda e: e.tensor_scalar(
        sc_a[0:1, 0:1], sc_a[0:1, 0:1], 127.0, None, op0=OP.mult), wait=[t_r], tick=True)
    tbh1 = bcast_col(scol_a, sc_a, wait=[tqi], bank=2)
    tbc2 = bcast_col(scol_b, sc_b, wait=[tqi], bank=2)
    if debug_taps:
        pr.op("sync", lambda e: e.dma_start(
            out=dbg_sc[2][:, 0:1], in_=scol_a[:, 0:1]), wait=[tbh1], dma=True)
        pr.op("sync", lambda e: e.dma_start(
            out=dbg_sc[3][:, 0:1], in_=scol_b[:, 0:1]), wait=[tbc2], dma=True)
    pr.barrier()
    wdq_t = sb("wdqF0", [P, HID0], BF16)
    for m in range(DT):
        tks = quant_rows_to(wdq_t, w_down, P * m, HID0)
        tst_ = pr.op("scalar", lambda e, m=m: e.dma_start(
            out=wd_sp[P * m:P * (m + 1), :], in_=wdq_t[:, :]), wait=tks, dma=True)
        qst["wq_free"] = tst_
    pr.barrier()
    hqb = [sb("hqbF0_%d" % k, [P, NBW], BF16) for k in range(NC0)]
    wdl = [sb("wdlF0_%d" % i, [P, HID0], BF16) for i in range(2)]
    hin = sb("hinF0", [P, NBW], BF16)
    hii = sb("hiiF0", [P, NBW], I32)
    xs2 = [sb("xs2F0_%d" % m, [P, NBW]) for m in range(DT)]
    xr2 = [sb("xr2F0_%d" % m, [P, NBW]) for m in range(DT)]
    sqt2 = sb("sqt2F0", [P, NBW])
    rows2 = {"mrow": sb("mrow2", [1, NBW]), "m2row": sb("m2row2", [1, NBW]),
             "vrow": sb("vrow2", [1, NBW]), "rrow": sb("rrow2", [1, NBW]),
             "xo": sb("xo2", [P, NBW])}
    for n in range(NB):
        tq = [None] * NC0
        for k in range(NC0):
            tl = pr.op("sync", lambda e, k=k, n=n: e.dma_start(
                out=hin[:, :], in_=h_sp[P * k:P * (k + 1), NBW * n:NBW * (n + 1)]),
                wait=[tq[k - 1] if k else None], dma=True)
            t1_ = pr.op("scalar", lambda e: e.activation(
                hii[:, :], hin[:, :], AF.Copy, scale=scol_a[:, 0:1]), wait=[tl], tick=True)
            t2_ = pr.op("vector", lambda e: e.tensor_scalar(
                hii[:, :], hii[:, :], 127.0, -128.0, op0=OP.min, op1=OP.max),
                wait=[t1_], tick=True)
            tq[k] = pr.op("scalar", lambda e, k=k: e.activation(
                hqb[k][:, :], hii[:, :], AF.Copy, scale=scol_b[:, 0:1]),
                wait=[t2_], tick=True)
        txs = [None] * DT
        for m_ in range(DT):
            txs[m_] = pr.op("scalar", lambda e, m_=m_, n=n: e.dma_start(
                out=xs2[m_][:, :], in_=xres[P * m_:P * (m_ + 1), NBW * n:NBW * (n + 1)]),
                dma=True)
        xr_tok = [None] * DT
        tok_psf = {}
        tok_wdl = {0: None, 1: None}
        for m_ in range(DT):
            bank = m_ % 2
            twl = pr.op("gpsimd", lambda e, m_=m_, bank=bank: e.dma_start(
                out=wdl[bank][:, :], in_=wd_sp[P * m_:P * (m_ + 1), :]),
                wait=[tok_wdl[bank]], dma=True)
            for k in range(NC0):
                mm(PS[bank][:, :], wdl[bank][:, P * k:P * (k + 1)], hqb[k][:, :],
                   start=(k == 0), stop=(k == NC0 - 1),
                   wait=([tq[k]] if m_ < 2 else []) + ([twl] if k == 0 else []) +
                        ([tok_psf.get(bank)] if k == 0 else []))
            tdm = pr.tick("tensor")
            tok_wdl[bank] = tdm
            xr_tok[m_] = pr.op("vector", lambda e, m_=m_, bank=bank: e.tensor_tensor(
                xr2[m_][:, :], PS[bank][:, :], xs2[m_][:, :], op=OP.add),
                wait=[tdm, txs[m_]], tick=True)
            tok_psf[bank] = xr_tok[m_]
        rows2["xr_tok"] = xr_tok

        def out_fn2(m_, xo_t, tok, n=n):
            return pr.op("scalar", lambda e, m_=m_, n=n: e.dma_start(
                out=xres[P * m_:P * (m_ + 1), NBW * n:NBW * (n + 1)], in_=xo_t[:, :]),
                wait=[tok], dma=True)

        feat_ln_block(n, xr2, sqt2, rows2, None, (4, 5), out_fn2)
        pr.barrier()
    end()
    if debug_taps:
        pr.op("sync", lambda e: e.dma_start(out=dbg_x2[:, :], in_=xres[:, :]),
              dma=True)
    pr.barrier()

    begin()
    halo3v = halo3_i[0:1, :].rearrange("one (d t) -> d (one t)", t=3)
    halo3ov = halo3_o[0:1, :].rearrange("one (d t) -> d (one t)", t=3)
    th_w = pr.op("sync", lambda e: e.dma_start(
        out=halo3v, in_=xres[:, TLOC - 3:TLOC]), dma=True)
    tcc = pr.op("gpsimd", lambda e: e.collective_compute(
        "AllGather", OP.bypass, replica_groups=[[0, 1], [2, 3], [4, 5], [6, 7]],
        ins=[halo3_i[:, :]], outs=[halo3_o[:, :]]), wait=[th_w], tick=True)
    ipw = [sb("ipwMA%d" % k, [P, 2 * INNER], BF16) for k in range(DT)]
    dtw = [sb("dtwMA%d" % k, [P, INNER], BF16) for k in range(16)]
    bw_sb = sb("bwMA", [P, 16 * 16], BF16)
    cw_sb2 = sb("cwMA", [P, 16 * 16], BF16)
    cwc = sb("cwcMA", [P, 16 * 4])
    cbc = sb("cbcMA", [P, 16])
    dtbc = sb("dtbMA", [P, 16])
    for k in range(DT):
        pr.op("gpsimd", lambda e, k=k: e.dma_start(
            out=ipw[k][:, :], in_=inp_wT[P * k:P * (k + 1), :]), dma=True)
    for k in range(16):
        pr.op("gpsimd", lambda e, k=k: e.dma_start(
            out=dtw[k][:, :], in_=dt_wT[P * k:P * (k + 1), :]), dma=True)
    for k in range(16):
        pr.op("gpsimd", lambda e, k=k: e.dma_start(
            out=bw_sb[:, 16 * k:16 * (k + 1)], in_=b_wT[P * k:P * (k + 1), :]), dma=True)
        pr.op("gpsimd", lambda e, k=k: e.dma_start(
            out=cw_sb2[:, 16 * k:16 * (k + 1)], in_=c_wT[P * k:P * (k + 1), :]), dma=True)
        pr.op("sync", lambda e, k=k: e.dma_start(
            out=cwc[:, 4 * k:4 * (k + 1)], in_=conv_w[P * k:P * (k + 1), :]), dma=True)
        pr.op("sync", lambda e, k=k: e.dma_start(
            out=cbc[:, k:k + 1], in_=conv_b[P * k:P * (k + 1), :]), dma=True)
        pr.op("sync", lambda e, k=k: e.dma_start(
            out=dtbc[:, k:k + 1], in_=dt_b[P * k:P * (k + 1), :]), dma=True)
    xbm = [sb("xbMA%d" % k, [P, NBW], BF16) for k in range(DT)]
    xh = sb("xhMA", [P, 3])
    xhb = [sb("xhbMA%d" % k, [P, 3], BF16) for k in range(DT)]
    xin_ext = [sb("xinMA%d" % m, [P, 515]) for m in range(16)]
    cv1 = sb("cv1MA", [P, NBW])
    cv2 = sb("cv2MA", [P, NBW])
    xin_b = [sb("xinbMA%d" % m, [P, NBW], BF16) for m in range(16)]
    gt = sb("gtMA", [P, NBW], BF16)
    ztile = sb("ztMA", [P, NBW])
    abst = cv1
    expt = cv2
    lnt = cv1
    dtv = sb("dtvMA", [P, NBW])
    dxv = sb("dxvMA", [P, NBW], BF16)
    brow = sb("browMA", [16, NBW])
    pr.barrier()
    thbs = []
    for k in range(DT):
        tl = pr.op("sync", lambda e, k=k: e.dma_start(
            out=xh[:, :], in_=halo3ov[P * k:P * (k + 1), :]),
            wait=[tcc, thbs[-1] if thbs else None], dma=True)
        tm_ = pr.op("vector", lambda e: e.tensor_scalar(
            xh[:, :], xh[:, :], hcol[:, 0:1], None, op0=OP.mult), wait=[tl], tick=True)
        thbs.append(pr.op("vector", lambda e, k=k: e.tensor_copy(xhb[k][:, :], xh[:, :]),
                          wait=[tm_], tick=True))
    thb = pr.tick("vector")
    tgw = pr.dma_tokens("gpsimd")
    tprev_ev = None
    for m_ in range(16):
        for k in range(DT):
            mm(PS[2][:, 0:3], ipw[k][:, P * m_:P * (m_ + 1)], xhb[k][:, :],
               start=(k == 0), stop=(k == DT - 1),
               wait=([thb, tprev_ev] + tgw) if k == 0 else [])
        th_ = pr.tick("tensor")
        tprev_ev = pr.op("vector", lambda e, m_=m_: e.tensor_copy(
            xin_ext[m_][:, 0:3], PS[2][:, 0:3]), wait=[th_], tick=True)
    pr.barrier()
    for n in range(NB):
        for k in range(DT):
            pr.op("gpsimd", lambda e, k=k, n=n: e.dma_start(
                out=xbm[k][:, :], in_=xres[P * k:P * (k + 1), NBW * n:NBW * (n + 1)]),
                dma=True)
        tgl = pr.dma_tokens("gpsimd")
        if n > 0:
            for m_ in range(16):
                pr.op("vector", lambda e, m_=m_: e.tensor_copy(
                    xin_ext[m_][:, 0:3], xin_ext[m_][:, NBW:NBW + 3]), tick=True)
        tok_tail = pr.tick("vector")
        tok_psf = {}
        tev_x = [None] * 16
        tok_g = None
        for m_ in range(32):
            bank = m_ % 2
            for k in range(DT):
                mm(PS[bank][:, :], ipw[k][:, P * m_:P * (m_ + 1)], xbm[k][:, :],
                   start=(k == 0), stop=(k == DT - 1),
                   wait=((tgl + [tok_tail]) if m_ < 2 and k == 0 else []) +
                        ([tok_psf.get(bank)] if k == 0 else []))
            tmm_ = pr.tick("tensor")
            if m_ < 16:
                tev_x[m_] = pr.op("vector", lambda e, m_=m_, bank=bank: e.tensor_copy(
                    xin_ext[m_][:, 3:3 + NBW], PS[bank][:, :]), wait=[tmm_], tick=True)
                tok_psf[bank] = tev_x[m_]
            else:
                tsl = pr.op("scalar", lambda e, bank=bank: e.activation(
                    gt[:, :], PS[bank][:, :], AF.Silu), wait=[tmm_, tok_g], tick=True)
                tok_psf[bank] = tsl
                r = m_ - 16
                tok_g = pr.op("scalar", lambda e, r=r, n=n: e.dma_start(
                    out=a2a_g_i[r // 2, P * (r % 2):P * (r % 2) + P, NBW * n:NBW * (n + 1)],
                    in_=gt[:, :]), wait=[tsl], dma=True)
        tsil = [None] * 16
        tok_cv = None
        for m_ in range(16):
            t1_ = pr.op("vector", lambda e, m_=m_: e.tensor_scalar(
                cv1[:, :], xin_ext[m_][:, 0:NBW], cwc[:, 4 * m_:4 * m_ + 1],
                cbc[:, m_:m_ + 1], op0=OP.mult, op1=OP.add),
                wait=[tev_x[m_], tok_cv], tick=True)
            pr.op("vector", lambda e, m_=m_: e.scalar_tensor_tensor(
                cv2[:, :], xin_ext[m_][:, 1:1 + NBW], cwc[:, 4 * m_ + 1:4 * m_ + 2],
                cv1[:, :], op0=OP.mult, op1=OP.add), tick=True)
            pr.op("vector", lambda e, m_=m_: e.scalar_tensor_tensor(
                cv1[:, :], xin_ext[m_][:, 2:2 + NBW], cwc[:, 4 * m_ + 2:4 * m_ + 3],
                cv2[:, :], op0=OP.mult, op1=OP.add), tick=True)
            t4_ = pr.op("vector", lambda e, m_=m_: e.scalar_tensor_tensor(
                cv2[:, :], xin_ext[m_][:, 3:3 + NBW], cwc[:, 4 * m_ + 3:4 * m_ + 4],
                cv1[:, :], op0=OP.mult, op1=OP.add), tick=True)
            tsil[m_] = pr.op("scalar", lambda e, m_=m_: e.activation(
                xin_b[m_][:, :], cv2[:, :], AF.Silu), wait=[t4_], tick=True)
            tok_cv = tsil[m_]
        tok_psf = {}
        tok_z = None
        tok_dtv = None
        tok_dxv = None
        tok_ln = None
        tok_abst = None
        for md in range(16):
            bank = md % 2
            for k in range(16):
                mm(PS[bank][:, :], dtw[k][:, P * md:P * (md + 1)], xin_b[k][:, :],
                   start=(k == 0), stop=(k == 15),
                   wait=([tsil[k]] if md < 2 else []) +
                        ([tok_psf.get(bank)] if k == 0 else []))
            tmm_ = pr.tick("tensor")
            tz = pr.op("vector", lambda e, md=md, bank=bank: e.tensor_scalar(
                ztile[:, :], PS[bank][:, :], dtbc[:, md:md + 1], None, op0=OP.add),
                wait=[tmm_, tok_z], tick=True)
            tok_psf[bank] = tz
            ta = pr.op("vector", lambda e: e.scalar_tensor_tensor(
                abst[:, :], ztile[:, :], -1.0, ztile[:, :], op0=OP.mult, op1=OP.max),
                wait=[tz, tok_abst], tick=True)
            te = pr.op("scalar", lambda e: e.activation(
                expt[:, :], abst[:, :], AF.Exp, scale=-1.0), wait=[ta], tick=True)
            tok_abst = te
            tl_ = pr.op("scalar", lambda e: e.activation(
                lnt[:, :], expt[:, :], AF.Ln, bias=1.0), wait=[tok_ln], tick=True)
            td = pr.op("vector", lambda e: e.scalar_tensor_tensor(
                dtv[:, :], ztile[:, :], 0.0, lnt[:, :], op0=OP.max, op1=OP.add),
                wait=[tl_, tok_dtv], tick=True)
            tok_z = td
            tok_ln = td
            tx_ = pr.op("vector", lambda e, md=md: e.tensor_tensor(
                dxv[:, :], dtv[:, :], xin_b[md][:, :], op=OP.mult),
                wait=[td, tok_dxv], tick=True)
            tok_dtv = pr.op("gpsimd", lambda e, md=md, n=n: e.dma_start(
                out=a2a_dt_i[md // 2, P * (md % 2):P * (md % 2) + P, NBW * n:NBW * (n + 1)],
                in_=dtv[:, :]), wait=[td], dma=True)
            tok_dxv = pr.op("sync", lambda e, md=md, n=n: e.dma_start(
                out=a2a_dx_i[md // 2, P * (md % 2):P * (md % 2) + P, NBW * n:NBW * (n + 1)],
                in_=dxv[:, :]), wait=[tx_], dma=True)
        for wsb, rrow0 in ((bw_sb, 0), (cw_sb2, 16)):
            for k in range(16):
                mm(PS[2][0:16, :], wsb[:, 16 * k:16 * (k + 1)], xin_b[k][:, :],
                   start=(k == 0), stop=(k == 15),
                   wait=[rows_g.get("bc_free")] if k == 0 else [])
            tmm_ = pr.tick("tensor")
            tbr = pr.op("scalar", lambda e: e.copy(brow[0:16, :], PS[2][0:16, :]),
                        wait=[tmm_, rows_g.get("bc_st")], tick=True)
            rows_g["bc_free"] = tbr
            rows_g["bc_st"] = pr.op("sync", lambda e, rrow0=rrow0, n=n: e.dma_start(
                out=bc_ag_i[rrow0:rrow0 + 16, NBW * n:NBW * (n + 1)], in_=brow[0:16, :]),
                wait=[tbr], dma=True)
        pr.barrier()
    tg1 = pr.op("gpsimd", lambda e: e.collective_compute(
        "AllGather", OP.bypass, replica_groups=[list(range(NCORE))],
        ins=[bc_ag_i[:, :]], outs=[bc_ag_o[:, :]]), tick=True)
    tg2 = pr.op("gpsimd", lambda e: e.collective_compute(
        "AllToAll", OP.bypass, replica_groups=[list(range(NCORE))],
        ins=[a2a_dt_i[:, :, :]], outs=[a2a_dt_o[:, :, :]]), tick=True)
    tg3 = pr.op("gpsimd", lambda e: e.collective_compute(
        "AllToAll", OP.bypass, replica_groups=[list(range(NCORE))],
        ins=[a2a_dx_i[:, :, :]], outs=[a2a_dx_o[:, :, :]]), tick=True)
    tg4 = pr.op("gpsimd", lambda e: e.collective_compute(
        "AllToAll", OP.bypass, replica_groups=[list(range(NCORE))],
        ins=[a2a_g_i[:, :, :]], outs=[a2a_g_o[:, :, :]]), tick=True)
    end()

    begin()
    sel_sb = sb("sel_sb", [16, 16 * P])
    pr.op("sync", lambda e: e.dma_start(out=sel_sb[:, :], in_=c_sel16[:, :]), dma=True)
    dt_sb = sb("dtMB", [P, 2 * TLOC], BF16)
    dx_sb = sb("dxMB", [P, 2 * TLOC], BF16)
    g_sb = sb("gMB", [P, 2 * TLOC], BF16)
    y_sb = sb("yMB", [P, 2 * TLOC])
    yb_sb = sb("ybMB", [P, 2 * TLOC], BF16)
    bg_sb = sb("bgMB", [16, 2 * TLOC])
    cg_sb = sb("cgMB", [16, 2 * TLOC])
    a_t = [sb("atMB%d" % i, [P, NBW]) for i in range(2)]
    u_t = [sb("utMB%d" % i, [P, NBW]) for i in range(2)]
    st_t = [sb("stMB%d" % i, [P, NBW]) for i in range(2)]
    w_t = sb("wtMB", [P, NBW])
    TCN = 2 * TLOC // NBW
    tok_yst = None
    for b_ in range(4):
        tbgl = []
        for hh in range(2):
            tbgl.append(pr.op("sync", lambda e, b_=b_, hh=hh: e.dma_start(
                out=bg_sb[0:16, TLOC * hh:TLOC * (hh + 1)],
                in_=bc_ag_o[32 * (2 * b_ + hh):32 * (2 * b_ + hh) + 16, :]),
                wait=[tg1], dma=True))
            tbgl.append(pr.op("sync", lambda e, b_=b_, hh=hh: e.dma_start(
                out=cg_sb[0:16, TLOC * hh:TLOC * (hh + 1)],
                in_=bc_ag_o[32 * (2 * b_ + hh) + 16:32 * (2 * b_ + hh) + 32, :]),
                dma=True))
        for dtile in range(2):
            r0 = P * dtile
            tld = []
            for hh in range(2):
                tld.append(pr.op("sync", lambda e, b_=b_, hh=hh, r0=r0: e.dma_start(
                    out=dt_sb[:, TLOC * hh:TLOC * (hh + 1)],
                    in_=a2a_dt_o[2 * b_ + hh, r0:r0 + P, :]), wait=[tg2], dma=True))
                tld.append(pr.op("scalar", lambda e, b_=b_, hh=hh, r0=r0: e.dma_start(
                    out=dx_sb[:, TLOC * hh:TLOC * (hh + 1)],
                    in_=a2a_dx_o[2 * b_ + hh, r0:r0 + P, :]), wait=[tg3], dma=True))
                tld.append(pr.op("gpsimd", lambda e, b_=b_, hh=hh, r0=r0: e.dma_start(
                    out=g_sb[:, TLOC * hh:TLOC * (hh + 1)],
                    in_=a2a_g_o[2 * b_ + hh, r0:r0 + P, :]), wait=[tg4], dma=True))
            tym = pr.op("vector", lambda e: e.memset(y_sb[:, :], 0.0),
                        wait=[tok_yst], tick=True)
            tok_u = {}
            tok_w = {}
            tok_afree = {0: None, 1: None}
            tok_scan = {0: None, 1: None}
            for s_ in range(NSTATE):
                for tc in range(TCN):
                    bb = 2 + 2 * (tc % 2)
                    bcq = 3 + 2 * (tc % 2)
                    ia = tc % 2
                    tB = mm(PS[bb][:, :], sel_sb[:, P * s_:P * (s_ + 1)],
                            bg_sb[0:16, NBW * tc:NBW * (tc + 1)], start=True, stop=True,
                            wait=[tok_u.get(bb)] + tbgl, tick=True)
                    tC = mm(PS[bcq][:, :], sel_sb[:, P * s_:P * (s_ + 1)],
                            cg_sb[0:16, NBW * tc:NBW * (tc + 1)], start=True, stop=True,
                            wait=[tok_w.get(bcq)], tick=True)
                    ta_ = pr.op("scalar",
                                lambda e, s_=s_, tc=tc, ia=ia, dtile=dtile: e.activation(
                                    a_t[ia][:, :], dt_sb[:, NBW * tc:NBW * (tc + 1)],
                                    AF.Exp, scale=aneg[dtile][:, s_:s_ + 1]),
                                wait=[tok_afree[ia], tld[0], tld[3]], tick=True)
                    tu_ = pr.op("vector", lambda e, tc=tc, ia=ia, bb=bb: e.tensor_tensor(
                        u_t[ia][:, :], dx_sb[:, NBW * tc:NBW * (tc + 1)], PS[bb][:, :],
                        op=OP.mult), wait=[tB, tld[1], tld[4]], tick=True)
                    tok_u[bb] = tu_
                    init = 0.0 if tc == 0 else st_t[(tc + 1) % 2][:, NBW - 1:NBW]
                    tst_ = pr.op("vector", lambda e, ia=ia, init=init: e.tensor_tensor_scan(
                        st_t[ia][:, :], a_t[ia][:, :], u_t[ia][:, :], init,
                        op0=OP.mult, op1=OP.add), wait=[ta_], tick=True)
                    tok_afree[ia] = tst_
                    tw_ = pr.op("vector", lambda e, ia=ia, bcq=bcq: e.tensor_tensor(
                        w_t[:, :], st_t[ia][:, :], PS[bcq][:, :], op=OP.mult),
                        wait=[tC], tick=True)
                    tok_w[bcq] = tw_
                    pr.op("vector", lambda e, tc=tc: e.tensor_tensor(
                        y_sb[:, NBW * tc:NBW * (tc + 1)], y_sb[:, NBW * tc:NBW * (tc + 1)],
                        w_t[:, :], op=OP.add), wait=[tym], tick=True)
            tyv = pr.tick("vector")
            tyb = pr.op("vector", lambda e: e.tensor_tensor(
                yb_sb[:, :], y_sb[:, :], g_sb[:, :], op=OP.mult),
                wait=[tyv, tld[2], tld[5]], tick=True)
            ty1 = pr.op("sync", lambda e, b_=b_, r0=r0: e.dma_start(
                out=a2a_y_i[2 * b_, r0:r0 + P, :], in_=yb_sb[:, 0:TLOC]),
                wait=[tyb], dma=True)
            ty2 = pr.op("sync", lambda e, b_=b_, r0=r0: e.dma_start(
                out=a2a_y_i[2 * b_ + 1, r0:r0 + P, :], in_=yb_sb[:, TLOC:2 * TLOC]),
                wait=[tyb], dma=True)
            tok_yst = ty2
            pr.barrier()
    tg5 = pr.op("gpsimd", lambda e: e.collective_compute(
        "AllToAll", OP.bypass, replica_groups=[list(range(NCORE))],
        ins=[a2a_y_i[:, :, :]], outs=[a2a_y_o[:, :, :]]), tick=True)
    end()

    begin()
    owT = [sb("owMC%d" % k, [P, D], BF16) for k in range(16)]
    for k in range(16):
        pr.op("gpsimd", lambda e, k=k: e.dma_start(
            out=owT[k][:, :], in_=out_wT[P * k:P * (k + 1), :]), dma=True)
    ytl = [sb("ytlMC%d" % k, [P, NBW], BF16) for k in range(16)]
    xs3 = [sb("xs3MC%d" % m, [P, NBW]) for m in range(DT)]
    xr3 = [sb("xr3MC%d" % m, [P, NBW]) for m in range(DT)]
    sqt3 = sb("sqt3MC", [P, NBW])
    rows3 = {"mrow": sb("mrow3", [1, NBW]), "m2row": sb("m2row3", [1, NBW]),
             "vrow": sb("vrow3", [1, NBW]), "rrow": sb("rrow3", [1, NBW]),
             "xo": sb("xo3", [P, NBW])}
    pr.barrier()
    for n in range(NB):
        ty = [None] * 16
        for k in range(16):
            ty[k] = pr.op("sync", lambda e, k=k, n=n: e.dma_start(
                out=ytl[k][:, :],
                in_=a2a_y_o[k // 2, P * (k % 2):P * (k % 2) + P, NBW * n:NBW * (n + 1)]),
                wait=[tg5], dma=True)
        txs = [None] * DT
        for m_ in range(DT):
            txs[m_] = pr.op("scalar", lambda e, m_=m_, n=n: e.dma_start(
                out=xs3[m_][:, :], in_=xres[P * m_:P * (m_ + 1), NBW * n:NBW * (n + 1)]),
                dma=True)
        xr_tok = [None] * DT
        tok_psf = {}
        for m_ in range(DT):
            bank = m_ % 2
            for k in range(16):
                mm(PS[bank][:, :], owT[k][:, P * m_:P * (m_ + 1)], ytl[k][:, :],
                   start=(k == 0), stop=(k == 15),
                   wait=([ty[k]] + pr.dma_tokens("gpsimd") if m_ < 2 else []) +
                        ([tok_psf.get(bank)] if k == 0 else []))
            tdm = pr.tick("tensor")
            xr_tok[m_] = pr.op("vector", lambda e, m_=m_, bank=bank: e.tensor_tensor(
                xr3[m_][:, :], PS[bank][:, :], xs3[m_][:, :], op=OP.add),
                wait=[tdm, txs[m_]], tick=True)
            tok_psf[bank] = xr_tok[m_]
        rows3["xr_tok"] = xr_tok

        def out_fn3(m_, xo_t, tok, n=n):
            return pr.op("scalar", lambda e, m_=m_, n=n: e.dma_start(
                out=xres[P * m_:P * (m_ + 1), NBW * n:NBW * (n + 1)], in_=xo_t[:, :]),
                wait=[tok], dma=True)

        feat_ln_block(n, xr3, sqt3, rows3, None, (4, 5), out_fn3)
        pr.barrier()
    end()

    begin()
    halo1vA = halo1_i[0:1, 0:8 * D].rearrange("one (d t) -> d (one t)", t=8)
    halo1vB = halo1_i[0:1, 8 * D:16 * D].rearrange("one (d t) -> d (one t)", t=8)
    halo1oA = halo1_o[1:2, 0:8 * D].rearrange("one (d t) -> d (one t)", t=8)
    halo1oB = halo1_o[0:1, 8 * D:16 * D].rearrange("one (d t) -> d (one t)", t=8)
    tw1 = pr.op("sync", lambda e: e.dma_start(out=halo1vA, in_=xres[:, 0:8]), dma=True)
    tw2 = pr.op("sync", lambda e: e.dma_start(out=halo1vB,
                                              in_=xres[:, TLOC - 8:TLOC]), dma=True)
    tcch = pr.op("gpsimd", lambda e: e.collective_compute(
        "AllGather", OP.bypass, replica_groups=[[0, 1], [2, 3], [4, 5], [6, 7]],
        ins=[halo1_i[:, :]], outs=[halo1_o[:, :]]), wait=[tw1, tw2], tick=True)
    hmacc = sb("hmaccF1", [P, TLOC], BF16)
    s2cols = sb("s2colsF1", [P, 32])
    is2cols = sb("is2colsF1", [P, 32])
    xq_hal = [sb("xqhF1_%d" % k, [P, 2], BF16) for k in range(DT)]
    sr32 = sb("sr32F1", [32, P])
    begin()
    xqb_f = [sb("xqbF1_%d" % k, [P, TLOC], BF16) for k in range(DT)]
    begin()
    xfm = [sb("xfmF1_%d" % i, [P, P]) for i in range(2)]
    xtm = sb("xtmF1", [P, D])
    xqtm = sb("xqtmF1", [P, D], BF16)
    xitm = sb("xitmF1", [P, D], I32)
    tkmax = sb("tkmaxF1", [P, 1])
    tscl = sb("tsclF1", [P, 1])
    tiscl = sb("tisclF1", [P, 1])
    tok_xfm = {0: None, 1: None}
    tok_trev = {0: None, 1: None, "b0": None, "b1": None}
    tok_round = None
    tok_xtm = None
    for tb in range(TLOC // P):
        tcps = []
        for f in range(DT):
            i2 = f % 2
            bk = 2 + i2
            tlx = pr.op("sync", lambda e, f=f, tb=tb, i2=i2: e.dma_start(
                out=xfm[i2][:, :], in_=xres[P * f:P * (f + 1), P * tb:P * (tb + 1)]),
                wait=[tok_xfm[i2]], dma=True)
            ttr_ = tr(PS[bk][:, 0:P], xfm[i2][:, :], eye_f[:, :],
                      wait=[tlx, tok_trev[i2]], tick=True)
            tok_xfm[i2] = ttr_
            tcp = pr.op("scalar", lambda e, f=f, bk=bk: e.copy(
                xtm[:, P * f:P * (f + 1)], PS[bk][:, 0:P]),
                wait=[ttr_, tok_xtm], tick=True)
            tok_trev[i2] = tcp
            tcps.append(tcp)
        tmx = pr.op("vector", lambda e: e.tensor_reduce(
            tkmax[:, 0:1], xtm[:, :], axis=AX, op=OP.max,
            apply_absolute_value=True), wait=tcps, tick=True)
        tkc = pr.op("vector", lambda e: e.tensor_scalar(
            tkmax[:, 0:1], tkmax[:, 0:1], 1e-5, None, op0=OP.max),
            wait=[tmx], tick=True)
        trk = pr.op("vector", lambda e: e.reciprocal(tscl[:, 0:1], tkmax[:, 0:1]),
                    wait=[tkc], tick=True)
        tsc = pr.op("vector", lambda e: e.tensor_scalar(
            tscl[:, 0:1], tscl[:, 0:1], 127.0, None, op0=OP.mult), wait=[trk], tick=True)
        tis = pr.op("vector", lambda e: e.tensor_scalar(
            tiscl[:, 0:1], tkmax[:, 0:1], 1.0 / 127.0, None, op0=OP.mult),
            wait=[tok_round], tick=True)
        tri = pr.op("scalar", lambda e: e.activation(
            xitm[:, :], xtm[:, :], AF.Copy, scale=tscl[:, 0:1]),
            wait=[tsc, tok_round], tick=True)
        tok_xtm = tri
        trq = pr.op("scalar", lambda e: e.activation(
            xqtm[:, :], xitm[:, :], AF.Copy, scale=tiscl[:, 0:1]),
            wait=[tis, tok_trev["b0"], tok_trev["b1"]], tick=True)
        tok_round = trq
        for f in range(DT):
            i2 = f % 2
            ttr_ = tr(PTB[i2][:, 0:P], xqtm[:, P * f:P * (f + 1)], eye_b[:, :],
                      wait=[trq, tok_trev["b%d" % i2]], tick=True)
            tok_trev["b%d" % i2] = pr.op("scalar", lambda e, f=f, tb=tb, i2=i2: e.copy(
                xqb_f[f][:, P * tb:P * (tb + 1)], PTB[i2][:, 0:P]), wait=[ttr_], tick=True)
    pr.barrier()
    if debug_taps:
        for k in range(DT):
            pr.op("sync", lambda e, k=k: e.dma_start(
                out=dbg_f1xq[P * k:P * (k + 1), :], in_=xqb_f[k][:, :]), dma=True)
    xcol = sb("xcolF1", [P, 2])
    xci = sb("xciF1", [P, 2], I32)
    hmx2 = sb("hmx2F1", [P, 2])
    m11a = sb("m11aF1", [1, 1])
    colsc = sb("colscF1", [P, 2])
    xcl = [sb("xclF1_%d" % k, [P, 2]) for k in range(DT)]
    xcl8 = sb("xcl8F1", [P, 8])
    tmsk = []
    tl2 = None
    for k in range(DT):
        tl1 = pr.op("sync", lambda e, k=k: e.dma_start(
            out=xcl8[:, :], in_=halo1oB[P * k:P * (k + 1), :]),
            wait=[tcch, tl2], dma=True)
        tc1 = pr.op("vector", lambda e, k=k: e.tensor_copy(
            xcl[k][:, 0:1], xcl8[:, 7:8]), wait=[tl1], tick=True)
        tl2 = pr.op("sync", lambda e, k=k: e.dma_start(
            out=xcl8[:, :], in_=halo1oA[P * k:P * (k + 1), :]), wait=[tc1], dma=True)
        tc2 = pr.op("vector", lambda e, k=k: e.tensor_copy(
            xcl[k][:, 1:2], xcl8[:, 0:1]), wait=[tl2], tick=True)
        tm1 = pr.op("vector", lambda e, k=k: e.tensor_scalar(
            xcl[k][:, 0:1], xcl[k][:, 0:1], hcol[:, 0:1], None, op0=OP.mult),
            wait=[tc1], tick=True)
        tm2 = pr.op("vector", lambda e, k=k: e.tensor_scalar(
            xcl[k][:, 1:2], xcl[k][:, 1:2], rcol[:, 0:1], None, op0=OP.mult),
            wait=[tc2], tick=True)
        tmsk.append(tm2)
        if k == 0:
            pr.op("vector", lambda e, k=k: e.scalar_tensor_tensor(
                hmx2[:, :], xcl[k][:, :], -1.0, xcl[k][:, :], op0=OP.mult, op1=OP.max),
                wait=[tm1, tm2], tick=True)
        else:
            pr.op("vector", lambda e, k=k: e.scalar_tensor_tensor(
                xcol[:, :], xcl[k][:, :], -1.0, xcl[k][:, :], op0=OP.mult, op1=OP.max),
                wait=[tm1, tm2], tick=True)
            pr.op("vector", lambda e: e.tensor_tensor(
                hmx2[:, :], hmx2[:, :], xcol[:, :], op=OP.max), tick=True)
    thx = pr.tick("vector")
    for cidx in range(2):
        tpm_ = part_max(m11a, hmx2[:, cidx:cidx + 1], 2, wait=[thx])
        pr.op("vector", lambda e: e.tensor_scalar(
            m11a[0:1, 0:1], m11a[0:1, 0:1], 1e-5, None, op0=OP.max), wait=[tpm_], tick=True)
        tbs = bcast_col(colsc, m11a, wait=[pr.tick("vector")], bank=2)
        tcl = pr.op("vector", lambda e, cidx=cidx: e.tensor_scalar(
            colsc[:, 0:1], colsc[:, 0:1], 1e30, None, op0=OP.min), wait=[tbs], tick=True)
        trc_ = pr.op("vector", lambda e: e.reciprocal(colsc[:, 1:2], colsc[:, 0:1]),
                     wait=[tcl], tick=True)
        ts1 = pr.op("vector", lambda e: e.tensor_scalar(
            colsc[:, 1:2], colsc[:, 1:2], 127.0, None, op0=OP.mult),
            wait=[trc_], tick=True)
        ts2 = pr.op("vector", lambda e: e.tensor_scalar(
            colsc[:, 0:1], colsc[:, 0:1], 1.0 / 127.0, None, op0=OP.mult), tick=True)
        for k in range(DT):
            t1_ = pr.op("scalar", lambda e, k=k, cidx=cidx: e.activation(
                xci[:, cidx:cidx + 1], xcl[k][:, cidx:cidx + 1], AF.Copy,
                scale=colsc[:, 1:2]), wait=[ts1, ts2], tick=True)
            pr.op("scalar", lambda e, k=k, cidx=cidx: e.activation(
                xq_hal[k][:, cidx:cidx + 1], xci[:, cidx:cidx + 1], AF.Copy,
                scale=colsc[:, 0:1]), tick=True)
    end()
    w1q = [sb("w1qF1_%d" % k, [P, HID1], BF16) for k in range(DT)]
    begin()
    wnat1 = sb("wnat1F1", [P, HID1])
    wacc = sb("waccF1", [P, 1])
    wi1 = sb("wi1F1", [P, HID1], I32)
    s1col = sb("s1colF1", [P, 1])
    d1col = sb("d1colF1", [P, 1])
    tqtmp = sb("tqtmpF1", [P, 1])
    m11w = sb("m11wF1", [1, 1])

    def ternary_quant(dst_list, src_dram, nkc, ncols, nel):
        tred = None
        tldw = None
        for k in range(nkc):
            tldw = pr.op("sync", lambda e, k=k: e.dma_start(
                out=wnat1[:, 0:ncols], in_=src_dram[P * k:P * (k + 1), 0:ncols]),
                wait=[tred], dma=True)
            if k == 0:
                tred = pr.op("vector", lambda e: e.tensor_reduce(
                    wacc[:, 0:1], wnat1[:, 0:ncols], axis=AX, op=OP.add,
                    apply_absolute_value=True), wait=[tldw], tick=True)
            else:
                trt_ = pr.op("vector", lambda e: e.tensor_reduce(
                    tqtmp[:, 0:1], wnat1[:, 0:ncols], axis=AX, op=OP.add,
                    apply_absolute_value=True), wait=[tldw], tick=True)
                tred = pr.op("vector", lambda e: e.tensor_tensor(
                    wacc[:, 0:1], wacc[:, 0:1], tqtmp[:, 0:1], op=OP.add),
                    wait=[trt_], tick=True)
        if debug_taps:
            pr.op("sync", lambda e: e.dma_start(
                out=dbg_tqw[:, 0:1], in_=wacc[:, 0:1]), wait=[tred], dma=True)
        ttp = tr(PS[2][0:1, 0:P], wacc[:, 0:1], eye_f[:, :],
                 wait=[tred, _bc_last.get(2)], tick=True)
        tcp = pr.op("scalar", lambda e: e.copy(rows_g["tmprow"][0:1, 0:P],
                                               PS[2][0:1, 0:P]),
                    wait=[ttp, rows_g.get("tmprow_free")], tick=True)
        _bc_last[2] = tcp
        if debug_taps:
            tcp = pr.op("sync", lambda e: e.dma_start(
                out=dbg_tqr[0:1, :], in_=rows_g["tmprow"][0:1, 0:P]),
                wait=[tcp], dma=True)
        rows_g["tmprow_free"] = pr.op("vector", lambda e: e.tensor_reduce(
            m11w[0:1, 0:1], rows_g["tmprow"][0:1, 0:P], axis=AX, op=OP.add),
            wait=[tcp], tick=True)
        tnl = pr.op("vector", lambda e, nel=nel: e.tensor_scalar(
            m11w[0:1, 0:1], m11w[0:1, 0:1], 1.0 / nel, 1e-5, op0=OP.mult, op1=OP.max),
            wait=[rows_g["tmprow_free"]], tick=True)
        if debug_taps:
            pr.op("sync", lambda e: e.dma_start(
                out=dbg_tqm[0:1, 0:1], in_=m11w[0:1, 0:1]), wait=[tnl], dma=True)
        tmn = pr.tick("vector")
        td1 = bcast_col(d1col, m11w, wait=[tmn], bank=2)
        pr.op("vector", lambda e: e.reciprocal(m11w[0:1, 0:1], m11w[0:1, 0:1]),
              wait=[td1], tick=True)
        tiv = pr.tick("vector")
        ts1_ = bcast_col(s1col, m11w, wait=[tiv], bank=2)
        tw_free = None
        for k in range(nkc):
            tldw = pr.op("sync", lambda e, k=k: e.dma_start(
                out=wnat1[:, 0:ncols], in_=src_dram[P * k:P * (k + 1), 0:ncols]),
                wait=[tw_free, ts1_ if k == 0 else None], dma=True)
            t1_ = pr.op("scalar", lambda e: e.activation(
                wi1[:, 0:ncols], wnat1[:, 0:ncols], AF.Copy, scale=s1col[:, 0:1]),
                wait=[tldw, ts1_], tick=True)
            tw_free = t1_
            t2_ = pr.op("vector", lambda e: e.tensor_scalar(
                wi1[:, 0:ncols], wi1[:, 0:ncols], 1.0, -1.0, op0=OP.min, op1=OP.max),
                wait=[t1_], tick=True)
            pr.op("scalar", lambda e, k=k: e.activation(
                dst_list[k][:, 0:ncols], wi1[:, 0:ncols], AF.Copy, scale=d1col[:, 0:1]),
                wait=[t2_], tick=True)
        return pr.tick("scalar")

    tq1w = ternary_quant(w1q, w1T, DT, HID1, HID1 * D)
    if debug_taps:
        pr.op("sync", lambda e: e.dma_start(
            out=dbg_tqd[:, 0:1], in_=d1col[:, 0:1]), wait=[tq1w], dma=True)
        pr.op("sync", lambda e: e.dma_start(
            out=dbg_tqs[:, 0:1], in_=s1col[:, 0:1]), wait=[tq1w], dma=True)
    if debug_taps:
        for k in range(DT):
            pr.op("sync", lambda e, k=k: e.dma_start(
                out=dbg_f1w1[P * k:P * (k + 1), :], in_=w1q[k][:, :]),
                wait=[tq1w], dma=True)
    end()
    hx = sb("hxF1", [P, TLOC + 2])
    cbuf = sb("cbufF1", [P, TLOC])
    hsb = sb("hsbF1", [P, TLOC], BF16)
    habs = cbuf
    cwf = sb("cwfF1", [P, 3])
    cbf = sb("cbfF1", [P, 1])
    tok_hxf = None
    tok_hs = None
    tok_hab = None
    tok_cvfree = None
    for mc in range(HID1 // P):
        tcw = pr.op("sync", lambda e, mc=mc: e.dma_start(
            out=cwf[:, :], in_=cw1[P * mc:P * (mc + 1), :]),
            wait=[tok_cvfree], dma=True)
        tcb = pr.op("sync", lambda e, mc=mc: e.dma_start(
            out=cbf[:, 0:1], in_=cb1[P * mc:P * (mc + 1), :]),
            wait=[tok_cvfree], dma=True)
        for cidx, col in ((0, 0), (1, TLOC + 1)):
            for k in range(DT):
                mm(PS[2][:, 0:1], w1q[k][:, P * mc:P * (mc + 1)],
                   xq_hal[k][:, cidx:cidx + 1], start=(k == 0), stop=(k == DT - 1),
                   wait=[tq1w, tok_hxf] if k == 0 else [])
            th_ = pr.tick("tensor")
            tok_hxf = pr.op("vector", lambda e, col=col: e.tensor_copy(
                hx[:, col:col + 1], PS[2][:, 0:1]), wait=[th_], tick=True)
        for n in range(NB):
            bank = n % 2
            for k in range(DT):
                mm(PS[bank][:, :], w1q[k][:, P * mc:P * (mc + 1)],
                   xqb_f[k][:, NBW * n:NBW * (n + 1)], start=(k == 0), stop=(k == DT - 1),
                   wait=[tok_hxf] if k == 0 else [])
            tmm_ = pr.tick("tensor")
            tok_hxf = pr.op("vector", lambda e, n=n, bank=bank: e.tensor_copy(
                hx[:, 1 + NBW * n:1 + NBW * (n + 1)], PS[bank][:, :]),
                wait=[tmm_], tick=True)
        if debug_taps and mc < DT:
            tok_hxf = pr.op("sync", lambda e, mc=mc: e.dma_start(
                out=dbg_f1hx[P * mc:P * (mc + 1), :], in_=hx[:, 1:1 + TLOC]),
                wait=[tok_hxf], dma=True)
        t1_ = pr.op("vector", lambda e: e.tensor_scalar(
            cbuf[:, :], hx[:, 0:TLOC], cwf[:, 0:1], cbf[:, 0:1],
            op0=OP.mult, op1=OP.add), wait=[tok_hxf, tcw, tcb, tok_hs], tick=True)
        t2_ = pr.op("vector", lambda e: e.scalar_tensor_tensor(
            cbuf[:, :], hx[:, 1:1 + TLOC], cwf[:, 1:2], cbuf[:, :],
            op0=OP.mult, op1=OP.add), tick=True)
        t3_ = pr.op("vector", lambda e: e.scalar_tensor_tensor(
            cbuf[:, :], hx[:, 2:2 + TLOC], cwf[:, 2:3], cbuf[:, :],
            op0=OP.mult, op1=OP.add), tick=True)
        tok_cvfree = t3_
        tok_hs = pr.op("scalar", lambda e: e.activation(
            hsb[:, :], cbuf[:, :], AF.Silu), wait=[t3_, tok_hab], tick=True)
        tab = pr.op("vector", lambda e: e.scalar_tensor_tensor(
            habs[:, :], hsb[:, :], -1.0, hsb[:, :], op0=OP.mult, op1=OP.max),
            wait=[tok_hs], tick=True)
        if mc == 0:
            tok_hab = pr.op("vector", lambda e: e.tensor_copy(
                hmacc[:, :], habs[:, :]), tick=True)
        else:
            tok_hab = pr.op("vector", lambda e: e.tensor_tensor(
                hmacc[:, :], hmacc[:, :], habs[:, :], op=OP.max), tick=True)
        tok_hs = pr.op("scalar", lambda e, mc=mc: e.dma_start(
            out=hs_sp[P * mc:P * (mc + 1), :], in_=hsb[:, :]), wait=[tok_hs], dma=True)
    end()
    s2row = sb("s2rowF1", [1, TLOC])
    is2row = sb("is2rowF1", [1, TLOC])
    tok_red = None
    for tb in range(TLOC // P):
        bk = tb % 2
        ttr_ = tr(PTB[bk][:, 0:P], hmacc[:, P * tb:P * (tb + 1)], eye_b[:, :],
                  wait=[tok_red], tick=True)
        tok_red = pr.op("vector", lambda e, tb=tb, bk=bk: e.tensor_reduce(
            s2cols[:, tb:tb + 1], PTB[bk][:, 0:P], axis=AX, op=OP.max),
            wait=[ttr_], tick=True)
    pr.op("vector", lambda e: e.tensor_scalar(
        s2cols[:, :], s2cols[:, :], 1e-5, None, op0=OP.max), wait=[tok_red], tick=True)
    tis2 = pr.op("vector", lambda e: e.tensor_scalar(
        is2cols[:, :], s2cols[:, :], 1.0 / 127.0, None, op0=OP.mult), tick=True)
    trs2 = pr.op("vector", lambda e: e.reciprocal(s2cols[:, :], s2cols[:, :]),
                 wait=[tis2], tick=True)
    tsc2 = pr.op("vector", lambda e: e.tensor_scalar(
        s2cols[:, :], s2cols[:, :], 127.0, None, op0=OP.mult), wait=[trs2], tick=True)
    tcp_prev = None
    tdma_prev = None
    for which, cols in ((s2row, s2cols), (is2row, is2cols)):
        ttr_ = tr(PS[2][0:32, 0:P], cols[:, 0:32], eye_f[:, :],
                  wait=[tsc2, tcp_prev], tick=True)
        tcp_ = pr.op("scalar", lambda e: e.copy(sr32[0:32, :], PS[2][0:32, 0:P]),
                     wait=[ttr_, tdma_prev], tick=True)
        tcp_prev = tcp_
        tdma_prev = pr.op("sync", lambda e, which=which: e.dma_start(
            out=which[0:1, :], in_=sr32[0:32, :]), wait=[tcp_], dma=True)
    pr.barrier()
    w2q = [sb("w2qF1_%d" % k, [P, D], BF16) for k in range(HID1 // P)]
    begin()
    wnat2 = sb("wnat2F1", [P, D])
    wi2 = sb("wi2F1", [P, D], I32)
    wacc2 = sb("wacc2F1", [P, 1])
    tqtmp2 = sb("tqtmp2F1", [P, 1])
    m11b = sb("m11b_F1", [1, 1])
    s1col2 = sb("s1col2F1", [P, 1])
    d1col2 = sb("d1col2F1", [P, 1])

    def ternary_quant2(dst_list, src_dram, nkc, ncols, nel):
        tred = None
        tldw = None
        for k in range(nkc):
            tldw = pr.op("sync", lambda e, k=k: e.dma_start(
                out=wnat2[:, 0:ncols], in_=src_dram[P * k:P * (k + 1), 0:ncols]),
                wait=[tred], dma=True)
            if k == 0:
                tred = pr.op("vector", lambda e: e.tensor_reduce(
                    wacc2[:, 0:1], wnat2[:, 0:ncols], axis=AX, op=OP.add,
                    apply_absolute_value=True), wait=[tldw], tick=True)
            else:
                trt_ = pr.op("vector", lambda e: e.tensor_reduce(
                    tqtmp2[:, 0:1], wnat2[:, 0:ncols], axis=AX, op=OP.add,
                    apply_absolute_value=True), wait=[tldw], tick=True)
                tred = pr.op("vector", lambda e: e.tensor_tensor(
                    wacc2[:, 0:1], wacc2[:, 0:1], tqtmp2[:, 0:1], op=OP.add),
                    wait=[trt_], tick=True)
        ttp = tr(PS[2][0:1, 0:P], wacc2[:, 0:1], eye_f[:, :],
                 wait=[tred, _bc_last.get(2)], tick=True)
        tcp = pr.op("scalar", lambda e: e.copy(rows_g["tmprow"][0:1, 0:P],
                                               PS[2][0:1, 0:P]),
                    wait=[ttp, rows_g.get("tmprow_free")], tick=True)
        _bc_last[2] = tcp
        rows_g["tmprow_free"] = pr.op("vector", lambda e: e.tensor_reduce(
            m11b[0:1, 0:1], rows_g["tmprow"][0:1, 0:P], axis=AX, op=OP.add),
            wait=[tcp], tick=True)
        pr.op("vector", lambda e, nel=nel: e.tensor_scalar(
            m11b[0:1, 0:1], m11b[0:1, 0:1], 1.0 / nel, 1e-5, op0=OP.mult, op1=OP.max),
            wait=[rows_g["tmprow_free"]], tick=True)
        tmn = pr.tick("vector")
        td1 = bcast_col(d1col2, m11b, wait=[tmn], bank=2)
        pr.op("vector", lambda e: e.reciprocal(m11b[0:1, 0:1], m11b[0:1, 0:1]),
              wait=[td1], tick=True)
        tiv = pr.tick("vector")
        ts1_ = bcast_col(s1col2, m11b, wait=[tiv], bank=2)
        tw_free = None
        for k in range(nkc):
            tldw = pr.op("sync", lambda e, k=k: e.dma_start(
                out=wnat2[:, 0:ncols], in_=src_dram[P * k:P * (k + 1), 0:ncols]),
                wait=[tw_free, ts1_ if k == 0 else None], dma=True)
            t1_ = pr.op("scalar", lambda e: e.activation(
                wi2[:, 0:ncols], wnat2[:, 0:ncols], AF.Copy, scale=s1col2[:, 0:1]),
                wait=[tldw, ts1_], tick=True)
            tw_free = t1_
            t2_ = pr.op("vector", lambda e: e.tensor_scalar(
                wi2[:, 0:ncols], wi2[:, 0:ncols], 1.0, -1.0, op0=OP.min, op1=OP.max),
                wait=[t1_], tick=True)
            pr.op("scalar", lambda e, k=k: e.activation(
                dst_list[k][:, 0:ncols], wi2[:, 0:ncols], AF.Copy, scale=d1col2[:, 0:1]),
                wait=[t2_], tick=True)
        return pr.tick("scalar")

    tq2w = ternary_quant2(w2q, w2T, HID1 // P, D, HID1 * D)
    end()
    hqk = [sb("hqkF1_%d" % k, [P, NBW], BF16) for k in range(HID1 // P)]
    hld = sb("hldF1", [P, NBW], BF16)
    hqi = sb("hqiF1", [P, NBW], I32)
    xs4 = [sb("xs4F1_%d" % m, [P, NBW]) for m in range(DT)]
    xr4 = [sb("xr4F1_%d" % m, [P, NBW]) for m in range(DT)]
    sqt4 = sb("sqt4F1", [P, NBW])
    rows4 = {"mrow": sb("mrow4", [1, NBW]), "m2row": sb("m2row4", [1, NBW]),
             "vrow": sb("vrow4", [1, NBW]), "rrow": sb("rrow4", [1, NBW]),
             "xo": sb("xo4", [P, NBW])}
    otm = [sb("otmF1_%d" % c4, [P, D]) for c4 in range(4)]
    pr.barrier()
    for n in range(NB):
        ts2b = mm(PS[4][:, :], ones_row[0:1, :], s2row[0:1, NBW * n:NBW * (n + 1)],
                  start=True, stop=True, tick=True)
        ti2b = mm(PS[5][:, :], ones_row[0:1, :], is2row[0:1, NBW * n:NBW * (n + 1)],
                  start=True, stop=True, tick=True)
        tq = [None] * (HID1 // P)
        tld_p = None
        thq_p = None
        for k in range(HID1 // P):
            tld_ = pr.op("sync", lambda e, k=k, n=n: e.dma_start(
                out=hld[:, :], in_=hs_sp[P * k:P * (k + 1), NBW * n:NBW * (n + 1)]),
                wait=[tld_p], dma=True)
            tld_p = pr.op("vector", lambda e: e.tensor_tensor(
                hqi[:, :], hld[:, :], PS[4][:, :], op=OP.mult),
                wait=[tld_, ts2b, thq_p], tick=True)
            thq_p = tq[k] = pr.op("vector", lambda e, k=k: e.tensor_tensor(
                hqk[k][:, :], hqi[:, :], PS[5][:, :], op=OP.mult),
                wait=[ti2b], tick=True)
        txs = [None] * DT
        for m_ in range(DT):
            txs[m_] = pr.op("scalar", lambda e, m_=m_, n=n: e.dma_start(
                out=xs4[m_][:, :], in_=xres[P * m_:P * (m_ + 1), NBW * n:NBW * (n + 1)]),
                dma=True)
        xr_tok = [None] * DT
        tok_psf = {}
        for m_ in range(DT):
            bank = m_ % 2
            for k in range(HID1 // P):
                mm(PS[bank][:, :], w2q[k][:, P * m_:P * (m_ + 1)], hqk[k][:, :],
                   start=(k == 0), stop=(k == HID1 // P - 1),
                   wait=([tq[k], tq2w] if m_ < 2 else []) +
                        ([tok_psf.get(bank)] if k == 0 else []))
            tdm = pr.tick("tensor")
            xr_tok[m_] = pr.op("vector", lambda e, m_=m_, bank=bank: e.tensor_tensor(
                xr4[m_][:, :], PS[bank][:, :], xs4[m_][:, :], op=OP.add),
                wait=[tdm, txs[m_]], tick=True)
            tok_psf[bank] = xr_tok[m_]
        rows4["xr_tok"] = xr_tok
        otm_bank = {0: None, 1: None}

        def out_fn4(m_, xo_t, tok, n=n, otm_bank=otm_bank):
            last = None
            for c4 in range(4):
                bk = c4 % 2
                ttr_ = tr(PS[bk][:, 0:P], xo_t[:, P * c4:P * (c4 + 1)], eye_f[:, :],
                          wait=[tok, otm_bank[bk]], tick=True)
                last = otm_bank[bk] = pr.op("scalar", lambda e, m_=m_, c4=c4, bk=bk: e.copy(
                    otm[c4][:, P * m_:P * (m_ + 1)], PS[bk][:, 0:P]),
                    wait=[ttr_], tick=True)
            return last

        feat_ln_block(n, xr4, sqt4, rows4, None, (2, 3), out_fn4)
        tfin = pr.tick("scalar")
        for c4 in range(4):
            pr.op("sync", lambda e, n=n, c4=c4: e.dma_start(
                out=out_d[NBW * n + P * c4:NBW * n + P * (c4 + 1), :],
                in_=otm[c4][:, :]), wait=[tfin], dma=True)
        pr.barrier()
    end()

    tap_outs = {}
    pr.barrier()
    for name in debug_taps:
        t = ds[name]
        tap_outs[name] = nc.dram_tensor("tap_" + name, list(t.shape), t.dtype,
                                        kind="ExternalOutput")
        sl = tuple(slice(None) for _ in t.shape)
        pr.op("sync", lambda e, name=name, sl=sl: e.dma_start(
            out=tap_outs[name][sl], in_=ds[name][sl]), dma=True)
    pr.op("sync", lambda e: e.nop(),
          wait=pr.dma_tokens("sync") + pr.dma_tokens("scalar") + pr.dma_tokens("gpsimd"))

    with nc.Block() as block:
        pr.finalize(block, sems)

    for k in reversed(list(sems)):
        sem_cms[k].__exit__(None, None, None)
    for cm in reversed(psum_cms):
        cm.__exit__(None, None, None)
    for cm in reversed(ctx_stack[0]):
        cm.__exit__(None, None, None)
    return nc


def _host_inputs(inputs):
    x = np.asarray(inputs["L0_attn_x"])
    base = {
        "w_qkv": np.asarray(inputs["L0_attn_W_qkv"]),
        "w_out": np.asarray(inputs["L0_attn_W_out"]),
        "b_out": np.asarray(inputs["L0_attn_b_out"]).reshape(1, D),
        "clip_x": np.asarray(inputs["L0_ffn_clip_x"]).reshape(1, 1),
        "clip_down": np.asarray(inputs["L0_ffn_clip_down"]).reshape(1, 1),
        "w_gate": np.asarray(inputs["L0_ffn_w_gate"]),
        "w_up": np.asarray(inputs["L0_ffn_w_up"]),
        "w_down": np.asarray(inputs["L0_ffn_w_down"]),
        "inp_wT": np.ascontiguousarray(np.asarray(inputs["L1_attn_in_proj_w"]).T),
        "conv_w": np.asarray(inputs["L1_attn_conv_w"]),
        "conv_b": np.asarray(inputs["L1_attn_conv_b"]).reshape(INNER, 1),
        "dt_wT": np.ascontiguousarray(np.asarray(inputs["L1_attn_dt_w"]).T),
        "dt_b": np.asarray(inputs["L1_attn_dt_b"]).reshape(INNER, 1),
        "a_log": np.asarray(inputs["L1_attn_A_log"]),
        "b_wT": np.ascontiguousarray(np.asarray(inputs["L1_attn_B_w"]).T),
        "c_wT": np.ascontiguousarray(np.asarray(inputs["L1_attn_C_w"]).T),
        "out_wT": np.ascontiguousarray(np.asarray(inputs["L1_attn_out_w"]).T),
        "w1T": np.ascontiguousarray(np.asarray(inputs["L1_ffn_w1"]).T),
        "cw1": np.asarray(inputs["L1_ffn_conv_w"]).reshape(HID1, 3),
        "cb1": np.asarray(inputs["L1_ffn_conv_b"]).reshape(HID1, 1),
        "w2T": np.ascontiguousarray(np.asarray(inputs["L1_ffn_w2"]).T),
        "c_eye": np.eye(P, dtype=np.float32),
        "c_ones_row": np.ones((1, P), np.float32),
        "c_ones_col": np.ones((P, 1), np.float32),
    }
    sel = np.zeros((16, 16 * P), np.float32)
    for s in range(16):
        sel[s, P * s:P * (s + 1)] = 1.0
    base["c_sel16"] = sel
    for f in range(DT):
        pt = np.zeros((16, P), np.float32)
        for h2 in range(2):
            pt[2 * f + h2, 64 * h2:64 * (h2 + 1)] = 1.0
        base["c_patt16_%d" % f] = pt
    in_maps = []
    for c in range(NCORE):
        b, h = c >> 1, c & 1
        m = dict(base)
        m["x_t"] = np.ascontiguousarray(x[b, h * TLOC:(h + 1) * TLOC, :].T)
        m["a_slice"] = np.ascontiguousarray(
            np.asarray(inputs["L1_attn_A_log"])[DSL * c:DSL * (c + 1), :])
        m["c_core"] = np.array([[h, float(h == 1), float(h == 0), c]], np.float32)
        in_maps.append(m)
    return in_maps


def kernel(**inputs):
    in_maps = _host_inputs(inputs)
    with allow_ici_all_to_all():
        nc = build()
        res = run_bass_kernel_spmd(nc, in_maps, core_ids=list(range(NCORE)))
    out = np.empty((4, 2 * TLOC, D), np.float32)
    for c in range(NCORE):
        b, h = c >> 1, c & 1
        out[b, h * TLOC:(h + 1) * TLOC, :] = res.results[c]["out"]
    return out
